# version 11
# baseline (speedup 1.0000x reference)
import contextlib
import numpy as np
import ml_dtypes
import concourse.bass as bass
import concourse.mybir as mybir
from concourse.bass_utils import run_bass_kernel_spmd
from concourse.bass_core_rust import IndirectOffsetOnAxis

F32 = mybir.dt.float32
BF16 = mybir.dt.bfloat16
I32 = mybir.dt.int32
AF = mybir.ActivationFunctionType
ALU = mybir.AluOpType
AX = mybir.AxisListType
BF = ml_dtypes.bfloat16

NCORES = 8
B, S, D = 2, 8192, 1024
T = 2048
HALO = 512
KV = HALO + T
NKV, HD = 4, 64
LW, GT = 256, 64
DFF = 2048
NE, CAP = 8, 2048
NG, NT = T // 512, T // 128
NKVG, NKVT = KV // 512, KV // 128
AGC = 4
CPC = CAP // AGC
RPC = NCORES * CPC
TPC = T // AGC
EPS_RMS = float(np.finfo(np.float32).eps)
EPS_LN = 1e-5
BIG = 1.0e9
PADROW = NCORES * CAP
CHUNK_RANGES = [(0, 127), (0, 255), (0, 383), (128, 512), (256, 512), (384, 512)]


def _allowed(q, k):
    return (k <= q) & ((k >= q - LW + 1) | (k < GT))


def _kvtok(col, s0):
    col = np.asarray(col)
    pos = np.where(col < GT, col, s0 - (HALO - GT) + (col - GT))
    pos = np.where(col >= HALO, s0 + (col - HALO), pos)
    return pos


def _local_masks(g, s0):
    outs = []
    for j in range(6):
        a, b = CHUNK_RANGES[j]
        kpos = _kvtok(g * 512 + 256 + 128 * j + np.arange(128), s0)
        qpos = s0 + g * 512 + np.arange(a, b)
        outs.append(_allowed(qpos[None, :], kpos[:, None]) & (kpos[:, None] >= 0))
    return outs


def _host_masks(s0):
    shared = np.concatenate(_local_masks(1, s0), axis=1).astype(BF)
    if s0 != 0:
        return shared, shared.copy(), np.ones((GT, 512), dtype=BF)
    g0_list = _local_masks(0, s0)
    g0 = np.concatenate(g0_list, axis=1).astype(BF)
    allowed = _allowed(np.arange(512)[None, :], np.arange(GT)[:, None])
    covered = np.zeros_like(allowed)
    for j in range(6):
        a, b = CHUNK_RANGES[j]
        kp = _kvtok(256 + 128 * j + np.arange(128), s0)
        for w in range(128):
            if 0 <= kp[w] < GT:
                covered[kp[w], a:b] |= g0_list[j][w]
    return shared, g0, (allowed & ~covered).astype(BF)


def _host_cossin(s0):
    pos = np.maximum(_kvtok(np.arange(KV), s0), 0).astype(np.float64)
    inv = 1.0 / (10000.0 ** (np.arange(0, HD, 2, dtype=np.float64) / HD))
    fr = pos[None, :] * inv[:, None]
    c = np.cos(fr).astype(np.float32)
    s = np.sin(fr).astype(np.float32)
    return (np.concatenate([c, c, c, c], axis=0),
            np.concatenate([s, -s, s, -s], axis=0))


def _host_tokval(core):
    out = np.zeros((128, 16), dtype=np.float32)
    n = np.arange(128)
    for r in range(16):
        tloc = r * 128 + n
        out[:, r] = 1 + (tloc // TPC) * RPC + core * TPC + (tloc % TPC)
    return out


def stage_core_inputs(core, X, Wq, Wk, Wv, Wo, RW, RB, EW, EV, EWo):
    b, s0 = core // 4, (core % 4) * T
    x = X[b]
    kvx = np.zeros((KV, D), dtype=np.float32)
    kvx[:GT] = x[:GT]
    lo = s0 - (HALO - GT)
    if lo >= 0:
        kvx[GT:HALO] = x[lo:s0]
    kvx[HALO:] = x[s0:s0 + T]
    shared, g0, gl = _host_masks(s0)
    csc, css = _host_cossin(s0)
    ident = np.eye(128, dtype=np.float32)
    rowsel = np.zeros((128, 16), dtype=np.float32)
    rowsel[16 * core + np.arange(16), np.arange(16)] = 1.0
    bdiag = np.zeros((128, 2), dtype=np.float32)
    bdiag[0:64, 0] = 1.0
    bdiag[64:128, 1] = 1.0
    exp2 = np.zeros((2, 128), dtype=np.float32)
    exp2[0, 0:64] = 1.0
    exp2[1, 64:128] = 1.0
    return {
        "xT": np.ascontiguousarray(kvx.T),
        "x_tok": np.ascontiguousarray(x[s0:s0 + T]),
        "wq": np.ascontiguousarray(Wq),
        "wk": np.ascontiguousarray(Wk),
        "wv": np.ascontiguousarray(Wv),
        "wo": np.ascontiguousarray(Wo),
        "woT": np.ascontiguousarray(Wo.T),
        "rwT": np.ascontiguousarray(RW.T),
        "rb": np.ascontiguousarray(RB.reshape(1, NE)),
        "eW": np.ascontiguousarray(EW[core]),
        "eV": np.ascontiguousarray(EV[core]),
        "eWo": np.ascontiguousarray(EWo[core]),
        "csc": csc,
        "css": css,
        "masksh": np.ascontiguousarray(shared),
        "maskg0": np.ascontiguousarray(g0),
        "maskgl": np.ascontiguousarray(gl),
        "identb": ident.astype(BF),
        "identf": ident,
        "lstrict": np.triu(np.ones((128, 128), dtype=np.float32), 1),
        "bdiag": bdiag.astype(BF),
        "expand2": exp2.astype(BF),
        "onesk1": np.ones((1, 128), dtype=BF),
        "onesk1f": np.ones((1, 128), dtype=np.float32),
        "onescol": np.ones((128, 1), dtype=np.float32),
        "tokval": _host_tokval(core),
        "rowsel": rowsel,
        "fconst": np.tile(np.array([[EPS_RMS, EPS_LN, 1.0, 1.0 / HD, 0.125, -1.0,
                                     0.0, 0.0]], dtype=np.float32), (128, 1)),
        "iotae": np.tile(np.arange(NE, dtype=np.float32), (128, 128)),
    }


class Sem:
    __slots__ = ("name", "n", "h")

    def __init__(self, name):
        self.name, self.n, self.h = name, 0, None


class Plan:
    ENGINES = ("pe", "act", "dve", "pool", "sync")

    def __init__(self):
        self.ops = {e: [] for e in self.ENGINES}
        self.sems = []
        self.gid = 0
        self.marks = {}

    def sem(self, name):
        s = Sem(name)
        self.sems.append(s)
        return s

    def emit(self, eng, fn, waits=(), inc=None, inc_k=1):
        waits = [(s, v) for (s, v) in waits if s is not None and v]
        if inc is not None:
            inc.n += inc_k
        self.ops[eng].append((self.gid, fn, waits, inc, inc_k))
        self.gid += 1
        return inc.n if inc is not None else None

    trunc = None

    def replay(self, eng_name, eng, nc=None):
        ctx = (nc.allow_low_precision(reason="intentional bf16 intermediates; "
               "tolerance admits bf16 matmul operands")
               if nc is not None else contextlib.nullcontext())
        with ctx:
            self._replay(eng_name, eng)

    def _replay(self, eng_name, eng):
        for gid, fn, waits, inc, inc_k in self.ops[eng_name]:
            if self.trunc is not None and gid >= self.trunc:
                continue
            for s, v in waits:
                eng.wait_ge(s.h, v)
            ins = fn(eng)
            if inc is not None:
                ins.then_inc(inc.h, inc_k)


_LAST_PLAN = {}


def build_nc(trunc=None):
    nc = bass.Bass("TRN2", target_bir_lowering=False)
    P = Plan()
    dram = {}

    def din(name, shape, dt):
        dram[name] = nc.dram_tensor(name, shape, dt, kind="ExternalInput")

    def dint(name, shape, dt):
        dram[name] = nc.dram_tensor(name, shape, dt)

    din("xT", [D, KV], F32)
    din("x_tok", [T, D], F32)
    din("wq", [D, D], F32)
    din("wk", [D, NKV * HD], F32)
    din("wv", [D, NKV * HD], F32)
    din("wo", [D, D], F32)
    din("woT", [D, D], F32)
    din("rwT", [D, NE], F32)
    din("rb", [1, NE], F32)
    din("eW", [D, DFF], F32)
    din("eV", [D, DFF], F32)
    din("eWo", [DFF, D], F32)
    din("csc", [128, KV], F32)
    din("css", [128, KV], F32)
    din("masksh", [128, 1533], BF16)
    din("maskg0", [128, 1533], BF16)
    din("maskgl", [GT, 512], BF16)
    din("identb", [128, 128], BF16)
    din("identf", [128, 128], F32)
    din("lstrict", [128, 128], F32)
    din("bdiag", [128, 2], BF16)
    din("expand2", [2, 128], BF16)
    din("onesk1", [1, 128], BF16)
    din("onesk1f", [1, 128], F32)
    din("onescol", [128, 1], F32)
    din("tokval", [128, 16], F32)
    din("rowsel", [128, 16], F32)
    din("fconst", [128, 8], F32)
    din("iotae", [128, 1024], F32)
    out_t = nc.dram_tensor("out", [T, D], F32, kind="ExternalOutput")

    dint("lg_loc", [T, NE], F32)
    dint("lg_all", [NCORES * T, NE], F32)
    dint("table_loc", [NCORES * CAP, 2], F32)
    dint("table_my", [CAP, 2], F32)
    dint("xm_my", [T, D], BF16)
    dint("xm_pad", [1 + NCORES * T, D], BF16)
    dint("xmf", [T, D], F32)
    dint("yexp_my", [CAP, D], BF16)
    dint("yexp_pad", [NCORES * CAP + 1, D], BF16)

    s_pe = P.sem("s_pe")
    s_act = P.sem("s_act")
    s_dve = P.sem("s_dve")
    d_c0 = P.sem("d_c0")
    d_zi = P.sem("d_zi")
    d_wq = P.sem("d_wq")
    d_wk = P.sem("d_wk")
    d_wv = P.sem("d_wv")
    d_wo = P.sem("d_wo")
    d_wrb = P.sem("d_wrb")
    d_woT = P.sem("d_woT")
    d_eW = P.sem("d_eW")
    d_eV = P.sem("d_eV")
    d_eWo = P.sem("d_eWo")
    d_x = [P.sem("d_x0"), P.sem("d_x1")]
    d_cs = [P.sem("d_cs0"), P.sem("d_cs1")]
    d_xt = [P.sem("d_xt0"), P.sem("d_xt1")]
    d_xmf = [P.sem("d_xmf0"), P.sem("d_xmf1")]
    d_xmb = [P.sem("d_xmb0"), P.sem("d_xmb1")]
    d_ye = [P.sem("d_ye0"), P.sem("d_ye1")]
    d_st = [P.sem("d_st0"), P.sem("d_st1")]
    d_ix = [P.sem("d_ix0"), P.sem("d_ix1")]
    d_cb = [P.sem(f"d_cb{i}") for i in range(4)]
    d_sc = P.sem("d_sc")
    d_ms = P.sem("d_ms")
    d_mk = P.sem("d_mk")
    d_io = P.sem("d_io")
    d_lg = P.sem("d_lg")
    c_cc = P.sem("c_cc")

    def mm_group(psum_ap, pairs, waits=()):
        n = len(pairs)

        def fn(te):
            last = None
            for i, (lf, rf) in enumerate(pairs):
                last = te.matmul(psum_ap(), lf(), rf(), start=(i == 0),
                                 stop=(i == n - 1))
            return last
        return P.emit("pe", fn, waits=waits, inc=s_pe)

    def gp_dma(dst, src, sem, waits=()):
        return P.emit("pool", lambda g, dst=dst, src=src: g.dma_start(
            out=dst(), in_=src()), waits=waits, inc=sem, inc_k=16)

    def sy_dma(dst, src, sem, waits=()):
        return P.emit("sync", lambda g, dst=dst, src=src: g.dma_start(
            out=dst(), in_=src()), waits=waits, inc=sem, inc_k=16)

    def r3(name, n):
        return dram[name].ap().rearrange("(c p) n -> p c n", p=128)

    REGS = {}

    def _setup_regs(g):
        r1 = g.alloc_register()
        g.reg_mov(r1, NCORES * CAP - 1)
        r2 = g.alloc_register()
        ins = g.reg_mov(r2, NCORES * T)
        REGS["b_scat"] = r1
        REGS["b_gath"] = r2
        return ins

    es0 = contextlib.ExitStack()
    with es0:
        sbufs = {}

        def sbuf(stk, name, shape, dt):
            t = stk.enter_context(nc.sbuf_tensor(name, shape, dt))
            sbufs[name] = t
            return t

        identb_sb = sbuf(es0, "identb_sb", [128, 128], BF16)
        identf_sb = sbuf(es0, "identf_sb", [128, 128], F32)
        lstrict_sb = sbuf(es0, "lstrict_sb", [128, 128], F32)
        zero_sb = sbuf(es0, "zero_sb", [128, 256], F32)
        zrow_sb = sbuf(es0, "zrow_sb", [128, 128], F32)
        zerobf_sb = sbuf(es0, "zerobf_sb", [1, D], BF16)
        onesk1_sb = sbuf(es0, "onesk1_sb", [1, 128], BF16)
        onesk1f_sb = sbuf(es0, "onesk1f_sb", [1, 128], F32)
        onescol_sb = sbuf(es0, "onescol_sb", [128, 1], F32)
        bdiag_sb = sbuf(es0, "bdiag_sb", [128, 2], BF16)
        expand2_sb = sbuf(es0, "expand2_sb", [2, 128], BF16)
        rwT_sb = sbuf(es0, "rwT_sb", [128, 8 * NE], F32)
        rwTb_sb = sbuf(es0, "rwTb_sb", [128, 8 * NE], BF16)
        rb_sb = sbuf(es0, "rb_sb", [1, NE], F32)
        tokval_sb = sbuf(es0, "tokval_sb", [128, 16], F32)
        rowsel_sb = sbuf(es0, "rowsel_sb", [128, 16], F32)
        w2_sb = sbuf(es0, "w2_sb", [128, 8 * NE], BF16)
        sbc_sb = sbuf(es0, "sbc_sb", [128, NE], F32)
        rbb_sb = sbuf(es0, "rbb_sb", [128, NE], F32)
        st_sb = [sbuf(es0, f"st_sb{i}", [128, 8], F32) for i in range(2)]
        fconst_sb = sbuf(es0, "fconst_sb", [128, 8], F32)
        lgl_sb = sbuf(es0, "lgl_sb", [128, NT * NE], F32)
        xrw_sb = sbuf(es0, "xrw_sb", [8, T], F32)
        lgs_sb = sbuf(es0, "lgs_sb", [128, NE], F32)
        xrt_sb = [sbuf(es0, f"xrt_sb{i}", [128, NE], F32) for i in range(2)]
        fT_sb = [sbuf(es0, f"fT_sb{i}", [128, 16], F32) for i in range(6)]
        fTi_sb = [sbuf(es0, f"fTi_sb{i}", [128, 16], I32) for i in range(4)]
        val_sb = [sbuf(es0, f"val_sb{i}", [128, 32], F32) for i in range(2)]
        tab_sb = sbuf(es0, "tab_sb", [128, 32], F32)
        gidx_sb = sbuf(es0, "gidx_sb", [128, 16], I32)
        gate_sb = sbuf(es0, "gate_sb", [128, 16], F32)
        sqln_sb = [sbuf(es0, f"sqln_sb{i}", [128, D], BF16) for i in range(2)]
        r_sb = [sbuf(es0, f"r_sb{i}", [128, D], F32) for i in range(2)]
        xmf_sb = [sbuf(es0, f"xmf_sb{i}", [128, D], F32) for i in range(2)]
        xmb_sb = [sbuf(es0, f"xmb_sb{i}", [128, D], BF16) for i in range(2)]
        yc_sb = [sbuf(es0, f"yc_sb{i}", [128, D], BF16) for i in range(4)]
        ye_sb = [sbuf(es0, f"ye_sb{i}", [128, D], BF16) for i in range(2)]
        silu_sb = [sbuf(es0, f"silu_sb{i}", [128, 512], F32) for i in range(2)]
        xe_sb = [sbuf(es0, f"xe_sb{i}", [128, D], BF16) for i in range(2)]

        pb = [es0.enter_context(nc.psum_tensor(f"pb{i}", [128, 512], F32))
              for i in range(7)]
        pbt = es0.enter_context(nc.psum_tensor("pbt", [128, 128], BF16))

        P.emit("pool", _setup_regs)
        sy_dma(lambda: identb_sb[:], lambda: dram["identb"].ap(), d_c0)
        sy_dma(lambda: identf_sb[:], lambda: dram["identf"].ap(), d_c0)
        sy_dma(lambda: lstrict_sb[:], lambda: dram["lstrict"].ap(), d_c0)
        sy_dma(lambda: onesk1_sb[:], lambda: dram["onesk1"].ap(), d_c0)
        sy_dma(lambda: onesk1f_sb[:], lambda: dram["onesk1f"].ap(), d_c0)
        sy_dma(lambda: onescol_sb[:], lambda: dram["onescol"].ap(), d_c0)
        sy_dma(lambda: bdiag_sb[:], lambda: dram["bdiag"].ap(), d_c0)
        sy_dma(lambda: expand2_sb[:], lambda: dram["expand2"].ap(), d_c0)
        sy_dma(lambda: rwT_sb[:].rearrange("p (c n) -> p c n", n=NE), lambda: r3("rwT", NE), d_c0)
        sy_dma(lambda: rb_sb[:], lambda: dram["rb"].ap(), d_c0)
        sy_dma(lambda: tokval_sb[:], lambda: dram["tokval"].ap(), d_c0)
        sy_dma(lambda: rowsel_sb[:], lambda: dram["rowsel"].ap(), d_c0)
        sy_dma(lambda: fconst_sb[:], lambda: dram["fconst"].ap(), d_c0)
        for _i, _v in enumerate([EPS_RMS, EPS_LN, 1.0, 1.0 / HD, 0.125, -1.0]):
            nc.const_aps.register(F32, float(_v), fconst_sb[:, _i:_i + 1])
        LD0 = d_c0.n
        ld_rwb = gp_dma(lambda: rwTb_sb[:].rearrange("p (c n) -> p c n", n=NE), lambda: r3("rwT", NE), d_wrb)

        n_zero = P.emit("dve", lambda v: v.memset(zero_sb[:], 0.0), inc=s_dve)
        n_zrow = P.emit("dve", lambda v: v.memset(zrow_sb[:], 0.0),
                        waits=[(s_dve, n_zero)], inc=s_dve)
        n_zbf = P.emit("dve", lambda v: v.memset(zerobf_sb[:], 0.0),
                       waits=[(s_dve, n_zrow)], inc=s_dve)
        sy_dma(lambda: dram["xm_pad"].ap()[0:1, :], lambda: zerobf_sb[:], d_zi,
               waits=[(s_dve, n_zbf)])
        sy_dma(lambda: dram["yexp_pad"].ap()[PADROW:PADROW + 1, :],
               lambda: zerobf_sb[:], d_zi, waits=[(s_dve, n_zbf)])
        sy_dma(lambda: dram["table_loc"].ap().rearrange("(a b) c -> a (b c)", a=128),
               lambda: zero_sb[:], d_zi, waits=[(s_dve, n_zero)])
        ZINIT = d_zi.n

        mm_group(lambda: pb[3][0:NE, 0:1],
                 [(lambda fc=fc: rwT_sb[:, fc * NE:(fc + 1) * NE],
                   lambda: onescol_sb[:]) for fc in range(8)],
                 waits=[(d_c0, LD0)])
        n_S = s_pe.n
        P.emit("act", lambda a: a.copy(out=st_sb[0][0:NE, 0:1],
                                       in_=pb[3][0:NE, 0:1]),
               waits=[(s_pe, n_S)], inc=s_act)
        n_Se = s_act.n
        P.emit("pe", lambda te: te.transpose(pb[3][0:1, 0:NE], st_sb[0][0:NE, 0:1],
                                             identf_sb[0:NE, 0:NE]),
               waits=[(s_act, n_Se)], inc=s_pe)
        n_St = s_pe.n
        P.emit("act", lambda a: a.copy(out=st_sb[1][0:1, 0:NE], in_=pb[3][0:1, 0:NE]),
               waits=[(s_pe, n_St)], inc=s_act)
        n_Str = s_act.n
        P.emit("pe", lambda te: te.matmul(pb[3][:, 0:NE], onesk1f_sb[0:1, :],
                                          st_sb[1][0:1, 0:NE], start=True, stop=True),
               waits=[(s_act, n_Str)], inc=s_pe)
        n_Sb = s_pe.n
        P.emit("act", lambda a: a.copy(out=sbc_sb[:], in_=pb[3][:, 0:NE]),
               waits=[(s_pe, n_Sb)], inc=s_act)
        n_Sbc = s_act.n
        P.emit("pe", lambda te: te.matmul(pb[3][:, 0:NE], onesk1f_sb[0:1, :],
                                          rb_sb[0:1, :], start=True, stop=True),
               waits=[(s_act, n_Sbc)], inc=s_pe)
        n_rb = s_pe.n
        P.emit("act", lambda a: a.copy(out=rbb_sb[:], in_=pb[3][:, 0:NE]),
               waits=[(s_pe, n_rb)], inc=s_act)
        n_rbb = s_act.n
        SETUP_PB3 = (s_act, n_rbb)

        es_a = contextlib.ExitStack()
        qhat = [sbuf(es_a, f"qhat{i}", [128, 1024], BF16) for i in range(8)]
        kTA = [sbuf(es_a, f"kTA{i}", [128, 1536], BF16) for i in range(2)]
        kTB = [sbuf(es_a, f"kTB{i}", [128, 1536], BF16) for i in range(2)]
        kTGA = sbuf(es_a, "kTGA", [128, 128], BF16)
        kTGB = sbuf(es_a, "kTGB", [128, 128], BF16)
        vst = sbuf(es_a, "vst", [128, 8 * 260], BF16)
        vstG = sbuf(es_a, "vstG", [64, 260], BF16)
        yT = [sbuf(es_a, f"yT{i}", [128, 1024], BF16) for i in range(8)]
        wo_sb = sbuf(es_a, "wo_sb", [128, 8 * D], BF16)
        masksh_sb = sbuf(es_a, "masksh_sb", [128, 1533], BF16)
        maskg0_sb = sbuf(es_a, "maskg0_sb", [128, 1533], BF16)
        maskgl_sb = sbuf(es_a, "maskgl_sb", [GT, 512], BF16)
        eT_sb = [sbuf(es_a, f"eT_sb{i}", [128, 512], BF16) for i in range(4)]
        rec_sb = [sbuf(es_a, f"rec_sb{i}", [1, 512], BF16) for i in range(2)]
        yv_sb = [sbuf(es_a, f"yv_sb{i}", [65, 512], F32) for i in range(2)]
        xTg = [sbuf(es_a, f"xTg{i}", [128, 8 * 512], BF16) for i in range(2)]
        wq_sb = sbuf(es_a, "wq_sb", [128, 8 * D], BF16)
        wk_sb = sbuf(es_a, "wk_sb", [128, 8 * 256], BF16)
        wv_sb = sbuf(es_a, "wv_sb", [128, 8 * 256], BF16)
        cscg = [sbuf(es_a, f"cscg{i}", [128, 512], F32) for i in range(2)]
        cssg = [sbuf(es_a, f"cssg{i}", [128, 512], F32) for i in range(2)]
        sq_sb = [sbuf(es_a, f"sq_sb{i}", [128, 512], BF16) for i in range(2)]
        rinv_sb = [sbuf(es_a, f"rinv_sb{i}", [2, 512], BF16) for i in range(2)]
        hat_sb = [sbuf(es_a, f"hat_sb{i}", [128, 512], F32) for i in range(2)]
        tmp_sb = [sbuf(es_a, f"tmp_sb{i}", [128, 512], F32) for i in range(2)]
        rot_sb = [sbuf(es_a, f"rot_sb{i}", [128, 512], F32) for i in range(2)]

        ld_wq = gp_dma(lambda: wq_sb[:].rearrange("p (c n) -> p c n", n=D), lambda: r3("wq", D), d_wq)
        ld_wk = gp_dma(lambda: wk_sb[:].rearrange("p (c n) -> p c n", n=256), lambda: r3("wk", 256), d_wk)
        ld_wv = gp_dma(lambda: wv_sb[:].rearrange("p (c n) -> p c n", n=256), lambda: r3("wv", 256), d_wv)
        ld_wo = gp_dma(lambda: wo_sb[:].rearrange("p (c n) -> p c n", n=D), lambda: r3("wo", D), d_wo)
        ld_msh = sy_dma(lambda: masksh_sb[:], lambda: dram["masksh"].ap(), d_mk)
        ld_mg0 = sy_dma(lambda: maskg0_sb[:], lambda: dram["maskg0"].ap(), d_mk)
        ld_mgl = sy_dma(lambda: maskgl_sb[:], lambda: dram["maskgl"].ap(), d_mk)
        ld_woT = []
        for oc in range(8):
            ld_woT.append(gp_dma(
                lambda oc=oc: yT[oc][:, 0:1024],
                lambda oc=oc: dram["woT"].ap()[oc * 128:(oc + 1) * 128, :],
                d_woT))
        w = [(d_woT, ld_woT[-1]), (d_wrb, ld_rwb), SETUP_PB3]
        for ic in range(8):
            mm_group(lambda ic=ic: pb[3][:, ic * NE:(ic + 1) * NE],
                     [(lambda oc=oc, ic=ic: yT[oc][:, ic * 128:(ic + 1) * 128],
                       lambda oc=oc: rwTb_sb[:, oc * NE:(oc + 1) * NE])
                      for oc in range(8)], waits=w if ic == 0 else [])
        n_w2mm = s_pe.n
        P.emit("act", lambda a: a.copy(out=w2_sb[:], in_=pb[3][:, 0:64]),
               waits=[(s_pe, n_w2mm)], inc=s_act)
        n_w2e = s_act.n

        n_vone = P.emit("dve", lambda v: v.memset(
            vst[:].rearrange("p (t g c) -> p t g c", t=8, g=4)[:, :, :, 64:65], 1.0),
            waits=[(s_dve, n_zbf)], inc=s_dve)
        n_voneg = P.emit("dve", lambda v: v.memset(
            vstG[:].rearrange("p (g c) -> p g c", g=4)[:, :, 64:65], 1.0),
            waits=[(s_dve, n_vone)], inc=s_dve)

        pp_free = [None, None]
        p4_free = None
        unit = [0]
        KDONE, QDONE = {}, {}
        vdone = {}
        ATTG, P4G = {}, {}
        XRWDONE = None
        xg_users = [None, None]
        cs_users = [None, None]
        rot_user = [None, None]
        pb01_owner = [None, None]
        vdefer = [None]
        VUSED = {}
        FINQ = []
        REC_USER = [None, None]
        YV_USER = [None, None]
        LG_CHUNK = []

        def proj_unit_gen(kind, gg, od, xbuf, DONE, key):
            nonlocal p4_free
            buf = unit[0] % 2
            unit[0] += 1
            if kind == "q":
                wsb, wstride = wq_sb, D
                dst, dcol, ldw = qhat[od], (gg % 2) * 512, (d_wq, ld_wq)
                warw = ATTG.get(gg - 2)
            else:
                wsb, wstride = wk_sb, 256
                dst, dcol, ldw = kTA[od], (gg % 3) * 512, (d_wk, ld_wk)
                warw = ATTG.get(gg - 3)
            while pb01_owner[buf] is not None:
                yield
            pb01_owner[buf] = 1
            w = [ldw, (d_x[xbuf[0]], xbuf[1])]
            if pp_free[buf]:
                w.append(pp_free[buf])
            mm_group(lambda buf=buf: pb[buf][:],
                     [(lambda fc=fc: wsb[:, fc * wstride + od * 128: fc * wstride + od * 128 + 128],
                       lambda fc=fc: xTg[xbuf[0]][:, fc * 512:(fc + 1) * 512])
                      for fc in range(8)], waits=w)
            n_mm = s_pe.n
            yield
            P.emit("act", lambda a, buf=buf: a.activation(
                out=sq_sb[buf][:], in_=pb[buf][:], func=AF.Square),
                waits=[(s_pe, n_mm)], inc=s_act)
            n_sq = s_act.n
            yield
            w = [(s_act, n_sq), (d_c0, LD0)]
            if p4_free:
                w.append(p4_free)
            mm_group(lambda: pb[4][0:2, :],
                     [(lambda: bdiag_sb[:], lambda buf=buf: sq_sb[buf][:])], waits=w)
            n_ssm = s_pe.n
            w = [(s_pe, n_ssm)]
            if rot_user[buf]:
                w.append(rot_user[buf])
            P.emit("act", lambda a, buf=buf: a.activation(
                out=rot_sb[buf][0:2, :], in_=pb[4][0:2, :], func=AF.Ln,
                scale=1.0 / HD, bias=EPS_RMS), waits=w, inc=s_act)
            n_rsq = s_act.n
            p4_free = (s_act, n_rsq)
            P.emit("act", lambda a, buf=buf: a.activation(
                out=rinv_sb[buf][:], in_=rot_sb[buf][0:2, :], func=AF.Exp, scale=-0.5),
                waits=[(s_act, n_rsq)], inc=s_act)
            n_ri = (s_act, s_act.n)
            yield
            w = [n_ri]
            if p4_free:
                w.append(p4_free)
            mm_group(lambda: pb[4][:],
                     [(lambda: expand2_sb[:], lambda buf=buf: rinv_sb[buf][:])],
                     waits=w)
            n_bcm = s_pe.n
            P.emit("act", lambda a, buf=buf: a.copy(
                out=sq_sb[buf][:], in_=pb[4][:]), waits=[(s_pe, n_bcm)], inc=s_act)
            n_bce = s_act.n
            p4_free = (s_act, n_bce)
            yield
            P.emit("dve", lambda v, buf=buf: v.tensor_mul(
                out=hat_sb[buf][:], in0=pb[buf][:], in1=sq_sb[buf][:]),
                waits=[(s_act, n_bce), (s_dve, s_dve.n)], inc=s_dve)
            n_hat = s_dve.n
            pp_free[buf] = (s_dve, n_hat)
            pb01_owner[buf] = None
            yield

            def swaps(v, buf=buf):
                last = None
                for h0 in (0, 64):
                    last = v.tensor_copy(out=tmp_sb[buf][h0:h0 + 32, :],
                                         in_=hat_sb[buf][h0 + 32:h0 + 64, :])
                    last = v.tensor_copy(out=tmp_sb[buf][h0 + 32:h0 + 64, :],
                                         in_=hat_sb[buf][h0:h0 + 32, :])
                return last
            P.emit("dve", swaps, waits=[(s_dve, n_hat)], inc=s_dve)
            n_sw = s_dve.n
            yield
            P.emit("dve", lambda v, buf=buf, cb=xbuf[0]: v.tensor_mul(
                out=rot_sb[buf][:], in0=hat_sb[buf][:], in1=cscg[cb][:]),
                waits=[(s_dve, n_sw), (d_cs[xbuf[0]], xbuf[2]), n_ri], inc=s_dve)
            n_r1 = s_dve.n
            yield
            P.emit("dve", lambda v, buf=buf, cb=xbuf[0]: v.tensor_mul(
                out=tmp_sb[buf][:], in0=tmp_sb[buf][:], in1=cssg[cb][:]),
                waits=[(s_dve, n_r1)], inc=s_dve)
            n_r2 = s_dve.n
            yield
            w = [(s_dve, n_r2)]
            if warw:
                w.append((s_dve, warw))
            P.emit("dve", lambda v, buf=buf, dst=dst, dcol=dcol: v.tensor_add(
                out=dst[:, dcol:dcol + 512], in0=rot_sb[buf][:], in1=tmp_sb[buf][:]),
                waits=w, inc=s_dve)
            rot_user[buf] = (s_dve, s_dve.n)
            if kind == "k":
                P.emit("dve", lambda v, od=od, dcol=dcol: v.tensor_copy(
                    out=kTB[od][0:64, dcol:dcol + 512],
                    in_=kTA[od][64:128, dcol:dcol + 512]),
                    waits=[(s_dve, s_dve.n)], inc=s_dve)
                P.emit("dve", lambda v, od=od, dcol=dcol: v.tensor_copy(
                    out=kTB[od][64:128, dcol:dcol + 512],
                    in_=kTA[od][0:64, dcol:dcol + 512]),
                    waits=[(s_dve, s_dve.n)], inc=s_dve)
            DONE[key] = s_dve.n

        def vproj_gen(t, xbuf):
            buf = unit[0] % 2
            unit[0] += 1
            while pb01_owner[buf] is not None:
                yield
            pb01_owner[buf] = 1
            w = [(d_wv, ld_wv), (d_x[xbuf[0]], xbuf[1]), (s_dve, n_voneg)]
            if pp_free[buf]:
                w.append(pp_free[buf])
            if t == 8:
                w.append((s_dve, GLOBC[0]))
            if t >= 10:
                nrd = 8 * len([1 for j in range(6)
                               if (t - 10 - j) % 4 == 0 and 0 <= (t - 10 - j) // 4 <= 3])
                while VUSED.get(t - 8, [0, None])[0] < nrd:
                    yield
                w.append(VUSED[t - 8][1])
            mm_group(lambda buf=buf: pb[buf][:, 0:256],
                     [(lambda fc=fc, t=t: xTg[xbuf[0]][:, fc * 512 + (t % 4) * 128: fc * 512 + (t % 4) * 128 + 128],
                       lambda fc=fc: wv_sb[:, fc * 256:(fc + 1) * 256]) for fc in range(8)],
                     waits=w)
            n_mm = s_pe.n
            yield
            P.emit("act", lambda a, buf=buf, t=t: a.copy(
                out=vst[:].rearrange("p (t g c) -> p t g c", t=8, g=4)[:, t % 8, :, 0:64],
                in_=pb[buf][:, 0:256]), waits=[(s_pe, n_mm)], inc=s_act)
            vdone[t] = (s_act, s_act.n)
            pp_free[buf] = (s_act, s_act.n)
            pb01_owner[buf] = None

        def xrw_gen(qg, gb, ldx):
            nonlocal p4_free, XRWDONE
            w = [(d_wrb, ld_rwb), (d_x[gb], ldx)]
            if XRWDONE:
                w.append(XRWDONE)
            if p4_free:
                w.append(p4_free)
            mm_group(lambda: pb[4][0:NE, :],
                     [(lambda fc=fc: rwTb_sb[:, fc * NE:(fc + 1) * NE],
                       lambda fc=fc, gb=gb: xTg[gb][:, fc * 512:(fc + 1) * 512])
                      for fc in range(8)], waits=w)
            n_x = s_pe.n
            P.emit("act", lambda a, qg=qg: a.copy(
                out=xrw_sb[:, qg * 512:(qg + 1) * 512], in_=pb[4][0:NE, :]),
                waits=[(s_pe, n_x)], inc=s_act)
            XRWDONE = (s_act, s_act.n)
            p4_free = (s_act, s_act.n)
            yield

        def glob_copy_gen():
            for od in range(2):
                P.emit("dve", lambda v, od=od: v.tensor_copy(
                    out=kTGA[:, od * 64:(od + 1) * 64], in_=kTA[od][:, 0:64]),
                    waits=[(s_dve, KDONE[(0, od)]), (s_dve, s_dve.n)], inc=s_dve)
                P.emit("dve", lambda v, od=od: v.tensor_copy(
                    out=kTGB[:, od * 64:(od + 1) * 64], in_=kTB[od][:, 0:64]),
                    waits=[(s_dve, s_dve.n)], inc=s_dve)
                yield
            P.emit("dve", lambda v: v.tensor_copy(out=vstG[:], in_=vst[0:64, 0:260]),
                   waits=[vdone[0], (s_dve, s_dve.n), (s_dve, n_voneg)], inc=s_dve)
            GLOBC[0] = s_dve.n

        GLOBC = [None]

        def drive2(att_q, fill_q, p4_q):
            cur_att, cur_p4 = None, None
            act_f = []
            while att_q or cur_att or fill_q or act_f or p4_q or cur_p4:
                if cur_att is None and att_q:
                    cur_att = att_q.pop(0)
                if cur_p4 is None and p4_q:
                    cur_p4 = p4_q.pop(0)
                while fill_q and len(act_f) < 2:
                    act_f.append(fill_q.pop(0))
                for slot, gx in (("a", cur_att), ("p", cur_p4), (0, None)):
                    pass
                for gx in [g_ for g_ in (cur_att, cur_p4) if g_] + list(act_f):
                    try:
                        next(gx)
                    except StopIteration:
                        if gx is cur_att:
                            cur_att = None
                        elif gx is cur_p4:
                            cur_p4 = None
                        else:
                            act_f.remove(gx)

        MOFF = np.cumsum([0] + [b - a for (a, b) in CHUNK_RANGES]).tolist()
        SBANK = [[5, 5], [6, 6]]
        ps_s_free = {5: None, 6: None}
        ps_y_free = {2: None, 3: (s_act, n_w2e)}
        eT_free = [None] * 4
        ATT_DONE = {}

        r_free = [None, None]
        xmf_free = [None, None]
        xmb_free = [None, None]
        LGDONE = []

        def emit_p4_tile(tt):
            nonlocal p4_free
            buf = tt % 2
            g = tt // 4
            attw = [(s_dve, ATT_DONE[(g, hp)]) for hp in range(8)]
            w = [r_free[buf]] if r_free[buf] else []
            P.emit("sync", lambda s, tt=tt, buf=buf: s.dma_start(
                out=r_sb[buf][:], in_=dram["x_tok"].ap()[tt * 128:(tt + 1) * 128, :]),
                waits=w, inc=d_xt[buf], inc_k=16)
            n_xl = (d_xt[buf], d_xt[buf].n)
            n_r = [None, None]
            for nh2 in range(2):
                while pb01_owner[nh2] is not None:
                    yield
                pb01_owner[nh2] = 1
                w = [(d_wo, ld_wo)] + attw
                if pp_free[nh2]:
                    w.append(pp_free[nh2])
                ytc = (g % 2) * 512 + (tt % 4) * 128
                mm_group(lambda nh2=nh2: pb[nh2][:],
                         [(lambda ic=ic, ytc=ytc: yT[ic][:, ytc:ytc + 128],
                           lambda ic=ic, nh2=nh2: wo_sb[:, ic * D + nh2 * 512: ic * D + nh2 * 512 + 512])
                          for ic in range(8)], waits=w)
                n_oh = s_pe.n
                w = [(s_pe, n_oh), n_xl, (s_dve, s_dve.n)]
                P.emit("dve", lambda v, nh2=nh2, buf=buf: v.tensor_add(
                    out=r_sb[buf][:, nh2 * 512:(nh2 + 1) * 512], in0=pb[nh2][:],
                    in1=r_sb[buf][:, nh2 * 512:(nh2 + 1) * 512]),
                    waits=w, inc=s_dve)
                n_r[nh2] = s_dve.n
                pp_free[nh2] = (s_dve, s_dve.n)
                pb01_owner[nh2] = None
                yield
            n_r = [n_r[0], n_r[1]]
            w = [(s_act, n_w2e)] + attw
            if p4_free:
                w.append(p4_free)
            ytc = (g % 2) * 512 + (tt % 4) * 128
            mm_group(lambda: pb[4][:, 0:NE],
                     [(lambda ic=ic, ytc=ytc: yT[ic][:, ytc:ytc + 128],
                       lambda ic=ic: w2_sb[:, ic * NE:(ic + 1) * NE]) for ic in range(8)],
                     waits=w)
            n_yw2 = s_pe.n
            P.emit("dve", lambda v: v.tensor_copy(out=lgs_sb[:], in_=pb[4][:, 0:NE]),
                   waits=[(s_pe, n_yw2), (s_dve, s_dve.n)], inc=s_dve)
            n_yw2c = s_dve.n
            p4_free = (s_dve, n_yw2c)
            yield
            w = [XRWDONE, (s_dve, n_yw2c), (d_c0, LD0)]
            if p4_free:
                w.append(p4_free)
            P.emit("pe", lambda te, tt=tt: te.transpose(
                pb[4][:, 0:NE], xrw_sb[:, tt * 128:(tt + 1) * 128],
                identf_sb[0:NE, 0:NE]), waits=w, inc=s_pe)
            n_xt = s_pe.n
            P.emit("dve", lambda v, buf=buf: v.tensor_copy(
                out=xrt_sb[buf][:], in_=pb[4][:, 0:NE]),
                waits=[(s_pe, n_xt), (s_dve, s_dve.n)], inc=s_dve)
            p4_free = (s_dve, s_dve.n)
            yield
            P.emit("dve", lambda v, buf=buf: v.reduce_sum(
                out=st_sb[buf][:, 0:1], in_=r_sb[buf][:], axis=AX.X),
                waits=[(s_dve, n_r[-1])], inc=s_dve)
            n_s1 = s_dve.n
            P.emit("act", lambda a, buf=buf: a.activation(
                out=sqln_sb[buf][:], in_=r_sb[buf][:], func=AF.Square),
                waits=[(s_dve, n_r[-1]), (s_act, s_act.n)], inc=s_act)
            n_sq = s_act.n
            yield
            P.emit("dve", lambda v, buf=buf: v.reduce_sum(
                out=st_sb[buf][:, 1:2], in_=sqln_sb[buf][:], axis=AX.X),
                waits=[(s_act, n_sq), (s_dve, n_s1)], inc=s_dve)
            P.emit("dve", lambda v, buf=buf: v.tensor_scalar(
                out=st_sb[buf][:, 2:3], in0=st_sb[buf][:, 0:1], scalar1=1.0 / D,
                scalar2=None, op0=ALU.mult), waits=[(s_dve, s_dve.n)], inc=s_dve)
            P.emit("dve", lambda v, buf=buf: v.tensor_mul(
                out=st_sb[buf][:, 3:4], in0=st_sb[buf][:, 2:3], in1=st_sb[buf][:, 2:3]),
                waits=[(s_dve, s_dve.n)], inc=s_dve)
            P.emit("dve", lambda v, buf=buf: v.tensor_scalar(
                out=st_sb[buf][:, 4:5], in0=st_sb[buf][:, 1:2], scalar1=1.0 / D,
                scalar2=None, op0=ALU.mult), waits=[(s_dve, s_dve.n)], inc=s_dve)
            P.emit("dve", lambda v, buf=buf: v.tensor_sub(
                out=st_sb[buf][:, 4:5], in0=st_sb[buf][:, 4:5], in1=st_sb[buf][:, 3:4]),
                waits=[(s_dve, s_dve.n)], inc=s_dve)
            n_var = s_dve.n
            yield
            P.emit("act", lambda a, buf=buf: a.activation(
                out=st_sb[buf][:, 5:6], in_=st_sb[buf][:, 4:5], func=AF.Ln,
                bias=EPS_LN), waits=[(s_dve, n_var)], inc=s_act)
            P.emit("act", lambda a, buf=buf: a.activation(
                out=st_sb[buf][:, 6:7], in_=st_sb[buf][:, 5:6], func=AF.Exp,
                scale=-0.5), waits=[(s_act, s_act.n)], inc=s_act)
            n_rs = s_act.n
            yield
            w = [(s_act, n_rs), (s_dve, s_dve.n)]
            if xmf_free[buf]:
                w.append(xmf_free[buf])
            P.emit("dve", lambda v, buf=buf: v.tensor_scalar(
                out=xmf_sb[buf][:], in0=r_sb[buf][:], scalar1=st_sb[buf][:, 2:3],
                scalar2=st_sb[buf][:, 6:7], op0=ALU.subtract, op1=ALU.mult),
                waits=w, inc=s_dve)
            n_xm = s_dve.n
            r_free[buf] = (s_dve, n_xm)
            w = [(s_dve, n_xm)]
            if xmb_free[buf]:
                w.append(xmb_free[buf])
            P.emit("act", lambda a, buf=buf: a.copy(out=xmb_sb[buf][:],
                                                    in_=xmf_sb[buf][:]),
                   waits=w, inc=s_act)
            n_xb = s_act.n
            P.emit("sync", lambda s, tt=tt, buf=buf: s.dma_start(
                out=dram["xmf"].ap()[tt * 128:(tt + 1) * 128, :], in_=xmf_sb[buf][:]),
                waits=[(s_dve, n_xm)], inc=d_xmf[buf], inc_k=16)
            xmf_free[buf] = (d_xmf[buf], d_xmf[buf].n)
            P.emit("sync", lambda s, tt=tt, buf=buf: s.dma_start(
                out=dram["xm_my"].ap()[tt * 128:(tt + 1) * 128, :], in_=xmb_sb[buf][:]),
                waits=[(s_act, n_xb)], inc=d_xmb[buf], inc_k=16)
            n_xmst = [(d_xmb[0], d_xmb[0].n), (d_xmb[1], d_xmb[1].n)]
            xmb_free[buf] = (d_xmb[buf], d_xmb[buf].n)
            yield
            P.emit("dve", lambda v, buf=buf: v.tensor_add(
                out=lgs_sb[:], in0=lgs_sb[:], in1=xrt_sb[buf][:]),
                waits=[(s_dve, s_dve.n)], inc=s_dve)
            P.emit("dve", lambda v, buf=buf: v.tensor_scalar(
                out=sqln_sb[buf][:, 0:NE], in0=sbc_sb[:],
                scalar1=st_sb[buf][:, 2:3], scalar2=None, op0=ALU.mult),
                waits=[(s_act, n_Sbc), (s_dve, s_dve.n)], inc=s_dve)
            P.emit("dve", lambda v, buf=buf: v.tensor_sub(
                out=lgs_sb[:], in0=lgs_sb[:], in1=sqln_sb[buf][:, 0:NE]),
                waits=[(s_dve, s_dve.n)], inc=s_dve)
            P.emit("dve", lambda v, buf=buf: v.tensor_scalar(
                out=lgs_sb[:], in0=lgs_sb[:], scalar1=st_sb[buf][:, 6:7],
                scalar2=None, op0=ALU.mult), waits=[(s_dve, s_dve.n)], inc=s_dve)
            P.emit("dve", lambda v, tt=tt: v.tensor_add(
                out=lgl_sb[:, tt * NE:(tt + 1) * NE], in0=lgs_sb[:], in1=rbb_sb[:]),
                waits=[(s_dve, s_dve.n), (s_act, n_rbb)], inc=s_dve)
            LGDONE.append(s_dve.n)
            if tt % 2 == 1:
                k2 = tt // 2
                P.emit("sync", lambda s, k2=k2: s.dma_start(
                    out=dram["lg_loc"].ap().rearrange(
                        "(tt p) e -> p tt e", p=128)[:, 2 * k2:2 * k2 + 2, :],
                    in_=lgl_sb[:].rearrange(
                        "p (tt e) -> p tt e", e=NE)[:, 2 * k2:2 * k2 + 2, :]),
                    waits=[(s_dve, LGDONE[-1]), (d_ms, d_ms.n)], inc=d_ms,
                    inc_k=16)
                P.emit("pool", lambda g_, k2=k2: g_.collective_compute(
                    "AllGather", ALU.bypass, replica_groups=[list(range(NCORES))],
                    ins=[dram["lg_loc"].ap()[k2 * 256:(k2 + 1) * 256, :]],
                    outs=[dram["lg_all"].ap()[k2 * 2048:(k2 + 1) * 2048, :]]),
                    waits=[(d_ms, d_ms.n)], inc=c_cc)
                LG_CHUNK.append(c_cc.n)
            if tt % (NT // AGC) == (NT // AGC) - 1:
                k = tt // (NT // AGC)
                P.emit("pool", lambda g_, k=k: g_.collective_compute(
                    "AllGather", ALU.bypass, replica_groups=[list(range(NCORES))],
                    ins=[dram["xm_my"].ap()[k * TPC:(k + 1) * TPC, :]],
                    outs=[dram["xm_pad"].ap()[1 + k * RPC: 1 + (k + 1) * RPC, :]]),
                    waits=n_xmst, inc=c_cc)

        def att_unit_gen(g, hp):
            nonlocal p4_free
            gv = hp // 2
            jj, sel = gv // 2, gv % 2
            kt0 = kTA[jj] if sel == 0 else kTB[jj]
            kt1 = kTB[jj] if sel == 0 else kTA[jj]
            ktg0 = kTGA if sel == 0 else kTGB
            ktg1 = kTGB if sel == 0 else kTGA
            qcol = (g % 2) * 512
            chunks = [("g", 0, 512, GT, 0)] + [
                (j, CHUNK_RANGES[j][0], CHUNK_RANGES[j][1], 128, 4 * g + 2 + j)
                for j in range(6)]
            av_last = [None, None]
            for ci, (j, a0_, b0, kw, vt) in enumerate(chunks):
                kc = 0 if j == "g" else g * 512 + 256 + 128 * j
                kg = kc // 512
                banks = [SBANK[0][ci % 2], SBANK[1][ci % 2]]
                w = [(s_dve, QDONE[(g, hp)])]
                if j == "g":
                    w.append((s_dve, GLOBC[0]))
                else:
                    w.append((s_dve, KDONE[(kg, jj)]))
                for bk in set(banks):
                    if ps_s_free[bk]:
                        w.append(ps_s_free[bk])

                if j == "g":
                    l0 = lambda jj=jj, ktg0=ktg0: ktg0[0:64, jj * 64:(jj + 1) * 64]
                    l1 = lambda jj=jj, ktg1=ktg1: ktg1[64:128, jj * 64:(jj + 1) * 64]
                else:
                    kr = (kc // 512 % 3) * 512 + (kc % 512)
                    l0 = lambda kr=kr, kw=kw, kt0=kt0: kt0[0:64, kr:kr + kw]
                    l1 = lambda kr=kr, kw=kw, kt1=kt1: kt1[64:128, kr:kr + kw]

                def sc_pair(te, a0_=a0_, b0=b0, kw=kw, banks=banks, l0=l0, l1=l1,
                            hp=hp, qcol=qcol):
                    te.matmul(pb[banks[0]][0:kw, a0_:b0], l0(),
                              qhat[hp][0:64, qcol + a0_:qcol + b0],
                              start=True, stop=True, tile_position=(0, 0))
                    return te.matmul(pb[banks[1]][0:kw, a0_:b0], l1(),
                                     qhat[hp][64:128, qcol + a0_:qcol + b0],
                                     start=True, stop=True, tile_position=(64, 0))
                P.emit("pe", sc_pair, waits=w, inc=s_pe)
                n_sc = s_pe.n
                n_ex = []
                for hh in range(2):
                    eb = 2 * hh + (ci % 2)
                    w = [(s_pe, n_sc)]
                    if eT_free[eb]:
                        w.append(eT_free[eb])
                    P.emit("act", lambda a_, eb=eb, a0_=a0_, b0=b0, kw=kw,
                           bk=banks[hh]: a_.activation(
                               out=eT_sb[eb][0:kw, 0:b0 - a0_],
                               in_=pb[bk][0:kw, a0_:b0], func=AF.Exp, scale=0.125),
                           waits=w, inc=s_act)
                    n_ex.append(s_act.n)
                    ps_s_free[banks[hh]] = (s_act, s_act.n)
                yield
                n_mk = [None, None]
                for hh in range(2):
                    eb = 2 * hh + (ci % 2)
                    if j == "g":
                        if g == 0:
                            P.emit("dve", lambda v, eb=eb: v.tensor_mul(
                                out=eT_sb[eb][0:GT, 0:512],
                                in0=eT_sb[eb][0:GT, 0:512], in1=maskgl_sb[:]),
                                waits=[(s_act, n_ex[hh]), (d_mk, ld_mgl),
                                       (s_dve, s_dve.n)], inc=s_dve)
                            n_mk[hh] = s_dve.n
                    else:
                        msb = maskg0_sb if g == 0 else masksh_sb
                        mo = MOFF[j]
                        P.emit("dve", lambda v, eb=eb, a0_=a0_, b0=b0, msb=msb,
                               mo=mo: v.tensor_mul(
                                   out=eT_sb[eb][:, 0:b0 - a0_],
                                   in0=eT_sb[eb][:, 0:b0 - a0_],
                                   in1=msb[:, mo:mo + (b0 - a0_)]),
                               waits=[(s_act, n_ex[hh]), (d_mk, ld_mgl),
                                      (s_dve, s_dve.n)], inc=s_dve)
                        n_mk[hh] = s_dve.n
                yield
                if j != "g":
                    while vt not in vdone:
                        yield
                for hh in range(2):
                    eb = 2 * hh + (ci % 2)
                    yb = 2 + hh
                    w = [(s_act, n_ex[hh])]
                    if j == "g":
                        w.append((s_dve, GLOBC[0]))
                    else:
                        w.append(vdone[vt])
                    if n_mk[hh] is not None:
                        w.append((s_dve, n_mk[hh]))
                    if ci == 0 and ps_y_free[yb]:
                        w.append(ps_y_free[yb])

                    if j == "g":
                        vap = lambda gv=gv: vstG[0:64, gv * 65: gv * 65 + 65]
                    else:
                        vs = (vt % 8) * 260
                        vap = lambda vs=vs, kw=kw, gv=gv: vst[0:kw, vs + gv * 65: vs + gv * 65 + 65]

                    def av(te, hh=hh, eb=eb, a0_=a0_, b0=b0, kw=kw, vap=vap,
                           ci=ci, yb=yb, nch=len(chunks)):
                        return te.matmul(
                            pb[yb][0:65, a0_:b0], vap(),
                            eT_sb[eb][0:kw, 0:b0 - a0_],
                            start=(ci == 0), stop=(ci == nch - 1),
                            skip_group_check=True)
                    P.emit("pe", av, waits=w, inc=s_pe)
                    av_last[hh] = s_pe.n
                    eT_free[eb] = (s_pe, s_pe.n)
                if j != "g":
                    rec = VUSED.setdefault(vt, [0, None])
                    rec[0] += 1
                    rec[1] = (s_pe, s_pe.n)
                yield
            n_rc = [None, None]
            for hh in range(2):
                yb = 2 + hh
                P.emit("act", lambda a, hh=hh, yb=yb: a.activation(
                    out=yv_sb[hh][64:65, :], in_=pb[yb][64:65, :], func=AF.Ln),
                    waits=[(s_pe, av_last[hh]), (s_act, s_act.n)], inc=s_act)
                P.emit("act", lambda a, hh=hh: a.activation(
                    out=rec_sb[hh][:], in_=yv_sb[hh][64:65, :], func=AF.Exp,
                    scale=-1.0), waits=[(s_act, s_act.n)], inc=s_act)
                n_rc[hh] = s_act.n
            for hh in range(2):
                yb = 2 + hh
                w = [(s_act, n_rc[hh]), (d_c0, LD0)]
                if p4_free:
                    w.append(p4_free)
                P.emit("pe", lambda te, hh=hh: te.matmul(
                    pb[4][0:64, :], onesk1_sb[0:1, 0:64], rec_sb[hh][:],
                    start=True, stop=True), waits=w, inc=s_pe)
                n_bc = s_pe.n
                P.emit("act", lambda a, hh=hh, yb=yb: a.copy(
                    out=yv_sb[hh][0:64, :], in_=pb[yb][0:64, :]),
                    waits=[(s_pe, n_bc)], inc=s_act)
                n_yv = s_act.n
                ps_y_free[yb] = (s_act, n_yv)
                wy = [(s_act, n_yv), (s_pe, n_w2mm), (s_dve, s_dve.n)]
                if P4G.get(g - 2) is not None:
                    wy.append((s_dve, P4G[g - 2]))
                P.emit("dve", lambda v, hh=hh, hp=hp, g=g: v.tensor_mul(
                    out=yT[hp][hh * 64:(hh + 1) * 64,
                               (g % 2) * 512:(g % 2) * 512 + 512],
                    in0=yv_sb[hh][0:64, :], in1=pb[4][0:64, :]),
                    waits=wy, inc=s_dve)
                p4_free = (s_dve, s_dve.n)
                yield
            ATT_DONE[(g, hp)] = s_dve.n

        for it in range(7):
            fill_q = []
            if it < NKVG:
                gb = it % 2
                w = []
                if xg_users[gb] is not None:
                    w.append((s_pe, xg_users[gb]))
                ldx = gp_dma(lambda gb=gb: xTg[gb][:],
                             lambda g5=it: dram["xT"].ap().rearrange(
                                 "(c p) n -> p c n", p=128)[:, :, g5 * 512:(g5 + 1) * 512],
                             d_x[gb], waits=w)
                wc = [cs_users[gb]] if cs_users[gb] else []
                ldc = sy_dma(lambda gb=gb: cscg[gb][:],
                             lambda g5=it: dram["csc"].ap()[:, g5 * 512:(g5 + 1) * 512],
                             d_cs[gb], waits=wc)
                ldc2 = sy_dma(lambda gb=gb: cssg[gb][:],
                              lambda g5=it: dram["css"].ap()[:, g5 * 512:(g5 + 1) * 512],
                              d_cs[gb])
                xbuf = (gb, ldx, ldc2)
                for od in range(2):
                    fill_q.append(proj_unit_gen("k", it, od, xbuf, KDONE, (it, od)))
                nt4 = 1 if it == 4 else 4
                for t4 in range(nt4):
                    fill_q.append(vproj_gen(it * 4 + t4, xbuf))
                if it == 4:
                    vdefer[0] = xbuf
                if it == 0:
                    fill_q.append(glob_copy_gen())
                if it >= 1:
                    for od in range(8):
                        fill_q.append(proj_unit_gen("q", it - 1, od, xbuf,
                                                    QDONE, (it - 1, od)))
                    fill_q.append(xrw_gen(it - 1, gb, ldx))
            if it == 5:
                for t4 in range(1, 4):
                    fill_q.append(vproj_gen(16 + t4, vdefer[0]))
            att_q = []
            if 2 <= it <= 5:
                att_q = [att_unit_gen(it - 2, hp) for hp in range(8)]
            p4_q = []
            if 3 <= it <= 6:
                p4_q = [emit_p4_tile(t) for t in range(4 * (it - 3), 4 * (it - 3) + 4)]
            drive2(att_q, fill_q, p4_q)
            if it < NKVG:
                xg_users[gb] = s_pe.n
                cs_users[gb] = (s_dve, s_dve.n)
            if 2 <= it <= 5:
                ATTG[it - 2] = s_dve.n
            if 3 <= it <= 6:
                P4G[it - 3] = s_dve.n
        P.marks['P3'] = P.gid
        XM_AG = c_cc.n

        P.marks['P4'] = P.gid
        P4 = [(s_pe, s_pe.n), (s_act, s_act.n), (s_dve, s_dve.n)]
        es_a.close()

        ptr_free = None
        es5 = contextlib.ExitStack()
        lga_sb = sbuf(es5, "lga_sb", [128, 1024], F32)
        iotae_sb = sbuf(es5, "iotae_sb", [128, 1024], F32)
        prod_sb = sbuf(es5, "prod_sb", [128, 1024], F32)
        tmr2_sb = [sbuf(es5, f"tmr2_sb{i}", [128, 128], F32) for i in range(2)]
        v1_sb = sbuf(es5, "v1_sb", [128, 128], F32)
        v2_sb = sbuf(es5, "v2_sb", [128, 128], F32)
        top1_sb = sbuf(es5, "top1_sb", [128, 1024], F32)
        top2_sb = sbuf(es5, "top2_sb", [128, 1024], F32)
        mskd_sb = sbuf(es5, "mskd_sb", [128, 1024], F32)
        asg_sb = sbuf(es5, "asg_sb", [128, 1024], F32)
        rank_sb = sbuf(es5, "rank_sb", [128, 1024], F32)
        rt_sb = sbuf(es5, "rt_sb", [128, NE], F32)
        base_sb = sbuf(es5, "base_sb", [128, NE], F32)
        g1_sb = sbuf(es5, "g1_sb", [128, 128], F32)
        g2_sb = sbuf(es5, "g2_sb", [128, 128], F32)
        fld_sb = [sbuf(es5, f"fld_sb{i}", [128, 128], F32) for i in range(6)]
        acc_sb = [sbuf(es5, f"acc_sb{i}", [128, 128], F32) for i in range(4)]
        tmpr_sb = [sbuf(es5, f"tmpr_sb{i}", [128, 128], F32) for i in range(2)]

        for kk in range(8):
            for cc in range(NCORES):
                P.emit("sync", lambda s, kk=kk, cc=cc: s.dma_start(
                    out=lga_sb[cc * 16 + kk * 2: cc * 16 + kk * 2 + 2, :]
                        .rearrange("p (n e) -> p n e", e=NE),
                    in_=dram["lg_all"].ap()[kk * 2048 + cc * 256:
                                            kk * 2048 + (cc + 1) * 256, :]
                        .rearrange("(p n) e -> p n e", p=2)),
                    waits=[(c_cc, LG_CHUNK[kk])] + P4, inc=d_lg, inc_k=16)
        n_lga = d_lg.n
        P.emit("sync", lambda s: s.dma_start(out=iotae_sb[:], in_=dram["iotae"].ap()),
               waits=[], inc=d_io, inc_k=16)
        n_lga2 = d_io.n

        def ev(t, e):
            return t[:].rearrange("p (n e) -> p e n", e=NE)[:, e:e + 1, :] \
                       .rearrange("p a n -> p (a n)")

        P.emit("dve", lambda v: v.reduce_max(
            out=v1_sb[:], in_=lga_sb[:].rearrange("p (n e) -> p n e", e=NE),
            axis=AX.X), waits=[(d_lg, n_lga), (s_dve, s_dve.n)], inc=s_dve)
        n_v1 = s_dve.n
        for e in range(NE):
            P.emit("dve", lambda v, e=e: v.tensor_tensor(
                out=ev(top1_sb, e), in0=ev(lga_sb, e), in1=v1_sb[:], op=ALU.is_equal),
                waits=[(s_dve, n_v1)], inc=s_dve)
        n_t1 = s_dve.n
        P.emit("dve", lambda v: v.scalar_tensor_tensor(
            out=mskd_sb[:], in0=top1_sb[:], scalar=-BIG, in1=lga_sb[:],
            op0=ALU.mult, op1=ALU.add), waits=[(s_dve, n_t1)], inc=s_dve)
        n_msk = s_dve.n
        P.emit("dve", lambda v: v.reduce_max(
            out=v2_sb[:], in_=mskd_sb[:].rearrange("p (n e) -> p n e", e=NE),
            axis=AX.X), waits=[(s_dve, n_msk)], inc=s_dve)
        n_v2 = s_dve.n
        for e in range(NE):
            P.emit("dve", lambda v, e=e: v.tensor_tensor(
                out=ev(top2_sb, e), in0=ev(mskd_sb, e), in1=v2_sb[:], op=ALU.is_equal),
                waits=[(s_dve, n_v2)], inc=s_dve)
        n_t2 = s_dve.n
        P.emit("dve", lambda v: v.tensor_sub(out=tmpr_sb[0][:], in0=v1_sb[:],
                                             in1=v2_sb[:]),
               waits=[(s_dve, n_t2)], inc=s_dve)
        n_d = s_dve.n
        P.emit("act", lambda a: a.activation(out=g1_sb[:], in_=tmpr_sb[0][:],
                                             func=AF.Sigmoid),
               waits=[(s_dve, n_d), (s_act, s_act.n)], inc=s_act)
        n_g1 = s_act.n
        P.emit("act", lambda a: a.activation(out=g2_sb[:], in_=g1_sb[:],
                                             func=AF.Copy, scale=-1.0, bias=1.0),
               waits=[(s_act, n_g1)], inc=s_act)
        n_g2 = s_act.n
        P.emit("dve", lambda v: v.tensor_add(out=asg_sb[:], in0=top1_sb[:],
                                             in1=top2_sb[:]),
               waits=[(s_dve, n_t2)], inc=s_dve)
        n_asg = s_dve.n
        P.emit("dve", lambda v: v.reduce_sum(
            out=rt_sb[:], in_=asg_sb[:].rearrange("p (n e) -> p e n", e=NE),
            axis=AX.X), waits=[(s_dve, n_asg)], inc=s_dve)
        n_rt = s_dve.n
        w = [(s_dve, n_rt), (d_c0, LD0)]
        if ps_y_free[2]:
            w.append(ps_y_free[2])
        mm_group(lambda: pb[2][:, 0:NE], [(lambda: lstrict_sb[:], lambda: rt_sb[:])],
                 waits=w)
        n_bm = s_pe.n
        P.emit("act", lambda a: a.copy(out=base_sb[:], in_=pb[2][:, 0:NE]),
               waits=[(s_pe, n_bm)], inc=s_act)
        n_bse = s_act.n
        for e in range(NE):
            P.emit("dve", lambda v, e=e: v.tensor_tensor_scan(
                out=ev(rank_sb, e), data0=ev(asg_sb, e), data1=zrow_sb[:],
                initial=0.0, op0=ALU.add, op1=ALU.add),
                waits=[(s_dve, s_dve.n), (s_dve, n_zrow)], inc=s_dve)
            P.emit("dve", lambda v, e=e: v.scalar_tensor_tensor(
                out=ev(rank_sb, e), in0=ev(rank_sb, e), scalar=base_sb[:, e:e + 1],
                in1=ev(asg_sb, e), op0=ALU.add, op1=ALU.subtract),
                waits=[(s_dve, s_dve.n), (s_act, n_bse)], inc=s_dve)
        n_rank = s_dve.n

        def accum2(dst, maskt, srct, n_after, extra=()):
            P.emit("dve", lambda v: v.tensor_mul(out=prod_sb[:], in0=maskt[:],
                                                 in1=srct[:]),
                   waits=[(s_dve, n_after)] + list(extra), inc=s_dve)
            P.emit("dve", lambda v: v.reduce_sum(
                out=dst[:], in_=prod_sb[:].rearrange("p (n e) -> p n e", e=NE),
                axis=AX.X), waits=[(s_dve, s_dve.n)], inc=s_dve)
            return s_dve.n

        n_e1 = accum2(acc_sb[0], top1_sb, iotae_sb, n_rank, [(d_io, n_lga2)])
        n_e2 = accum2(acc_sb[1], top2_sb, iotae_sb, n_e1)
        n_rk1 = accum2(acc_sb[2], top1_sb, rank_sb, n_e2)
        n_rk2 = accum2(acc_sb[3], top2_sb, rank_sb, n_rk1)

        def fields_gen(et, rkt, fi, scr, n_after):
            P.emit("dve", lambda v: v.tensor_scalar(
                out=scr[0][:], in0=rkt[:], scalar1=float(CAP), scalar2=None,
                op0=ALU.is_lt), waits=[(s_dve, n_after)], inc=s_dve)
            n_a = s_dve.n
            yield
            P.emit("dve", lambda v: v.tensor_scalar(
                out=scr[1][:], in0=scr[0][:], scalar1=-1.0, scalar2=1.0,
                op0=ALU.mult, op1=ALU.add), waits=[(s_dve, n_a)], inc=s_dve)
            n_b = s_dve.n
            yield
            P.emit("dve", lambda v: v.scalar_tensor_tensor(
                out=fld_sb[fi][:], in0=et[:], scalar=float(CAP), in1=rkt[:],
                op0=ALU.mult, op1=ALU.add), waits=[(s_dve, n_b)], inc=s_dve)
            n_c = s_dve.n
            yield
            P.emit("dve", lambda v: v.scalar_tensor_tensor(
                out=fld_sb[fi][:], in0=scr[1][:], scalar=BIG, in1=fld_sb[fi][:],
                op0=ALU.mult, op1=ALU.add), waits=[(s_dve, n_c)], inc=s_dve)
            yield
            P.emit("dve", lambda v: v.memset(scr[0][:], 0.0),
                   waits=[(s_dve, s_dve.n)], inc=s_dve)
            n_d = s_dve.n
            yield
            for i in range(1, AGC):
                P.emit("dve", lambda v, i=i: v.tensor_scalar(
                    out=scr[1][:], in0=rkt[:], scalar1=float(CPC * i),
                    scalar2=None, op0=ALU.is_ge),
                    waits=[(s_dve, s_dve.n)], inc=s_dve)
                P.emit("dve", lambda v: v.tensor_add(
                    out=scr[0][:], in0=scr[0][:], in1=scr[1][:]),
                    waits=[(s_dve, s_dve.n)], inc=s_dve)
                yield
            P.emit("dve", lambda v: v.scalar_tensor_tensor(
                out=fld_sb[fi + 1][:], in0=et[:], scalar=float(CPC), in1=rkt[:],
                op0=ALU.mult, op1=ALU.add), waits=[(s_dve, s_dve.n)], inc=s_dve)
            yield
            P.emit("dve", lambda v: v.scalar_tensor_tensor(
                out=fld_sb[fi + 1][:], in0=scr[0][:],
                scalar=float((NCORES - 1) * CPC), in1=fld_sb[fi + 1][:],
                op0=ALU.mult, op1=ALU.add), waits=[(s_dve, s_dve.n)], inc=s_dve)
            yield
            P.emit("dve", lambda v: v.tensor_scalar(
                out=scr[0][:], in0=rkt[:], scalar1=float(CAP), scalar2=None,
                op0=ALU.is_lt), waits=[(s_dve, s_dve.n)], inc=s_dve)
            yield
            P.emit("dve", lambda v: v.tensor_mul(
                out=fld_sb[fi + 1][:], in0=fld_sb[fi + 1][:], in1=scr[0][:]),
                waits=[(s_dve, s_dve.n)], inc=s_dve)
            yield
            P.emit("dve", lambda v: v.tensor_scalar(
                out=scr[0][:], in0=scr[0][:], scalar1=-1.0, scalar2=1.0,
                op0=ALU.mult, op1=ALU.add), waits=[(s_dve, s_dve.n)], inc=s_dve)
            yield
            P.emit("dve", lambda v: v.scalar_tensor_tensor(
                out=fld_sb[fi + 1][:], in0=scr[0][:], scalar=float(PADROW),
                in1=fld_sb[fi + 1][:], op0=ALU.mult, op1=ALU.add),
                waits=[(s_dve, s_dve.n)], inc=s_dve)

        _ga = fields_gen(acc_sb[0], acc_sb[2], 0, tmpr_sb, n_rk2)
        _gb = fields_gen(acc_sb[1], acc_sb[3], 2, tmr2_sb, n_rk2)
        _alive = [_ga, _gb]
        while _alive:
            for _gx in list(_alive):
                try:
                    next(_gx)
                except StopIteration:
                    _alive.remove(_gx)
        n_f2 = s_dve.n
        P.emit("dve", lambda v: v.tensor_copy(out=fld_sb[4][:], in_=g1_sb[:]),
               waits=[(s_act, n_g2), (s_dve, n_f2)], inc=s_dve)
        P.emit("dve", lambda v: v.tensor_copy(out=fld_sb[5][:], in_=g2_sb[:]),
               waits=[(s_dve, s_dve.n)], inc=s_dve)
        n_flds = s_dve.n

        n_fT = []
        for fi in range(6):
            w = [(s_dve, n_flds), (d_c0, LD0)]
            if ptr_free:
                w.append(ptr_free)
            elif ps_y_free[3]:
                w.append(ps_y_free[3])
            mm_group(lambda: pb[3][:, 0:16],
                     [(lambda fi=fi: fld_sb[fi][:], lambda: rowsel_sb[:])], waits=w)
            n_m = s_pe.n
            P.emit("act", lambda a, fi=fi: a.copy(out=fT_sb[fi][:],
                                                  in_=pb[3][:, 0:16]),
                   waits=[(s_pe, n_m), (s_act, s_act.n)], inc=s_act)
            ptr_free = (s_act, s_act.n)
            n_fT.append(s_act.n)
        for i, fi in enumerate((0, 2, 1, 3)):
            P.emit("dve", lambda v, i=i, fi=fi: v.tensor_copy(
                out=fTi_sb[i][:], in_=fT_sb[fi][:]),
                waits=[(s_act, n_fT[fi]), (s_dve, s_dve.n)], inc=s_dve)
        for sl in range(2):
            P.emit("dve", lambda v, sl=sl: v.tensor_copy(
                out=val_sb[sl][:].rearrange("p (r two) -> p r two", two=2)[:, :, 0:1],
                in_=tokval_sb[:].rearrange("p (r one) -> p r one", one=1)),
                waits=[(d_c0, LD0), (s_dve, s_dve.n)], inc=s_dve)
            P.emit("dve", lambda v, sl=sl: v.tensor_copy(
                out=val_sb[sl][:].rearrange("p (r two) -> p r two", two=2)[:, :, 1:2],
                in_=fT_sb[4 + sl][:].rearrange("p (r one) -> p r one", one=1)),
                waits=[(s_act, n_fT[4 + sl]), (s_dve, s_dve.n)], inc=s_dve)
        n_vals = s_dve.n

        for sl in range(2):
            for r in range(16):
                P.emit("pool", lambda g_, sl=sl, r=r: g_.indirect_dma_start(
                    out=dram["table_loc"].ap(),
                    out_offset=IndirectOffsetOnAxis(fTi_sb[sl][:, r:r + 1], 0),
                    in_=val_sb[sl][:, 2 * r:2 * r + 2], in_offset=None,
                    bounds_check=REGS["b_scat"], oob_is_err=False),
                    waits=[(s_dve, n_vals), (d_zi, ZINIT)], inc=d_sc, inc_k=16)
        N_SCAT = d_sc.n
        P.emit("pool", lambda g_: g_.collective_compute(
            "ReduceScatter", ALU.add, replica_groups=[list(range(NCORES))],
            ins=[dram["table_loc"].ap()], outs=[dram["table_my"].ap()]),
            waits=[(d_sc, N_SCAT)], inc=c_cc)
        RS_DONE = c_cc.n

        P.marks['RT'] = P.gid
        RT = [(s_pe, s_pe.n), (s_act, s_act.n), (s_dve, s_dve.n), (d_sc, N_SCAT)]
        es5.close()

        es6 = contextlib.ExitStack()
        eW_sb = sbuf(es6, "eW_sb", [128, 8 * DFF], BF16)
        eV_sb = sbuf(es6, "eV_sb", [128, 8 * DFF], BF16)
        eWo_sb = sbuf(es6, "eWo_sb", [128, 16 * D], BF16)
        xeT = [sbuf(es6, f"xeT{i}", [128, 1024], BF16) for i in range(8)]
        gT = [sbuf(es6, f"gT{i}", [128, 512], BF16) for i in range(16)]

        ld_eW = gp_dma(lambda: eW_sb[:].rearrange("p (c n) -> p c n", n=DFF), lambda: r3("eW", DFF), d_eW, waits=RT)
        ld_eV = gp_dma(lambda: eV_sb[:].rearrange("p (c n) -> p c n", n=DFF), lambda: r3("eV", DFF), d_eV)
        ld_eWo = gp_dma(lambda: eWo_sb[:].rearrange("p (c n) -> p c n", n=D), lambda: r3("eWo", D), d_eWo)

        P.emit("sync", lambda s: s.dma_start(
            out=tab_sb[:].rearrange("p (j f) -> p j f", f=2),
            in_=dram["table_my"].ap().rearrange("(j p) f -> p j f", p=128)),
            waits=[(c_cc, RS_DONE)], inc=d_ms, inc_k=16)
        n_tab = d_ms.n
        P.emit("dve", lambda v: v.tensor_copy(
            out=gidx_sb[:].rearrange("p (r one) -> p r one", one=1),
            in_=tab_sb[:].rearrange("p (r two) -> p r two", two=2)[:, :, 0:1]),
            waits=[(d_ms, n_tab), (s_dve, s_dve.n)], inc=s_dve)
        n_gidx = s_dve.n
        P.emit("dve", lambda v: v.tensor_copy(
            out=gate_sb[:].rearrange("p (r one) -> p r one", one=1),
            in_=tab_sb[:].rearrange("p (r two) -> p r two", two=2)[:, :, 1:2]),
            waits=[(s_dve, s_dve.n)], inc=s_dve)
        n_gate = s_dve.n

        xe_free = [None, None]
        pa_free = [pp_free[0], (s_act, n_bse)]
        pbk_free = [pp_free[1], ptr_free]
        py_free = [pbc_free if False else None, None]
        py_free = [None, None]
        gT_users = None
        ye_free = [None, None]
        TRD = {0: None}
        TRQ = []
        TR_DONE = {}
        tr_left = {}

        def emit_gather(j):
            w = [(s_dve, n_gidx), (c_cc, XM_AG), (d_zi, ZINIT)]
            if xe_free[j % 2]:
                w.append(xe_free[j % 2])
            P.emit("pool", lambda g_, j=j: g_.indirect_dma_start(
                out=xe_sb[j % 2][:], out_offset=None,
                in_=dram["xm_pad"].ap(),
                in_offset=IndirectOffsetOnAxis(gidx_sb[:, j:j + 1], 0),
                bounds_check=REGS["b_gath"], oob_is_err=False),
                waits=w, inc=d_ix[j % 2], inc_k=16)
            return (d_ix[j % 2], d_ix[j % 2].n)

        GATH = {}
        XU = [None, None]
        allow = [2]
        for j in range(2):
            GATH[j] = emit_gather(j)
            for fc in range(8):
                TRQ.append((j, fc))
        nextg = [2]

        def emit_transpose_pair():
            for _ in range(2):
                if not TRQ or TRQ[0][0] // 4 >= allow[0]:
                    return
                j, fc = TRQ.pop(0)
                half = (j // 4) % 2
                w = [GATH[j], (d_c0, LD0)]
                if TRD[0]:
                    w.append(TRD[0])
                if XU[half]:
                    w.append(XU[half])
                P.emit("pe", lambda te, j=j, fc=fc: te.transpose(
                    pbt[:], xe_sb[j % 2][:, fc * 128:(fc + 1) * 128], identb_sb[:]),
                    waits=w, inc=s_pe)
                n_tr = s_pe.n
                P.emit("dve", lambda v, j=j, fc=fc, half=half: v.tensor_copy(
                    out=xeT[fc][:, half * 512 + (j % 4) * 128: half * 512 + (j % 4) * 128 + 128],
                    in_=pbt[:]),
                    waits=[(s_pe, n_tr), (s_dve, s_dve.n)], inc=s_dve)
                TRD[0] = (s_dve, s_dve.n)
                if fc == 7:
                    TR_DONE[j] = s_dve.n
                    xe_free[j % 2] = (s_pe, s_pe.n)
                    if nextg[0] < 16:
                        GATH[nextg[0]] = emit_gather(nextg[0])
                        for fc2 in range(8):
                            TRQ.append((nextg[0], fc2))
                        nextg[0] += 1

        while TRQ and TRQ[0][0] < 4:
            emit_transpose_pair()

        for jg in range(4):
            for hc in range(16):
                ab = hc % 2
                w = [(d_eW, ld_eW), (s_dve, TR_DONE[jg * 4 + 3])]
                if pa_free[ab]:
                    w.append(pa_free[ab])
                mm_group(lambda ab=ab: pb[[0, 2][ab]][:],
                         [(lambda fc=fc, hc=hc: eW_sb[:, fc * DFF + hc * 128: fc * DFF + hc * 128 + 128],
                           lambda fc=fc, jg=jg: xeT[fc][:, (jg % 2) * 512:(jg % 2) * 512 + 512])
                          for fc in range(8)], waits=w)
                n_a = s_pe.n
                emit_transpose_pair()
                w = [(d_eV, ld_eV)]
                if pbk_free[ab]:
                    w.append(pbk_free[ab])
                mm_group(lambda ab=ab: pb[[1, 3][ab]][:],
                         [(lambda fc=fc, hc=hc: eV_sb[:, fc * DFF + hc * 128: fc * DFF + hc * 128 + 128],
                           lambda fc=fc, jg=jg: xeT[fc][:, (jg % 2) * 512:(jg % 2) * 512 + 512])
                          for fc in range(8)], waits=w)
                n_b = s_pe.n
                emit_transpose_pair()
                si = hc % 2
                P.emit("act", lambda a, si=si, ab=ab: a.activation(
                    out=silu_sb[si][:], in_=pb[[1, 3][ab]][:], func=AF.Silu),
                    waits=[(s_pe, n_b), (s_act, s_act.n)], inc=s_act)
                n_si = s_act.n
                pbk_free[ab] = (s_act, n_si)
                wg = [(s_act, n_si), (s_dve, s_dve.n)]
                if gT_users:
                    wg.append(gT_users)
                P.emit("dve", lambda v, hc=hc, si=si, ab=ab: v.tensor_mul(
                    out=gT[hc][:], in0=pb[[0, 2][ab]][:], in1=silu_sb[si][:]),
                    waits=wg, inc=s_dve)
                pa_free[ab] = (s_dve, s_dve.n)
            N_GT = s_dve.n
            XU[jg % 2] = (s_pe, s_pe.n)
            allow[0] = jg + 3
            for j4 in range(4):
                jt = jg * 4 + j4
                n_y = []
                for half in range(2):
                    w = [(d_eWo, ld_eWo), (s_dve, N_GT)]
                    if py_free[half]:
                        w.append(py_free[half])
                    if jg == 0 and j4 == 0:
                        if half == 0 and p4_free:
                            w.append(p4_free)
                        if half == 1 and ps_s_free[5]:
                            w.append(ps_s_free[5])
                    mm_group(lambda half=half: pb[4 + half][:],
                             [(lambda hc=hc, jt=jt: gT[hc][:, (jt % 4) * 128:(jt % 4 + 1) * 128],
                               lambda hc=hc, half=half: eWo_sb[:, hc * D + half * 512: hc * D + half * 512 + 512])
                              for hc in range(16)], waits=w)
                    n_y.append(s_pe.n)
                    emit_transpose_pair()
                buf = jt % 2
                w0 = [(s_pe, n_y[0]), (s_pe, n_y[1]), (s_dve, n_gate),
                      (s_dve, s_dve.n)]
                if ye_free[buf]:
                    w0.append(ye_free[buf])
                for half in range(2):
                    P.emit("dve", lambda v, half=half, jt=jt, buf=buf: v.tensor_scalar(
                        out=ye_sb[buf][:, half * 512:(half + 1) * 512],
                        in0=pb[4 + half][:], scalar1=gate_sb[:, jt:jt + 1],
                        scalar2=None, op0=ALU.mult),
                        waits=w0 if half == 0 else [(s_dve, s_dve.n)], inc=s_dve)
                    py_free[half] = (s_dve, s_dve.n)
                n_ye = s_dve.n
                P.emit("sync", lambda s, jt=jt, buf=buf: s.dma_start(
                    out=dram["yexp_my"].ap()[jt * 128:(jt + 1) * 128, :],
                    in_=ye_sb[buf][:]), waits=[(s_dve, n_ye)], inc=d_ye[buf], inc_k=16)
                ye_free[buf] = (d_ye[buf], d_ye[buf].n)
            gT_users = (s_pe, s_pe.n)
            P.emit("pool", lambda g_, m=jg: g_.collective_compute(
                "AllGather", ALU.bypass, replica_groups=[list(range(NCORES))],
                ins=[dram["yexp_my"].ap()[m * CPC:(m + 1) * CPC, :]],
                outs=[dram["yexp_pad"].ap()[m * RPC:(m + 1) * RPC, :]]),
                waits=[(d_ye[0], d_ye[0].n), (d_ye[1], d_ye[1].n)], inc=c_cc)
        YE_AG = c_cc.n

        P.marks['P7'] = P.gid
        f1_sb = r_sb
        o_sb = xmf_sb
        yc_free = [None] * 4
        f1_free = [r_free[0], r_free[1]]
        o_free = [xmf_free[0], xmf_free[1]]
        xmr_free = [None, None]

        def emit_p8_tile(tt):
            buf = tt % 2
            for q in range(2):
                cb = 2 * buf + q
                ww = [(c_cc, YE_AG), (d_zi, ZINIT)]
                if yc_free[cb]:
                    ww.append(yc_free[cb])
                P.emit("pool", lambda g_, q=q, tt=tt, cb=cb: g_.indirect_dma_start(
                    out=yc_sb[cb][:], out_offset=None,
                    in_=dram["yexp_pad"].ap(),
                    in_offset=IndirectOffsetOnAxis(fTi_sb[2 + q][:, tt:tt + 1], 0),
                    bounds_check=REGS["b_gath"], oob_is_err=False),
                    waits=ww, inc=d_cb[cb], inc_k=16)
            n_gth = [(d_cb[2 * buf], d_cb[2 * buf].n),
                     (d_cb[2 * buf + 1], d_cb[2 * buf + 1].n)]
            ww = [(d_xmf[0], d_xmf[0].n), (d_xmf[1], d_xmf[1].n)]
            if xmr_free[buf]:
                ww.append(xmr_free[buf])
            if o_free[buf]:
                ww.append(o_free[buf])
            P.emit("sync", lambda s, tt=tt, buf=buf: s.dma_start(
                out=o_sb[buf][:], in_=dram["xmf"].ap()[tt * 128:(tt + 1) * 128, :]),
                waits=ww, inc=d_xt[buf], inc_k=16)
            n_xr = (d_xt[buf], d_xt[buf].n)
            yield
            ww = n_gth + [(s_dve, s_dve.n)]
            if f1_free[buf]:
                ww.append(f1_free[buf])
            P.emit("dve", lambda v, buf=buf: v.tensor_add(
                out=f1_sb[buf][:], in0=yc_sb[2 * buf][:], in1=yc_sb[2 * buf + 1][:]),
                waits=ww, inc=s_dve)
            yc_free[2 * buf] = (s_dve, s_dve.n)
            yc_free[2 * buf + 1] = (s_dve, s_dve.n)
            n_ad = s_dve.n
            yield
            P.emit("dve", lambda v, buf=buf: v.tensor_add(
                out=f1_sb[buf][:], in0=f1_sb[buf][:], in1=o_sb[buf][:]),
                waits=[(s_dve, n_ad), n_xr], inc=s_dve)
            n_r2 = s_dve.n
            f1_free[buf] = (s_dve, n_r2)
            xmr_free[buf] = (s_dve, n_r2)
            o_free[buf] = None
            yield
            P.emit("dve", lambda v, buf=buf: v.reduce_sum(
                out=st_sb[buf][:, 0:1], in_=f1_sb[buf][:], axis=AX.X),
                waits=[(s_dve, n_r2)], inc=s_dve)
            n_s1 = s_dve.n
            P.emit("act", lambda a, buf=buf: a.activation(
                out=sqln_sb[buf][:], in_=f1_sb[buf][:], func=AF.Square),
                waits=[(s_dve, n_r2), (s_act, s_act.n)], inc=s_act)
            n_sq = s_act.n
            yield
            P.emit("dve", lambda v, buf=buf: v.reduce_sum(
                out=st_sb[buf][:, 1:2], in_=sqln_sb[buf][:], axis=AX.X),
                waits=[(s_act, n_sq), (s_dve, n_s1)], inc=s_dve)
            n_s2 = s_dve.n
            yield
            P.emit("dve", lambda v, buf=buf: v.tensor_scalar(
                out=st_sb[buf][:, 2:3], in0=st_sb[buf][:, 0:1], scalar1=1.0 / D,
                scalar2=None, op0=ALU.mult), waits=[(s_dve, n_s2)], inc=s_dve)
            P.emit("dve", lambda v, buf=buf: v.tensor_mul(
                out=st_sb[buf][:, 3:4], in0=st_sb[buf][:, 2:3],
                in1=st_sb[buf][:, 2:3]), waits=[(s_dve, s_dve.n)], inc=s_dve)
            P.emit("dve", lambda v, buf=buf: v.tensor_scalar(
                out=st_sb[buf][:, 4:5], in0=st_sb[buf][:, 1:2], scalar1=1.0 / D,
                scalar2=None, op0=ALU.mult), waits=[(s_dve, s_dve.n)], inc=s_dve)
            P.emit("dve", lambda v, buf=buf: v.tensor_sub(
                out=st_sb[buf][:, 4:5], in0=st_sb[buf][:, 4:5],
                in1=st_sb[buf][:, 3:4]), waits=[(s_dve, s_dve.n)], inc=s_dve)
            n_var = s_dve.n
            yield
            P.emit("act", lambda a, buf=buf: a.activation(
                out=st_sb[buf][:, 5:6], in_=st_sb[buf][:, 4:5], func=AF.Ln,
                bias=EPS_LN), waits=[(s_dve, n_var), (s_act, s_act.n)], inc=s_act)
            P.emit("act", lambda a, buf=buf: a.activation(
                out=st_sb[buf][:, 6:7], in_=st_sb[buf][:, 5:6], func=AF.Exp,
                scale=-0.5), waits=[(s_act, s_act.n)], inc=s_act)
            n_rs = s_act.n
            yield
            ww = [(s_act, n_rs), (s_dve, s_dve.n)]
            if o_free[buf]:
                ww.append(o_free[buf])
            P.emit("dve", lambda v, buf=buf: v.tensor_scalar(
                out=o_sb[buf][:], in0=f1_sb[buf][:], scalar1=st_sb[buf][:, 2:3],
                scalar2=st_sb[buf][:, 6:7], op0=ALU.subtract, op1=ALU.mult),
                waits=ww, inc=s_dve)
            n_o = s_dve.n
            P.emit("sync", lambda s, tt=tt, buf=buf: s.dma_start(
                out=out_t.ap()[tt * 128:(tt + 1) * 128, :], in_=o_sb[buf][:]),
                waits=[(s_dve, n_o)], inc=d_st[buf], inc_k=16)
            o_free[buf] = (d_st[buf], d_st[buf].n)

        gens = [emit_p8_tile(t) for t in range(NT)]
        import itertools as _it
        for i in range(0, NT, 2):
            ga, gb = gens[i], gens[i + 1]
            alive = [ga, gb]
            while alive:
                for gx in list(alive):
                    try:
                        next(gx)
                    except StopIteration:
                        alive.remove(gx)
        P.emit("sync", lambda s: s.nop(),
               waits=[(d_st[0], d_st[0].n), (d_st[1], d_st[1].n)])
        P.marks['P8'] = P.gid
        es6.close()

        _LAST_PLAN["marks"] = dict(P.marks)
        _LAST_PLAN["total"] = P.gid
        P.trunc = trunc
        for s in P.sems:
            s.h = es0.enter_context(nc.semaphore(s.name))
        with nc.Block() as block:
            @block.tensor
            def _(e):
                P.replay("pe", e, nc)

            @block.scalar
            def _(e):
                P.replay("act", e, nc)

            @block.vector
            def _(e):
                P.replay("dve", e, nc)

            @block.gpsimd
            def _(e):
                P.replay("pool", e, nc)

            @block.sync
            def _(e):
                P.replay("sync", e, nc)
    return nc


_NC_CACHE = {}


def kernel(L0_attn_x, L0_attn_Wq, L0_attn_Wk, L0_attn_Wv, L0_attn_Wo,
           L0_ffn_router_w, L0_ffn_router_b, L0_ffn_W, L0_ffn_V, L0_ffn_W_out):
    X = np.asarray(L0_attn_x, dtype=np.float32)
    Wq = np.asarray(L0_attn_Wq, dtype=np.float32)
    Wk = np.asarray(L0_attn_Wk, dtype=np.float32)
    Wv = np.asarray(L0_attn_Wv, dtype=np.float32)
    Wo = np.asarray(L0_attn_Wo, dtype=np.float32)
    RW = np.asarray(L0_ffn_router_w, dtype=np.float32)
    RB = np.asarray(L0_ffn_router_b, dtype=np.float32)
    EW = np.asarray(L0_ffn_W, dtype=np.float32)
    EV = np.asarray(L0_ffn_V, dtype=np.float32)
    EWo = np.asarray(L0_ffn_W_out, dtype=np.float32)
    if "nc" not in _NC_CACHE:
        _NC_CACHE["nc"] = build_nc()
    in_maps = [stage_core_inputs(c, X, Wq, Wk, Wv, Wo, RW, RB, EW, EV, EWo)
               for c in range(NCORES)]
    res = run_bass_kernel_spmd(_NC_CACHE["nc"], in_maps, core_ids=list(range(NCORES)))
    out = np.concatenate([np.asarray(res.results[c]["out"]) for c in range(NCORES)],
                         axis=0)
    return out.reshape(B, S, D).astype(np.float32)


# version 12
# speedup vs baseline: 1.0018x; 1.0018x over previous
import contextlib
import numpy as np
import ml_dtypes
import concourse.bass as bass
import concourse.mybir as mybir
from concourse.bass_utils import run_bass_kernel_spmd
from concourse.bass_core_rust import IndirectOffsetOnAxis

F32 = mybir.dt.float32
BF16 = mybir.dt.bfloat16
I32 = mybir.dt.int32
AF = mybir.ActivationFunctionType
ALU = mybir.AluOpType
AX = mybir.AxisListType
BF = ml_dtypes.bfloat16

NCORES = 8
B, S, D = 2, 8192, 1024
T = 2048
HALO = 512
KV = HALO + T
NKV, HD = 4, 64
LW, GT = 256, 64
DFF = 2048
NE, CAP = 8, 2048
NG, NT = T // 512, T // 128
NKVG, NKVT = KV // 512, KV // 128
AGC = 4
CPC = CAP // AGC
RPC = NCORES * CPC
TPC = T // AGC
EPS_RMS = float(np.finfo(np.float32).eps)
EPS_LN = 1e-5
BIG = 1.0e9
PADROW = NCORES * CAP
CHUNK_RANGES = [(0, 127), (0, 255), (0, 383), (128, 512), (256, 512), (384, 512)]


def _allowed(q, k):
    return (k <= q) & ((k >= q - LW + 1) | (k < GT))


def _kvtok(col, s0):
    col = np.asarray(col)
    pos = np.where(col < GT, col, s0 - (HALO - GT) + (col - GT))
    pos = np.where(col >= HALO, s0 + (col - HALO), pos)
    return pos


def _local_masks(g, s0):
    outs = []
    for j in range(6):
        a, b = CHUNK_RANGES[j]
        kpos = _kvtok(g * 512 + 256 + 128 * j + np.arange(128), s0)
        qpos = s0 + g * 512 + np.arange(a, b)
        outs.append(_allowed(qpos[None, :], kpos[:, None]) & (kpos[:, None] >= 0))
    return outs


def _host_masks(s0):
    shared = np.concatenate(_local_masks(1, s0), axis=1).astype(BF)
    if s0 != 0:
        return shared, shared.copy(), np.ones((GT, 512), dtype=BF)
    g0_list = _local_masks(0, s0)
    g0 = np.concatenate(g0_list, axis=1).astype(BF)
    allowed = _allowed(np.arange(512)[None, :], np.arange(GT)[:, None])
    covered = np.zeros_like(allowed)
    for j in range(6):
        a, b = CHUNK_RANGES[j]
        kp = _kvtok(256 + 128 * j + np.arange(128), s0)
        for w in range(128):
            if 0 <= kp[w] < GT:
                covered[kp[w], a:b] |= g0_list[j][w]
    return shared, g0, (allowed & ~covered).astype(BF)


def _host_cossin(s0):
    pos = np.maximum(_kvtok(np.arange(KV), s0), 0).astype(np.float64)
    inv = 1.0 / (10000.0 ** (np.arange(0, HD, 2, dtype=np.float64) / HD))
    fr = pos[None, :] * inv[:, None]
    c = np.cos(fr).astype(np.float32)
    s = np.sin(fr).astype(np.float32)
    return (np.concatenate([c, c, c, c], axis=0),
            np.concatenate([s, -s, s, -s], axis=0))


def _host_tokval(core):
    out = np.zeros((128, 16), dtype=np.float32)
    n = np.arange(128)
    for r in range(16):
        tloc = r * 128 + n
        out[:, r] = 1 + (tloc // TPC) * RPC + core * TPC + (tloc % TPC)
    return out


def stage_core_inputs(core, X, Wq, Wk, Wv, Wo, RW, RB, EW, EV, EWo):
    b, s0 = core // 4, (core % 4) * T
    x = X[b]
    kvx = np.zeros((KV, D), dtype=np.float32)
    kvx[:GT] = x[:GT]
    lo = s0 - (HALO - GT)
    if lo >= 0:
        kvx[GT:HALO] = x[lo:s0]
    kvx[HALO:] = x[s0:s0 + T]
    shared, g0, gl = _host_masks(s0)
    csc, css = _host_cossin(s0)
    ident = np.eye(128, dtype=np.float32)
    rowsel = np.zeros((128, 16), dtype=np.float32)
    rowsel[16 * core + np.arange(16), np.arange(16)] = 1.0
    bdiag = np.zeros((128, 2), dtype=np.float32)
    bdiag[0:64, 0] = 1.0
    bdiag[64:128, 1] = 1.0
    exp2 = np.zeros((2, 128), dtype=np.float32)
    exp2[0, 0:64] = 1.0
    exp2[1, 64:128] = 1.0
    return {
        "xT": np.ascontiguousarray(kvx.T),
        "x_tok": np.ascontiguousarray(x[s0:s0 + T]),
        "wq": np.ascontiguousarray(Wq),
        "wk": np.ascontiguousarray(Wk),
        "wv": np.ascontiguousarray(Wv),
        "wo": np.ascontiguousarray(Wo),
        "woT": np.ascontiguousarray(Wo.T),
        "rwT": np.ascontiguousarray(RW.T),
        "rb": np.ascontiguousarray(RB.reshape(1, NE)),
        "eW": np.ascontiguousarray(EW[core]),
        "eV": np.ascontiguousarray(EV[core]),
        "eWo": np.ascontiguousarray(EWo[core]),
        "csc": csc,
        "css": css,
        "masksh": np.ascontiguousarray(shared),
        "maskg0": np.ascontiguousarray(g0),
        "maskgl": np.ascontiguousarray(gl),
        "identb": ident.astype(BF),
        "identf": ident,
        "lstrict": np.triu(np.ones((128, 128), dtype=np.float32), 1),
        "bdiag": bdiag.astype(BF),
        "expand2": exp2.astype(BF),
        "onesk1": np.ones((1, 128), dtype=BF),
        "onesk1f": np.ones((1, 128), dtype=np.float32),
        "onescol": np.ones((128, 1), dtype=np.float32),
        "tokval": _host_tokval(core),
        "rowsel": rowsel,
        "fconst": np.tile(np.array([[EPS_RMS, EPS_LN, 1.0, 1.0 / HD, 0.125, -1.0,
                                     0.0, 0.0]], dtype=np.float32), (128, 1)),
        "iotae": np.tile(np.arange(NE, dtype=np.float32), (128, 128)),
    }


class Sem:
    __slots__ = ("name", "n", "h")

    def __init__(self, name):
        self.name, self.n, self.h = name, 0, None


class Plan:
    ENGINES = ("pe", "act", "dve", "pool", "sync")

    def __init__(self):
        self.ops = {e: [] for e in self.ENGINES}
        self.sems = []
        self.gid = 0
        self.marks = {}

    def sem(self, name):
        s = Sem(name)
        self.sems.append(s)
        return s

    def emit(self, eng, fn, waits=(), inc=None, inc_k=1):
        waits = [(s, v) for (s, v) in waits if s is not None and v]
        if inc is not None:
            inc.n += inc_k
        self.ops[eng].append((self.gid, fn, waits, inc, inc_k))
        self.gid += 1
        return inc.n if inc is not None else None

    trunc = None

    def replay(self, eng_name, eng, nc=None):
        ctx = (nc.allow_low_precision(reason="intentional bf16 intermediates; "
               "tolerance admits bf16 matmul operands")
               if nc is not None else contextlib.nullcontext())
        with ctx:
            self._replay(eng_name, eng)

    def _replay(self, eng_name, eng):
        for gid, fn, waits, inc, inc_k in self.ops[eng_name]:
            if self.trunc is not None and gid >= self.trunc:
                continue
            for s, v in waits:
                eng.wait_ge(s.h, v)
            ins = fn(eng)
            if inc is not None:
                ins.then_inc(inc.h, inc_k)


_LAST_PLAN = {}


def build_nc(trunc=None):
    nc = bass.Bass("TRN2", target_bir_lowering=False)
    P = Plan()
    dram = {}

    def din(name, shape, dt):
        dram[name] = nc.dram_tensor(name, shape, dt, kind="ExternalInput")

    def dint(name, shape, dt):
        dram[name] = nc.dram_tensor(name, shape, dt)

    din("xT", [D, KV], F32)
    din("x_tok", [T, D], F32)
    din("wq", [D, D], F32)
    din("wk", [D, NKV * HD], F32)
    din("wv", [D, NKV * HD], F32)
    din("wo", [D, D], F32)
    din("woT", [D, D], F32)
    din("rwT", [D, NE], F32)
    din("rb", [1, NE], F32)
    din("eW", [D, DFF], F32)
    din("eV", [D, DFF], F32)
    din("eWo", [DFF, D], F32)
    din("csc", [128, KV], F32)
    din("css", [128, KV], F32)
    din("masksh", [128, 1533], BF16)
    din("maskg0", [128, 1533], BF16)
    din("maskgl", [GT, 512], BF16)
    din("identb", [128, 128], BF16)
    din("identf", [128, 128], F32)
    din("lstrict", [128, 128], F32)
    din("bdiag", [128, 2], BF16)
    din("expand2", [2, 128], BF16)
    din("onesk1", [1, 128], BF16)
    din("onesk1f", [1, 128], F32)
    din("onescol", [128, 1], F32)
    din("tokval", [128, 16], F32)
    din("rowsel", [128, 16], F32)
    din("fconst", [128, 8], F32)
    din("iotae", [128, 1024], F32)
    out_t = nc.dram_tensor("out", [T, D], F32, kind="ExternalOutput")

    dint("lg_loc", [T, NE], F32)
    dint("lg_all", [NCORES * T, NE], F32)
    dint("table_loc", [NCORES * CAP, 2], F32)
    dint("table_my", [CAP, 2], F32)
    dint("xm_my", [T, D], BF16)
    dint("xm_pad", [1 + NCORES * T, D], BF16)
    dint("xmf", [T, D], F32)
    dint("yexp_my", [CAP, D], BF16)
    dint("yexp_pad", [NCORES * CAP + 1, D], BF16)

    s_pe = P.sem("s_pe")
    s_act = P.sem("s_act")
    s_dve = P.sem("s_dve")
    d_c0 = P.sem("d_c0")
    d_zi = P.sem("d_zi")
    d_wq = P.sem("d_wq")
    d_wk = P.sem("d_wk")
    d_wv = P.sem("d_wv")
    d_wo = P.sem("d_wo")
    d_wrb = P.sem("d_wrb")
    d_woT = P.sem("d_woT")
    d_eW = P.sem("d_eW")
    d_eV = P.sem("d_eV")
    d_eWo = P.sem("d_eWo")
    d_x = [P.sem("d_x0"), P.sem("d_x1")]
    d_cs = [P.sem("d_cs0"), P.sem("d_cs1")]
    d_xt = [P.sem("d_xt0"), P.sem("d_xt1")]
    d_xmf = [P.sem("d_xmf0"), P.sem("d_xmf1")]
    d_xmb = [P.sem("d_xmb0"), P.sem("d_xmb1")]
    d_ye = [P.sem("d_ye0"), P.sem("d_ye1")]
    d_st = [P.sem("d_st0"), P.sem("d_st1")]
    d_ix = [P.sem("d_ix0"), P.sem("d_ix1")]
    d_cb = [P.sem(f"d_cb{i}") for i in range(8)]
    d_o8 = [P.sem(f"d_o8_{i}") for i in range(4)]
    d_s8 = [P.sem(f"d_s8_{i}") for i in range(4)]
    d_sc = P.sem("d_sc")
    d_ms = P.sem("d_ms")
    d_mk = P.sem("d_mk")
    d_io = P.sem("d_io")
    d_lg = P.sem("d_lg")
    c_cc = P.sem("c_cc")

    def mm_group(psum_ap, pairs, waits=()):
        n = len(pairs)

        def fn(te):
            last = None
            for i, (lf, rf) in enumerate(pairs):
                last = te.matmul(psum_ap(), lf(), rf(), start=(i == 0),
                                 stop=(i == n - 1))
            return last
        return P.emit("pe", fn, waits=waits, inc=s_pe)

    def gp_dma(dst, src, sem, waits=()):
        return P.emit("pool", lambda g, dst=dst, src=src: g.dma_start(
            out=dst(), in_=src()), waits=waits, inc=sem, inc_k=16)

    def sy_dma(dst, src, sem, waits=()):
        return P.emit("sync", lambda g, dst=dst, src=src: g.dma_start(
            out=dst(), in_=src()), waits=waits, inc=sem, inc_k=16)

    def r3(name, n):
        return dram[name].ap().rearrange("(c p) n -> p c n", p=128)

    REGS = {}

    def _setup_regs(g):
        r1 = g.alloc_register()
        g.reg_mov(r1, NCORES * CAP - 1)
        r2 = g.alloc_register()
        ins = g.reg_mov(r2, NCORES * T)
        REGS["b_scat"] = r1
        REGS["b_gath"] = r2
        return ins

    es0 = contextlib.ExitStack()
    with es0:
        sbufs = {}

        def sbuf(stk, name, shape, dt):
            t = stk.enter_context(nc.sbuf_tensor(name, shape, dt))
            sbufs[name] = t
            return t

        identb_sb = sbuf(es0, "identb_sb", [128, 128], BF16)
        identf_sb = sbuf(es0, "identf_sb", [128, 128], F32)
        lstrict_sb = sbuf(es0, "lstrict_sb", [128, 128], F32)
        zero_sb = sbuf(es0, "zero_sb", [128, 256], F32)
        zrow_sb = sbuf(es0, "zrow_sb", [128, 128], F32)
        zerobf_sb = sbuf(es0, "zerobf_sb", [1, D], BF16)
        onesk1_sb = sbuf(es0, "onesk1_sb", [1, 128], BF16)
        onesk1f_sb = sbuf(es0, "onesk1f_sb", [1, 128], F32)
        onescol_sb = sbuf(es0, "onescol_sb", [128, 1], F32)
        bdiag_sb = sbuf(es0, "bdiag_sb", [128, 2], BF16)
        expand2_sb = sbuf(es0, "expand2_sb", [2, 128], BF16)
        rwT_sb = sbuf(es0, "rwT_sb", [128, 8 * NE], F32)
        rwTb_sb = sbuf(es0, "rwTb_sb", [128, 8 * NE], BF16)
        rb_sb = sbuf(es0, "rb_sb", [1, NE], F32)
        tokval_sb = sbuf(es0, "tokval_sb", [128, 16], F32)
        rowsel_sb = sbuf(es0, "rowsel_sb", [128, 16], F32)
        w2_sb = sbuf(es0, "w2_sb", [128, 8 * NE], BF16)
        sbc_sb = sbuf(es0, "sbc_sb", [128, NE], F32)
        rbb_sb = sbuf(es0, "rbb_sb", [128, NE], F32)
        st_sb = [sbuf(es0, f"st_sb{i}", [128, 8], F32) for i in range(2)]
        fconst_sb = sbuf(es0, "fconst_sb", [128, 8], F32)
        lgl_sb = sbuf(es0, "lgl_sb", [128, NT * NE], F32)
        xrw_sb = sbuf(es0, "xrw_sb", [8, T], F32)
        lgs_sb = sbuf(es0, "lgs_sb", [128, NE], F32)
        xrt_sb = [sbuf(es0, f"xrt_sb{i}", [128, NE], F32) for i in range(2)]
        fT_sb = [sbuf(es0, f"fT_sb{i}", [128, 16], F32) for i in range(6)]
        fTi_sb = [sbuf(es0, f"fTi_sb{i}", [128, 16], I32) for i in range(4)]
        val_sb = [sbuf(es0, f"val_sb{i}", [128, 32], F32) for i in range(2)]
        tab_sb = sbuf(es0, "tab_sb", [128, 32], F32)
        gidx_sb = sbuf(es0, "gidx_sb", [128, 16], I32)
        gate_sb = sbuf(es0, "gate_sb", [128, 16], F32)
        sqln_sb = [sbuf(es0, f"sqln_sb{i}", [128, D], BF16) for i in range(2)]
        r_sb = [sbuf(es0, f"r_sb{i}", [128, D], F32) for i in range(2)]
        xmf_sb = [sbuf(es0, f"xmf_sb{i}", [128, D], F32) for i in range(2)]
        xmb_sb = [sbuf(es0, f"xmb_sb{i}", [128, D], BF16) for i in range(2)]
        yc_sb = [sbuf(es0, f"yc_sb{i}", [128, D], BF16) for i in range(4)]
        ye_sb = [sbuf(es0, f"ye_sb{i}", [128, D], BF16) for i in range(2)]
        silu_sb = [sbuf(es0, f"silu_sb{i}", [128, 512], F32) for i in range(2)]
        xe_sb = [sbuf(es0, f"xe_sb{i}", [128, D], BF16) for i in range(2)]

        pb = [es0.enter_context(nc.psum_tensor(f"pb{i}", [128, 512], F32))
              for i in range(7)]
        pbt = es0.enter_context(nc.psum_tensor("pbt", [128, 128], BF16))

        P.emit("pool", _setup_regs)
        sy_dma(lambda: identb_sb[:], lambda: dram["identb"].ap(), d_c0)
        sy_dma(lambda: identf_sb[:], lambda: dram["identf"].ap(), d_c0)
        sy_dma(lambda: lstrict_sb[:], lambda: dram["lstrict"].ap(), d_c0)
        sy_dma(lambda: onesk1_sb[:], lambda: dram["onesk1"].ap(), d_c0)
        sy_dma(lambda: onesk1f_sb[:], lambda: dram["onesk1f"].ap(), d_c0)
        sy_dma(lambda: onescol_sb[:], lambda: dram["onescol"].ap(), d_c0)
        sy_dma(lambda: bdiag_sb[:], lambda: dram["bdiag"].ap(), d_c0)
        sy_dma(lambda: expand2_sb[:], lambda: dram["expand2"].ap(), d_c0)
        sy_dma(lambda: rwT_sb[:].rearrange("p (c n) -> p c n", n=NE), lambda: r3("rwT", NE), d_c0)
        sy_dma(lambda: rb_sb[:], lambda: dram["rb"].ap(), d_c0)
        sy_dma(lambda: tokval_sb[:], lambda: dram["tokval"].ap(), d_c0)
        sy_dma(lambda: rowsel_sb[:], lambda: dram["rowsel"].ap(), d_c0)
        sy_dma(lambda: fconst_sb[:], lambda: dram["fconst"].ap(), d_c0)
        for _i, _v in enumerate([EPS_RMS, EPS_LN, 1.0, 1.0 / HD, 0.125, -1.0]):
            nc.const_aps.register(F32, float(_v), fconst_sb[:, _i:_i + 1])
        LD0 = d_c0.n
        ld_rwb = gp_dma(lambda: rwTb_sb[:].rearrange("p (c n) -> p c n", n=NE), lambda: r3("rwT", NE), d_wrb)

        n_zero = P.emit("dve", lambda v: v.memset(zero_sb[:], 0.0), inc=s_dve)
        n_zrow = P.emit("dve", lambda v: v.memset(zrow_sb[:], 0.0),
                        waits=[(s_dve, n_zero)], inc=s_dve)
        n_zbf = P.emit("dve", lambda v: v.memset(zerobf_sb[:], 0.0),
                       waits=[(s_dve, n_zrow)], inc=s_dve)
        sy_dma(lambda: dram["xm_pad"].ap()[0:1, :], lambda: zerobf_sb[:], d_zi,
               waits=[(s_dve, n_zbf)])
        sy_dma(lambda: dram["yexp_pad"].ap()[PADROW:PADROW + 1, :],
               lambda: zerobf_sb[:], d_zi, waits=[(s_dve, n_zbf)])
        sy_dma(lambda: dram["table_loc"].ap().rearrange("(a b) c -> a (b c)", a=128),
               lambda: zero_sb[:], d_zi, waits=[(s_dve, n_zero)])
        ZINIT = d_zi.n

        mm_group(lambda: pb[3][0:NE, 0:1],
                 [(lambda fc=fc: rwT_sb[:, fc * NE:(fc + 1) * NE],
                   lambda: onescol_sb[:]) for fc in range(8)],
                 waits=[(d_c0, LD0)])
        n_S = s_pe.n
        P.emit("act", lambda a: a.copy(out=st_sb[0][0:NE, 0:1],
                                       in_=pb[3][0:NE, 0:1]),
               waits=[(s_pe, n_S)], inc=s_act)
        n_Se = s_act.n
        P.emit("pe", lambda te: te.transpose(pb[3][0:1, 0:NE], st_sb[0][0:NE, 0:1],
                                             identf_sb[0:NE, 0:NE]),
               waits=[(s_act, n_Se)], inc=s_pe)
        n_St = s_pe.n
        P.emit("act", lambda a: a.copy(out=st_sb[1][0:1, 0:NE], in_=pb[3][0:1, 0:NE]),
               waits=[(s_pe, n_St)], inc=s_act)
        n_Str = s_act.n
        P.emit("pe", lambda te: te.matmul(pb[3][:, 0:NE], onesk1f_sb[0:1, :],
                                          st_sb[1][0:1, 0:NE], start=True, stop=True),
               waits=[(s_act, n_Str)], inc=s_pe)
        n_Sb = s_pe.n
        P.emit("act", lambda a: a.copy(out=sbc_sb[:], in_=pb[3][:, 0:NE]),
               waits=[(s_pe, n_Sb)], inc=s_act)
        n_Sbc = s_act.n
        P.emit("pe", lambda te: te.matmul(pb[3][:, 0:NE], onesk1f_sb[0:1, :],
                                          rb_sb[0:1, :], start=True, stop=True),
               waits=[(s_act, n_Sbc)], inc=s_pe)
        n_rb = s_pe.n
        P.emit("act", lambda a: a.copy(out=rbb_sb[:], in_=pb[3][:, 0:NE]),
               waits=[(s_pe, n_rb)], inc=s_act)
        n_rbb = s_act.n
        SETUP_PB3 = (s_act, n_rbb)

        es_a = contextlib.ExitStack()
        qhat = [sbuf(es_a, f"qhat{i}", [128, 1024], BF16) for i in range(8)]
        kTA = [sbuf(es_a, f"kTA{i}", [128, 1536], BF16) for i in range(2)]
        kTB = [sbuf(es_a, f"kTB{i}", [128, 1536], BF16) for i in range(2)]
        kTGA = sbuf(es_a, "kTGA", [128, 128], BF16)
        kTGB = sbuf(es_a, "kTGB", [128, 128], BF16)
        vst = sbuf(es_a, "vst", [128, 8 * 260], BF16)
        vstG = sbuf(es_a, "vstG", [64, 260], BF16)
        yT = [sbuf(es_a, f"yT{i}", [128, 1024], BF16) for i in range(8)]
        wo_sb = sbuf(es_a, "wo_sb", [128, 8 * D], BF16)
        masksh_sb = sbuf(es_a, "masksh_sb", [128, 1533], BF16)
        maskg0_sb = sbuf(es_a, "maskg0_sb", [128, 1533], BF16)
        maskgl_sb = sbuf(es_a, "maskgl_sb", [GT, 512], BF16)
        eT_sb = [sbuf(es_a, f"eT_sb{i}", [128, 512], BF16) for i in range(4)]
        rec_sb = [sbuf(es_a, f"rec_sb{i}", [1, 512], BF16) for i in range(2)]
        yv_sb = [sbuf(es_a, f"yv_sb{i}", [65, 512], F32) for i in range(2)]
        xTg = [sbuf(es_a, f"xTg{i}", [128, 8 * 512], BF16) for i in range(2)]
        wq_sb = sbuf(es_a, "wq_sb", [128, 8 * D], BF16)
        wk_sb = sbuf(es_a, "wk_sb", [128, 8 * 256], BF16)
        wv_sb = sbuf(es_a, "wv_sb", [128, 8 * 256], BF16)
        cscg = [sbuf(es_a, f"cscg{i}", [128, 512], F32) for i in range(2)]
        cssg = [sbuf(es_a, f"cssg{i}", [128, 512], F32) for i in range(2)]
        sq_sb = [sbuf(es_a, f"sq_sb{i}", [128, 512], BF16) for i in range(2)]
        rinv_sb = [sbuf(es_a, f"rinv_sb{i}", [2, 512], BF16) for i in range(2)]
        hat_sb = [sbuf(es_a, f"hat_sb{i}", [128, 512], F32) for i in range(2)]
        tmp_sb = [sbuf(es_a, f"tmp_sb{i}", [128, 512], F32) for i in range(2)]
        rot_sb = [sbuf(es_a, f"rot_sb{i}", [128, 512], F32) for i in range(2)]

        ld_wq = gp_dma(lambda: wq_sb[:].rearrange("p (c n) -> p c n", n=D), lambda: r3("wq", D), d_wq)
        ld_wk = gp_dma(lambda: wk_sb[:].rearrange("p (c n) -> p c n", n=256), lambda: r3("wk", 256), d_wk)
        ld_wv = gp_dma(lambda: wv_sb[:].rearrange("p (c n) -> p c n", n=256), lambda: r3("wv", 256), d_wv)
        ld_wo = gp_dma(lambda: wo_sb[:].rearrange("p (c n) -> p c n", n=D), lambda: r3("wo", D), d_wo)
        ld_msh = sy_dma(lambda: masksh_sb[:], lambda: dram["masksh"].ap(), d_mk)
        ld_mg0 = sy_dma(lambda: maskg0_sb[:], lambda: dram["maskg0"].ap(), d_mk)
        ld_mgl = sy_dma(lambda: maskgl_sb[:], lambda: dram["maskgl"].ap(), d_mk)
        ld_woT = []
        for oc in range(8):
            ld_woT.append(gp_dma(
                lambda oc=oc: yT[oc][:, 0:1024],
                lambda oc=oc: dram["woT"].ap()[oc * 128:(oc + 1) * 128, :],
                d_woT))
        w = [(d_woT, ld_woT[-1]), (d_wrb, ld_rwb), SETUP_PB3]
        for ic in range(8):
            mm_group(lambda ic=ic: pb[3][:, ic * NE:(ic + 1) * NE],
                     [(lambda oc=oc, ic=ic: yT[oc][:, ic * 128:(ic + 1) * 128],
                       lambda oc=oc: rwTb_sb[:, oc * NE:(oc + 1) * NE])
                      for oc in range(8)], waits=w if ic == 0 else [])
        n_w2mm = s_pe.n
        P.emit("act", lambda a: a.copy(out=w2_sb[:], in_=pb[3][:, 0:64]),
               waits=[(s_pe, n_w2mm)], inc=s_act)
        n_w2e = s_act.n

        n_vone = P.emit("dve", lambda v: v.memset(
            vst[:].rearrange("p (t g c) -> p t g c", t=8, g=4)[:, :, :, 64:65], 1.0),
            waits=[(s_dve, n_zbf)], inc=s_dve)
        n_voneg = P.emit("dve", lambda v: v.memset(
            vstG[:].rearrange("p (g c) -> p g c", g=4)[:, :, 64:65], 1.0),
            waits=[(s_dve, n_vone)], inc=s_dve)

        pp_free = [None, None]
        p4_free = None
        unit = [0]
        KDONE, QDONE = {}, {}
        vdone = {}
        ATTG, P4G = {}, {}
        XRWDONE = None
        xg_users = [None, None]
        cs_users = [None, None]
        rot_user = [None, None]
        pb01_owner = [None, None]
        vdefer = [None]
        VUSED = {}
        FINQ = []
        REC_USER = [None, None]
        YV_USER = [None, None]
        LG_CHUNK = []

        def proj_unit_gen(kind, gg, od, xbuf, DONE, key):
            nonlocal p4_free
            buf = unit[0] % 2
            unit[0] += 1
            if kind == "q":
                wsb, wstride = wq_sb, D
                dst, dcol, ldw = qhat[od], (gg % 2) * 512, (d_wq, ld_wq)
                warw = ATTG.get(gg - 2)
            else:
                wsb, wstride = wk_sb, 256
                dst, dcol, ldw = kTA[od], (gg % 3) * 512, (d_wk, ld_wk)
                warw = ATTG.get(gg - 3)
            while pb01_owner[buf] is not None:
                yield
            pb01_owner[buf] = 1
            w = [ldw, (d_x[xbuf[0]], xbuf[1])]
            if pp_free[buf]:
                w.append(pp_free[buf])
            mm_group(lambda buf=buf: pb[buf][:],
                     [(lambda fc=fc: wsb[:, fc * wstride + od * 128: fc * wstride + od * 128 + 128],
                       lambda fc=fc: xTg[xbuf[0]][:, fc * 512:(fc + 1) * 512])
                      for fc in range(8)], waits=w)
            n_mm = s_pe.n
            yield
            P.emit("act", lambda a, buf=buf: a.activation(
                out=sq_sb[buf][:], in_=pb[buf][:], func=AF.Square),
                waits=[(s_pe, n_mm)], inc=s_act)
            n_sq = s_act.n
            yield
            w = [(s_act, n_sq), (d_c0, LD0)]
            if p4_free:
                w.append(p4_free)
            mm_group(lambda: pb[4][0:2, :],
                     [(lambda: bdiag_sb[:], lambda buf=buf: sq_sb[buf][:])], waits=w)
            n_ssm = s_pe.n
            w = [(s_pe, n_ssm)]
            if rot_user[buf]:
                w.append(rot_user[buf])
            P.emit("act", lambda a, buf=buf: a.activation(
                out=rot_sb[buf][0:2, :], in_=pb[4][0:2, :], func=AF.Ln,
                scale=1.0 / HD, bias=EPS_RMS), waits=w, inc=s_act)
            n_rsq = s_act.n
            p4_free = (s_act, n_rsq)
            P.emit("act", lambda a, buf=buf: a.activation(
                out=rinv_sb[buf][:], in_=rot_sb[buf][0:2, :], func=AF.Exp, scale=-0.5),
                waits=[(s_act, n_rsq)], inc=s_act)
            n_ri = (s_act, s_act.n)
            yield
            w = [n_ri]
            if p4_free:
                w.append(p4_free)
            mm_group(lambda: pb[4][:],
                     [(lambda: expand2_sb[:], lambda buf=buf: rinv_sb[buf][:])],
                     waits=w)
            n_bcm = s_pe.n
            P.emit("act", lambda a, buf=buf: a.copy(
                out=sq_sb[buf][:], in_=pb[4][:]), waits=[(s_pe, n_bcm)], inc=s_act)
            n_bce = s_act.n
            p4_free = (s_act, n_bce)
            yield
            P.emit("dve", lambda v, buf=buf: v.tensor_mul(
                out=hat_sb[buf][:], in0=pb[buf][:], in1=sq_sb[buf][:]),
                waits=[(s_act, n_bce), (s_dve, s_dve.n)], inc=s_dve)
            n_hat = s_dve.n
            pp_free[buf] = (s_dve, n_hat)
            pb01_owner[buf] = None
            yield

            def swaps(v, buf=buf):
                last = None
                for h0 in (0, 64):
                    last = v.tensor_copy(out=tmp_sb[buf][h0:h0 + 32, :],
                                         in_=hat_sb[buf][h0 + 32:h0 + 64, :])
                    last = v.tensor_copy(out=tmp_sb[buf][h0 + 32:h0 + 64, :],
                                         in_=hat_sb[buf][h0:h0 + 32, :])
                return last
            P.emit("dve", swaps, waits=[(s_dve, n_hat)], inc=s_dve)
            n_sw = s_dve.n
            yield
            P.emit("dve", lambda v, buf=buf, cb=xbuf[0]: v.tensor_mul(
                out=rot_sb[buf][:], in0=hat_sb[buf][:], in1=cscg[cb][:]),
                waits=[(s_dve, n_sw), (d_cs[xbuf[0]], xbuf[2]), n_ri], inc=s_dve)
            n_r1 = s_dve.n
            yield
            P.emit("dve", lambda v, buf=buf, cb=xbuf[0]: v.tensor_mul(
                out=tmp_sb[buf][:], in0=tmp_sb[buf][:], in1=cssg[cb][:]),
                waits=[(s_dve, n_r1)], inc=s_dve)
            n_r2 = s_dve.n
            yield
            w = [(s_dve, n_r2)]
            if warw:
                w.append((s_dve, warw))
            P.emit("dve", lambda v, buf=buf, dst=dst, dcol=dcol: v.tensor_add(
                out=dst[:, dcol:dcol + 512], in0=rot_sb[buf][:], in1=tmp_sb[buf][:]),
                waits=w, inc=s_dve)
            rot_user[buf] = (s_dve, s_dve.n)
            if kind == "k":
                P.emit("dve", lambda v, od=od, dcol=dcol: v.tensor_copy(
                    out=kTB[od][0:64, dcol:dcol + 512],
                    in_=kTA[od][64:128, dcol:dcol + 512]),
                    waits=[(s_dve, s_dve.n)], inc=s_dve)
                P.emit("dve", lambda v, od=od, dcol=dcol: v.tensor_copy(
                    out=kTB[od][64:128, dcol:dcol + 512],
                    in_=kTA[od][0:64, dcol:dcol + 512]),
                    waits=[(s_dve, s_dve.n)], inc=s_dve)
            DONE[key] = s_dve.n

        def vproj_gen(t, xbuf):
            buf = unit[0] % 2
            unit[0] += 1
            while pb01_owner[buf] is not None:
                yield
            pb01_owner[buf] = 1
            w = [(d_wv, ld_wv), (d_x[xbuf[0]], xbuf[1]), (s_dve, n_voneg)]
            if pp_free[buf]:
                w.append(pp_free[buf])
            if t == 8:
                w.append((s_dve, GLOBC[0]))
            if t >= 10:
                nrd = 8 * len([1 for j in range(6)
                               if (t - 10 - j) % 4 == 0 and 0 <= (t - 10 - j) // 4 <= 3])
                while VUSED.get(t - 8, [0, None])[0] < nrd:
                    yield
                w.append(VUSED[t - 8][1])
            mm_group(lambda buf=buf: pb[buf][:, 0:256],
                     [(lambda fc=fc, t=t: xTg[xbuf[0]][:, fc * 512 + (t % 4) * 128: fc * 512 + (t % 4) * 128 + 128],
                       lambda fc=fc: wv_sb[:, fc * 256:(fc + 1) * 256]) for fc in range(8)],
                     waits=w)
            n_mm = s_pe.n
            yield
            P.emit("act", lambda a, buf=buf, t=t: a.copy(
                out=vst[:].rearrange("p (t g c) -> p t g c", t=8, g=4)[:, t % 8, :, 0:64],
                in_=pb[buf][:, 0:256]), waits=[(s_pe, n_mm)], inc=s_act)
            vdone[t] = (s_act, s_act.n)
            pp_free[buf] = (s_act, s_act.n)
            pb01_owner[buf] = None

        def xrw_gen(qg, gb, ldx):
            nonlocal p4_free, XRWDONE
            w = [(d_wrb, ld_rwb), (d_x[gb], ldx)]
            if XRWDONE:
                w.append(XRWDONE)
            if p4_free:
                w.append(p4_free)
            mm_group(lambda: pb[4][0:NE, :],
                     [(lambda fc=fc: rwTb_sb[:, fc * NE:(fc + 1) * NE],
                       lambda fc=fc, gb=gb: xTg[gb][:, fc * 512:(fc + 1) * 512])
                      for fc in range(8)], waits=w)
            n_x = s_pe.n
            P.emit("act", lambda a, qg=qg: a.copy(
                out=xrw_sb[:, qg * 512:(qg + 1) * 512], in_=pb[4][0:NE, :]),
                waits=[(s_pe, n_x)], inc=s_act)
            XRWDONE = (s_act, s_act.n)
            p4_free = (s_act, s_act.n)
            yield

        def glob_copy_gen():
            for od in range(2):
                P.emit("dve", lambda v, od=od: v.tensor_copy(
                    out=kTGA[:, od * 64:(od + 1) * 64], in_=kTA[od][:, 0:64]),
                    waits=[(s_dve, KDONE[(0, od)]), (s_dve, s_dve.n)], inc=s_dve)
                P.emit("dve", lambda v, od=od: v.tensor_copy(
                    out=kTGB[:, od * 64:(od + 1) * 64], in_=kTB[od][:, 0:64]),
                    waits=[(s_dve, s_dve.n)], inc=s_dve)
                yield
            P.emit("dve", lambda v: v.tensor_copy(out=vstG[:], in_=vst[0:64, 0:260]),
                   waits=[vdone[0], (s_dve, s_dve.n), (s_dve, n_voneg)], inc=s_dve)
            GLOBC[0] = s_dve.n

        GLOBC = [None]

        def drive2(att_q, fill_q, p4_q):
            cur_att, cur_p4 = None, None
            act_f = []
            while att_q or cur_att or fill_q or act_f or p4_q or cur_p4:
                if cur_att is None and att_q:
                    cur_att = att_q.pop(0)
                if cur_p4 is None and p4_q:
                    cur_p4 = p4_q.pop(0)
                while fill_q and len(act_f) < 2:
                    act_f.append(fill_q.pop(0))
                for slot, gx in (("a", cur_att), ("p", cur_p4), (0, None)):
                    pass
                for gx in [g_ for g_ in (cur_att, cur_p4) if g_] + list(act_f):
                    try:
                        next(gx)
                    except StopIteration:
                        if gx is cur_att:
                            cur_att = None
                        elif gx is cur_p4:
                            cur_p4 = None
                        else:
                            act_f.remove(gx)

        MOFF = np.cumsum([0] + [b - a for (a, b) in CHUNK_RANGES]).tolist()
        SBANK = [[5, 5], [6, 6]]
        ps_s_free = {5: None, 6: None}
        ps_y_free = {2: None, 3: (s_act, n_w2e)}
        eT_free = [None] * 4
        ATT_DONE = {}

        r_free = [None, None]
        xmf_free = [None, None]
        xmb_free = [None, None]
        LGDONE = []

        def emit_p4_tile(tt):
            nonlocal p4_free
            buf = tt % 2
            g = tt // 4
            attw = [(s_dve, ATT_DONE[(g, hp)]) for hp in range(8)]
            w = [r_free[buf]] if r_free[buf] else []
            P.emit("sync", lambda s, tt=tt, buf=buf: s.dma_start(
                out=r_sb[buf][:], in_=dram["x_tok"].ap()[tt * 128:(tt + 1) * 128, :]),
                waits=w, inc=d_xt[buf], inc_k=16)
            n_xl = (d_xt[buf], d_xt[buf].n)
            n_r = [None, None]
            for nh2 in range(2):
                while pb01_owner[nh2] is not None:
                    yield
                pb01_owner[nh2] = 1
                w = [(d_wo, ld_wo)] + attw
                if pp_free[nh2]:
                    w.append(pp_free[nh2])
                ytc = (g % 2) * 512 + (tt % 4) * 128
                mm_group(lambda nh2=nh2: pb[nh2][:],
                         [(lambda ic=ic, ytc=ytc: yT[ic][:, ytc:ytc + 128],
                           lambda ic=ic, nh2=nh2: wo_sb[:, ic * D + nh2 * 512: ic * D + nh2 * 512 + 512])
                          for ic in range(8)], waits=w)
                n_oh = s_pe.n
                w = [(s_pe, n_oh), n_xl, (s_dve, s_dve.n)]
                P.emit("dve", lambda v, nh2=nh2, buf=buf: v.tensor_add(
                    out=r_sb[buf][:, nh2 * 512:(nh2 + 1) * 512], in0=pb[nh2][:],
                    in1=r_sb[buf][:, nh2 * 512:(nh2 + 1) * 512]),
                    waits=w, inc=s_dve)
                n_r[nh2] = s_dve.n
                pp_free[nh2] = (s_dve, s_dve.n)
                pb01_owner[nh2] = None
                yield
            n_r = [n_r[0], n_r[1]]
            w = [(s_act, n_w2e)] + attw
            if p4_free:
                w.append(p4_free)
            ytc = (g % 2) * 512 + (tt % 4) * 128
            mm_group(lambda: pb[4][:, 0:NE],
                     [(lambda ic=ic, ytc=ytc: yT[ic][:, ytc:ytc + 128],
                       lambda ic=ic: w2_sb[:, ic * NE:(ic + 1) * NE]) for ic in range(8)],
                     waits=w)
            n_yw2 = s_pe.n
            P.emit("dve", lambda v: v.tensor_copy(out=lgs_sb[:], in_=pb[4][:, 0:NE]),
                   waits=[(s_pe, n_yw2), (s_dve, s_dve.n)], inc=s_dve)
            n_yw2c = s_dve.n
            p4_free = (s_dve, n_yw2c)
            yield
            w = [XRWDONE, (s_dve, n_yw2c), (d_c0, LD0)]
            if p4_free:
                w.append(p4_free)
            P.emit("pe", lambda te, tt=tt: te.transpose(
                pb[4][:, 0:NE], xrw_sb[:, tt * 128:(tt + 1) * 128],
                identf_sb[0:NE, 0:NE]), waits=w, inc=s_pe)
            n_xt = s_pe.n
            P.emit("dve", lambda v, buf=buf: v.tensor_copy(
                out=xrt_sb[buf][:], in_=pb[4][:, 0:NE]),
                waits=[(s_pe, n_xt), (s_dve, s_dve.n)], inc=s_dve)
            p4_free = (s_dve, s_dve.n)
            yield
            P.emit("dve", lambda v, buf=buf: v.reduce_sum(
                out=st_sb[buf][:, 0:1], in_=r_sb[buf][:], axis=AX.X),
                waits=[(s_dve, n_r[-1])], inc=s_dve)
            n_s1 = s_dve.n
            P.emit("act", lambda a, buf=buf: a.activation(
                out=sqln_sb[buf][:], in_=r_sb[buf][:], func=AF.Square),
                waits=[(s_dve, n_r[-1]), (s_act, s_act.n)], inc=s_act)
            n_sq = s_act.n
            yield
            P.emit("dve", lambda v, buf=buf: v.reduce_sum(
                out=st_sb[buf][:, 1:2], in_=sqln_sb[buf][:], axis=AX.X),
                waits=[(s_act, n_sq), (s_dve, n_s1)], inc=s_dve)
            P.emit("dve", lambda v, buf=buf: v.tensor_scalar(
                out=st_sb[buf][:, 2:3], in0=st_sb[buf][:, 0:1], scalar1=1.0 / D,
                scalar2=None, op0=ALU.mult), waits=[(s_dve, s_dve.n)], inc=s_dve)
            P.emit("dve", lambda v, buf=buf: v.tensor_mul(
                out=st_sb[buf][:, 3:4], in0=st_sb[buf][:, 2:3], in1=st_sb[buf][:, 2:3]),
                waits=[(s_dve, s_dve.n)], inc=s_dve)
            P.emit("dve", lambda v, buf=buf: v.tensor_scalar(
                out=st_sb[buf][:, 4:5], in0=st_sb[buf][:, 1:2], scalar1=1.0 / D,
                scalar2=None, op0=ALU.mult), waits=[(s_dve, s_dve.n)], inc=s_dve)
            P.emit("dve", lambda v, buf=buf: v.tensor_sub(
                out=st_sb[buf][:, 4:5], in0=st_sb[buf][:, 4:5], in1=st_sb[buf][:, 3:4]),
                waits=[(s_dve, s_dve.n)], inc=s_dve)
            n_var = s_dve.n
            yield
            P.emit("act", lambda a, buf=buf: a.activation(
                out=st_sb[buf][:, 5:6], in_=st_sb[buf][:, 4:5], func=AF.Ln,
                bias=EPS_LN), waits=[(s_dve, n_var)], inc=s_act)
            P.emit("act", lambda a, buf=buf: a.activation(
                out=st_sb[buf][:, 6:7], in_=st_sb[buf][:, 5:6], func=AF.Exp,
                scale=-0.5), waits=[(s_act, s_act.n)], inc=s_act)
            n_rs = s_act.n
            yield
            w = [(s_act, n_rs), (s_dve, s_dve.n)]
            if xmf_free[buf]:
                w.append(xmf_free[buf])
            P.emit("dve", lambda v, buf=buf: v.tensor_scalar(
                out=xmf_sb[buf][:], in0=r_sb[buf][:], scalar1=st_sb[buf][:, 2:3],
                scalar2=st_sb[buf][:, 6:7], op0=ALU.subtract, op1=ALU.mult),
                waits=w, inc=s_dve)
            n_xm = s_dve.n
            r_free[buf] = (s_dve, n_xm)
            w = [(s_dve, n_xm)]
            if xmb_free[buf]:
                w.append(xmb_free[buf])
            P.emit("act", lambda a, buf=buf: a.copy(out=xmb_sb[buf][:],
                                                    in_=xmf_sb[buf][:]),
                   waits=w, inc=s_act)
            n_xb = s_act.n
            P.emit("sync", lambda s, tt=tt, buf=buf: s.dma_start(
                out=dram["xmf"].ap()[tt * 128:(tt + 1) * 128, :], in_=xmf_sb[buf][:]),
                waits=[(s_dve, n_xm)], inc=d_xmf[buf], inc_k=16)
            xmf_free[buf] = (d_xmf[buf], d_xmf[buf].n)
            P.emit("sync", lambda s, tt=tt, buf=buf: s.dma_start(
                out=dram["xm_my"].ap()[tt * 128:(tt + 1) * 128, :], in_=xmb_sb[buf][:]),
                waits=[(s_act, n_xb)], inc=d_xmb[buf], inc_k=16)
            n_xmst = [(d_xmb[0], d_xmb[0].n), (d_xmb[1], d_xmb[1].n)]
            xmb_free[buf] = (d_xmb[buf], d_xmb[buf].n)
            yield
            P.emit("dve", lambda v, buf=buf: v.tensor_add(
                out=lgs_sb[:], in0=lgs_sb[:], in1=xrt_sb[buf][:]),
                waits=[(s_dve, s_dve.n)], inc=s_dve)
            P.emit("dve", lambda v, buf=buf: v.tensor_scalar(
                out=sqln_sb[buf][:, 0:NE], in0=sbc_sb[:],
                scalar1=st_sb[buf][:, 2:3], scalar2=None, op0=ALU.mult),
                waits=[(s_act, n_Sbc), (s_dve, s_dve.n)], inc=s_dve)
            P.emit("dve", lambda v, buf=buf: v.tensor_sub(
                out=lgs_sb[:], in0=lgs_sb[:], in1=sqln_sb[buf][:, 0:NE]),
                waits=[(s_dve, s_dve.n)], inc=s_dve)
            P.emit("dve", lambda v, buf=buf: v.tensor_scalar(
                out=lgs_sb[:], in0=lgs_sb[:], scalar1=st_sb[buf][:, 6:7],
                scalar2=None, op0=ALU.mult), waits=[(s_dve, s_dve.n)], inc=s_dve)
            P.emit("dve", lambda v, tt=tt: v.tensor_add(
                out=lgl_sb[:, tt * NE:(tt + 1) * NE], in0=lgs_sb[:], in1=rbb_sb[:]),
                waits=[(s_dve, s_dve.n), (s_act, n_rbb)], inc=s_dve)
            LGDONE.append(s_dve.n)
            if tt % 2 == 1:
                k2 = tt // 2
                P.emit("sync", lambda s, k2=k2: s.dma_start(
                    out=dram["lg_loc"].ap().rearrange(
                        "(tt p) e -> p tt e", p=128)[:, 2 * k2:2 * k2 + 2, :],
                    in_=lgl_sb[:].rearrange(
                        "p (tt e) -> p tt e", e=NE)[:, 2 * k2:2 * k2 + 2, :]),
                    waits=[(s_dve, LGDONE[-1]), (d_ms, d_ms.n)], inc=d_ms,
                    inc_k=16)
                P.emit("pool", lambda g_, k2=k2: g_.collective_compute(
                    "AllGather", ALU.bypass, replica_groups=[list(range(NCORES))],
                    ins=[dram["lg_loc"].ap()[k2 * 256:(k2 + 1) * 256, :]],
                    outs=[dram["lg_all"].ap()[k2 * 2048:(k2 + 1) * 2048, :]]),
                    waits=[(d_ms, d_ms.n)], inc=c_cc)
                LG_CHUNK.append(c_cc.n)
            if tt % (NT // AGC) == (NT // AGC) - 1:
                k = tt // (NT // AGC)
                P.emit("pool", lambda g_, k=k: g_.collective_compute(
                    "AllGather", ALU.bypass, replica_groups=[list(range(NCORES))],
                    ins=[dram["xm_my"].ap()[k * TPC:(k + 1) * TPC, :]],
                    outs=[dram["xm_pad"].ap()[1 + k * RPC: 1 + (k + 1) * RPC, :]]),
                    waits=n_xmst, inc=c_cc)

        def att_unit_gen(g, hp):
            nonlocal p4_free
            gv = hp // 2
            jj, sel = gv // 2, gv % 2
            kt0 = kTA[jj] if sel == 0 else kTB[jj]
            kt1 = kTB[jj] if sel == 0 else kTA[jj]
            ktg0 = kTGA if sel == 0 else kTGB
            ktg1 = kTGB if sel == 0 else kTGA
            qcol = (g % 2) * 512
            chunks = [("g", 0, 512, GT, 0)] + [
                (j, CHUNK_RANGES[j][0], CHUNK_RANGES[j][1], 128, 4 * g + 2 + j)
                for j in range(6)]
            av_last = [None, None]
            for ci, (j, a0_, b0, kw, vt) in enumerate(chunks):
                kc = 0 if j == "g" else g * 512 + 256 + 128 * j
                kg = kc // 512
                banks = [SBANK[0][ci % 2], SBANK[1][ci % 2]]
                w = [(s_dve, QDONE[(g, hp)])]
                if j == "g":
                    w.append((s_dve, GLOBC[0]))
                else:
                    w.append((s_dve, KDONE[(kg, jj)]))
                for bk in set(banks):
                    if ps_s_free[bk]:
                        w.append(ps_s_free[bk])

                if j == "g":
                    l0 = lambda jj=jj, ktg0=ktg0: ktg0[0:64, jj * 64:(jj + 1) * 64]
                    l1 = lambda jj=jj, ktg1=ktg1: ktg1[64:128, jj * 64:(jj + 1) * 64]
                else:
                    kr = (kc // 512 % 3) * 512 + (kc % 512)
                    l0 = lambda kr=kr, kw=kw, kt0=kt0: kt0[0:64, kr:kr + kw]
                    l1 = lambda kr=kr, kw=kw, kt1=kt1: kt1[64:128, kr:kr + kw]

                def sc_pair(te, a0_=a0_, b0=b0, kw=kw, banks=banks, l0=l0, l1=l1,
                            hp=hp, qcol=qcol):
                    te.matmul(pb[banks[0]][0:kw, a0_:b0], l0(),
                              qhat[hp][0:64, qcol + a0_:qcol + b0],
                              start=True, stop=True, tile_position=(0, 0))
                    return te.matmul(pb[banks[1]][0:kw, a0_:b0], l1(),
                                     qhat[hp][64:128, qcol + a0_:qcol + b0],
                                     start=True, stop=True, tile_position=(64, 0))
                P.emit("pe", sc_pair, waits=w, inc=s_pe)
                n_sc = s_pe.n
                n_ex = []
                for hh in range(2):
                    eb = 2 * hh + (ci % 2)
                    w = [(s_pe, n_sc)]
                    if eT_free[eb]:
                        w.append(eT_free[eb])
                    P.emit("act", lambda a_, eb=eb, a0_=a0_, b0=b0, kw=kw,
                           bk=banks[hh]: a_.activation(
                               out=eT_sb[eb][0:kw, 0:b0 - a0_],
                               in_=pb[bk][0:kw, a0_:b0], func=AF.Exp, scale=0.125),
                           waits=w, inc=s_act)
                    n_ex.append(s_act.n)
                    ps_s_free[banks[hh]] = (s_act, s_act.n)
                yield
                n_mk = [None, None]
                for hh in range(2):
                    eb = 2 * hh + (ci % 2)
                    if j == "g":
                        if g == 0:
                            P.emit("dve", lambda v, eb=eb: v.tensor_mul(
                                out=eT_sb[eb][0:GT, 0:512],
                                in0=eT_sb[eb][0:GT, 0:512], in1=maskgl_sb[:]),
                                waits=[(s_act, n_ex[hh]), (d_mk, ld_mgl),
                                       (s_dve, s_dve.n)], inc=s_dve)
                            n_mk[hh] = s_dve.n
                    else:
                        msb = maskg0_sb if g == 0 else masksh_sb
                        mo = MOFF[j]
                        P.emit("dve", lambda v, eb=eb, a0_=a0_, b0=b0, msb=msb,
                               mo=mo: v.tensor_mul(
                                   out=eT_sb[eb][:, 0:b0 - a0_],
                                   in0=eT_sb[eb][:, 0:b0 - a0_],
                                   in1=msb[:, mo:mo + (b0 - a0_)]),
                               waits=[(s_act, n_ex[hh]), (d_mk, ld_mgl),
                                      (s_dve, s_dve.n)], inc=s_dve)
                        n_mk[hh] = s_dve.n
                yield
                if j != "g":
                    while vt not in vdone:
                        yield
                for hh in range(2):
                    eb = 2 * hh + (ci % 2)
                    yb = 2 + hh
                    w = [(s_act, n_ex[hh])]
                    if j == "g":
                        w.append((s_dve, GLOBC[0]))
                    else:
                        w.append(vdone[vt])
                    if n_mk[hh] is not None:
                        w.append((s_dve, n_mk[hh]))
                    if ci == 0 and ps_y_free[yb]:
                        w.append(ps_y_free[yb])

                    if j == "g":
                        vap = lambda gv=gv: vstG[0:64, gv * 65: gv * 65 + 65]
                    else:
                        vs = (vt % 8) * 260
                        vap = lambda vs=vs, kw=kw, gv=gv: vst[0:kw, vs + gv * 65: vs + gv * 65 + 65]

                    def av(te, hh=hh, eb=eb, a0_=a0_, b0=b0, kw=kw, vap=vap,
                           ci=ci, yb=yb, nch=len(chunks)):
                        return te.matmul(
                            pb[yb][0:65, a0_:b0], vap(),
                            eT_sb[eb][0:kw, 0:b0 - a0_],
                            start=(ci == 0), stop=(ci == nch - 1),
                            skip_group_check=True)
                    P.emit("pe", av, waits=w, inc=s_pe)
                    av_last[hh] = s_pe.n
                    eT_free[eb] = (s_pe, s_pe.n)
                if j != "g":
                    rec = VUSED.setdefault(vt, [0, None])
                    rec[0] += 1
                    rec[1] = (s_pe, s_pe.n)
                yield
            n_rc = [None, None]
            for hh in range(2):
                yb = 2 + hh
                P.emit("act", lambda a, hh=hh, yb=yb: a.activation(
                    out=yv_sb[hh][64:65, :], in_=pb[yb][64:65, :], func=AF.Ln),
                    waits=[(s_pe, av_last[hh]), (s_act, s_act.n)], inc=s_act)
                P.emit("act", lambda a, hh=hh: a.activation(
                    out=rec_sb[hh][:], in_=yv_sb[hh][64:65, :], func=AF.Exp,
                    scale=-1.0), waits=[(s_act, s_act.n)], inc=s_act)
                n_rc[hh] = s_act.n
            for hh in range(2):
                yb = 2 + hh
                w = [(s_act, n_rc[hh]), (d_c0, LD0)]
                if p4_free:
                    w.append(p4_free)
                P.emit("pe", lambda te, hh=hh: te.matmul(
                    pb[4][0:64, :], onesk1_sb[0:1, 0:64], rec_sb[hh][:],
                    start=True, stop=True), waits=w, inc=s_pe)
                n_bc = s_pe.n
                P.emit("act", lambda a, hh=hh, yb=yb: a.copy(
                    out=yv_sb[hh][0:64, :], in_=pb[yb][0:64, :]),
                    waits=[(s_pe, n_bc)], inc=s_act)
                n_yv = s_act.n
                ps_y_free[yb] = (s_act, n_yv)
                wy = [(s_act, n_yv), (s_pe, n_w2mm), (s_dve, s_dve.n)]
                if P4G.get(g - 2) is not None:
                    wy.append((s_dve, P4G[g - 2]))
                P.emit("dve", lambda v, hh=hh, hp=hp, g=g: v.tensor_mul(
                    out=yT[hp][hh * 64:(hh + 1) * 64,
                               (g % 2) * 512:(g % 2) * 512 + 512],
                    in0=yv_sb[hh][0:64, :], in1=pb[4][0:64, :]),
                    waits=wy, inc=s_dve)
                p4_free = (s_dve, s_dve.n)
                yield
            ATT_DONE[(g, hp)] = s_dve.n

        for it in range(7):
            fill_q = []
            if it < NKVG:
                gb = it % 2
                w = []
                if xg_users[gb] is not None:
                    w.append((s_pe, xg_users[gb]))
                ldx = gp_dma(lambda gb=gb: xTg[gb][:],
                             lambda g5=it: dram["xT"].ap().rearrange(
                                 "(c p) n -> p c n", p=128)[:, :, g5 * 512:(g5 + 1) * 512],
                             d_x[gb], waits=w)
                wc = [cs_users[gb]] if cs_users[gb] else []
                ldc = sy_dma(lambda gb=gb: cscg[gb][:],
                             lambda g5=it: dram["csc"].ap()[:, g5 * 512:(g5 + 1) * 512],
                             d_cs[gb], waits=wc)
                ldc2 = sy_dma(lambda gb=gb: cssg[gb][:],
                              lambda g5=it: dram["css"].ap()[:, g5 * 512:(g5 + 1) * 512],
                              d_cs[gb])
                xbuf = (gb, ldx, ldc2)
                for od in range(2):
                    fill_q.append(proj_unit_gen("k", it, od, xbuf, KDONE, (it, od)))
                nt4 = 1 if it == 4 else 4
                for t4 in range(nt4):
                    fill_q.append(vproj_gen(it * 4 + t4, xbuf))
                if it == 4:
                    vdefer[0] = xbuf
                if it == 0:
                    fill_q.append(glob_copy_gen())
                if it >= 1:
                    for od in range(8):
                        fill_q.append(proj_unit_gen("q", it - 1, od, xbuf,
                                                    QDONE, (it - 1, od)))
                    fill_q.append(xrw_gen(it - 1, gb, ldx))
            if it == 5:
                for t4 in range(1, 4):
                    fill_q.append(vproj_gen(16 + t4, vdefer[0]))
            att_q = []
            if 2 <= it <= 5:
                att_q = [att_unit_gen(it - 2, hp) for hp in range(8)]
            p4_q = []
            if 3 <= it <= 6:
                p4_q = [emit_p4_tile(t) for t in range(4 * (it - 3), 4 * (it - 3) + 4)]
            drive2(att_q, fill_q, p4_q)
            if it < NKVG:
                xg_users[gb] = s_pe.n
                cs_users[gb] = (s_dve, s_dve.n)
            if 2 <= it <= 5:
                ATTG[it - 2] = s_dve.n
            if 3 <= it <= 6:
                P4G[it - 3] = s_dve.n
        P.marks['P3'] = P.gid
        XM_AG = c_cc.n

        P.marks['P4'] = P.gid
        P4 = [(s_pe, s_pe.n), (s_act, s_act.n), (s_dve, s_dve.n)]
        es_a.close()

        ptr_free = None
        es5 = contextlib.ExitStack()
        lga_sb = sbuf(es5, "lga_sb", [128, 1024], F32)
        iotae_sb = sbuf(es5, "iotae_sb", [128, 1024], F32)
        prod_sb = sbuf(es5, "prod_sb", [128, 1024], F32)
        tmr2_sb = [sbuf(es5, f"tmr2_sb{i}", [128, 128], F32) for i in range(2)]
        v1_sb = sbuf(es5, "v1_sb", [128, 128], F32)
        v2_sb = sbuf(es5, "v2_sb", [128, 128], F32)
        top1_sb = sbuf(es5, "top1_sb", [128, 1024], F32)
        top2_sb = sbuf(es5, "top2_sb", [128, 1024], F32)
        mskd_sb = sbuf(es5, "mskd_sb", [128, 1024], F32)
        asg_sb = sbuf(es5, "asg_sb", [128, 1024], F32)
        rank_sb = sbuf(es5, "rank_sb", [128, 1024], F32)
        rt_sb = sbuf(es5, "rt_sb", [128, NE], F32)
        base_sb = sbuf(es5, "base_sb", [128, NE], F32)
        g1_sb = sbuf(es5, "g1_sb", [128, 128], F32)
        g2_sb = sbuf(es5, "g2_sb", [128, 128], F32)
        fld_sb = [sbuf(es5, f"fld_sb{i}", [128, 128], F32) for i in range(6)]
        acc_sb = [sbuf(es5, f"acc_sb{i}", [128, 128], F32) for i in range(4)]
        tmpr_sb = [sbuf(es5, f"tmpr_sb{i}", [128, 128], F32) for i in range(2)]

        for kk in range(8):
            for cc in range(NCORES):
                P.emit("sync", lambda s, kk=kk, cc=cc: s.dma_start(
                    out=lga_sb[cc * 16 + kk * 2: cc * 16 + kk * 2 + 2, :]
                        .rearrange("p (n e) -> p n e", e=NE),
                    in_=dram["lg_all"].ap()[kk * 2048 + cc * 256:
                                            kk * 2048 + (cc + 1) * 256, :]
                        .rearrange("(p n) e -> p n e", p=2)),
                    waits=[(c_cc, LG_CHUNK[kk])] + P4, inc=d_lg, inc_k=16)
        n_lga = d_lg.n
        P.emit("sync", lambda s: s.dma_start(out=iotae_sb[:], in_=dram["iotae"].ap()),
               waits=[], inc=d_io, inc_k=16)
        n_lga2 = d_io.n

        def ev(t, e):
            return t[:].rearrange("p (n e) -> p e n", e=NE)[:, e:e + 1, :] \
                       .rearrange("p a n -> p (a n)")

        P.emit("dve", lambda v: v.reduce_max(
            out=v1_sb[:], in_=lga_sb[:].rearrange("p (n e) -> p n e", e=NE),
            axis=AX.X), waits=[(d_lg, n_lga), (s_dve, s_dve.n)], inc=s_dve)
        n_v1 = s_dve.n
        for e in range(NE):
            P.emit("dve", lambda v, e=e: v.tensor_tensor(
                out=ev(top1_sb, e), in0=ev(lga_sb, e), in1=v1_sb[:], op=ALU.is_equal),
                waits=[(s_dve, n_v1)], inc=s_dve)
        n_t1 = s_dve.n
        P.emit("dve", lambda v: v.scalar_tensor_tensor(
            out=mskd_sb[:], in0=top1_sb[:], scalar=-BIG, in1=lga_sb[:],
            op0=ALU.mult, op1=ALU.add), waits=[(s_dve, n_t1)], inc=s_dve)
        n_msk = s_dve.n
        P.emit("dve", lambda v: v.reduce_max(
            out=v2_sb[:], in_=mskd_sb[:].rearrange("p (n e) -> p n e", e=NE),
            axis=AX.X), waits=[(s_dve, n_msk)], inc=s_dve)
        n_v2 = s_dve.n
        for e in range(NE):
            P.emit("dve", lambda v, e=e: v.tensor_tensor(
                out=ev(top2_sb, e), in0=ev(mskd_sb, e), in1=v2_sb[:], op=ALU.is_equal),
                waits=[(s_dve, n_v2)], inc=s_dve)
        n_t2 = s_dve.n
        P.emit("dve", lambda v: v.tensor_sub(out=tmpr_sb[0][:], in0=v1_sb[:],
                                             in1=v2_sb[:]),
               waits=[(s_dve, n_t2)], inc=s_dve)
        n_d = s_dve.n
        P.emit("act", lambda a: a.activation(out=g1_sb[:], in_=tmpr_sb[0][:],
                                             func=AF.Sigmoid),
               waits=[(s_dve, n_d), (s_act, s_act.n)], inc=s_act)
        n_g1 = s_act.n
        P.emit("act", lambda a: a.activation(out=g2_sb[:], in_=g1_sb[:],
                                             func=AF.Copy, scale=-1.0, bias=1.0),
               waits=[(s_act, n_g1)], inc=s_act)
        n_g2 = s_act.n
        P.emit("dve", lambda v: v.tensor_add(out=asg_sb[:], in0=top1_sb[:],
                                             in1=top2_sb[:]),
               waits=[(s_dve, n_t2)], inc=s_dve)
        n_asg = s_dve.n
        P.emit("dve", lambda v: v.reduce_sum(
            out=rt_sb[:], in_=asg_sb[:].rearrange("p (n e) -> p e n", e=NE),
            axis=AX.X), waits=[(s_dve, n_asg)], inc=s_dve)
        n_rt = s_dve.n
        w = [(s_dve, n_rt), (d_c0, LD0)]
        if ps_y_free[2]:
            w.append(ps_y_free[2])
        mm_group(lambda: pb[2][:, 0:NE], [(lambda: lstrict_sb[:], lambda: rt_sb[:])],
                 waits=w)
        n_bm = s_pe.n
        P.emit("act", lambda a: a.copy(out=base_sb[:], in_=pb[2][:, 0:NE]),
               waits=[(s_pe, n_bm)], inc=s_act)
        n_bse = s_act.n
        for e in range(NE):
            P.emit("dve", lambda v, e=e: v.tensor_tensor_scan(
                out=ev(rank_sb, e), data0=ev(asg_sb, e), data1=zrow_sb[:],
                initial=0.0, op0=ALU.add, op1=ALU.add),
                waits=[(s_dve, s_dve.n), (s_dve, n_zrow)], inc=s_dve)
            P.emit("dve", lambda v, e=e: v.scalar_tensor_tensor(
                out=ev(rank_sb, e), in0=ev(rank_sb, e), scalar=base_sb[:, e:e + 1],
                in1=ev(asg_sb, e), op0=ALU.add, op1=ALU.subtract),
                waits=[(s_dve, s_dve.n), (s_act, n_bse)], inc=s_dve)
        n_rank = s_dve.n

        def accum2(dst, maskt, srct, n_after, extra=()):
            P.emit("dve", lambda v: v.tensor_mul(out=prod_sb[:], in0=maskt[:],
                                                 in1=srct[:]),
                   waits=[(s_dve, n_after)] + list(extra), inc=s_dve)
            P.emit("dve", lambda v: v.reduce_sum(
                out=dst[:], in_=prod_sb[:].rearrange("p (n e) -> p n e", e=NE),
                axis=AX.X), waits=[(s_dve, s_dve.n)], inc=s_dve)
            return s_dve.n

        n_e1 = accum2(acc_sb[0], top1_sb, iotae_sb, n_rank, [(d_io, n_lga2)])
        n_e2 = accum2(acc_sb[1], top2_sb, iotae_sb, n_e1)
        n_rk1 = accum2(acc_sb[2], top1_sb, rank_sb, n_e2)
        n_rk2 = accum2(acc_sb[3], top2_sb, rank_sb, n_rk1)

        def fields_gen(et, rkt, fi, scr, n_after):
            P.emit("dve", lambda v: v.tensor_scalar(
                out=scr[0][:], in0=rkt[:], scalar1=float(CAP), scalar2=None,
                op0=ALU.is_lt), waits=[(s_dve, n_after)], inc=s_dve)
            n_a = s_dve.n
            yield
            P.emit("dve", lambda v: v.tensor_scalar(
                out=scr[1][:], in0=scr[0][:], scalar1=-1.0, scalar2=1.0,
                op0=ALU.mult, op1=ALU.add), waits=[(s_dve, n_a)], inc=s_dve)
            n_b = s_dve.n
            yield
            P.emit("dve", lambda v: v.scalar_tensor_tensor(
                out=fld_sb[fi][:], in0=et[:], scalar=float(CAP), in1=rkt[:],
                op0=ALU.mult, op1=ALU.add), waits=[(s_dve, n_b)], inc=s_dve)
            n_c = s_dve.n
            yield
            P.emit("dve", lambda v: v.scalar_tensor_tensor(
                out=fld_sb[fi][:], in0=scr[1][:], scalar=BIG, in1=fld_sb[fi][:],
                op0=ALU.mult, op1=ALU.add), waits=[(s_dve, n_c)], inc=s_dve)
            yield
            P.emit("dve", lambda v: v.memset(scr[0][:], 0.0),
                   waits=[(s_dve, s_dve.n)], inc=s_dve)
            n_d = s_dve.n
            yield
            for i in range(1, AGC):
                P.emit("dve", lambda v, i=i: v.tensor_scalar(
                    out=scr[1][:], in0=rkt[:], scalar1=float(CPC * i),
                    scalar2=None, op0=ALU.is_ge),
                    waits=[(s_dve, s_dve.n)], inc=s_dve)
                P.emit("dve", lambda v: v.tensor_add(
                    out=scr[0][:], in0=scr[0][:], in1=scr[1][:]),
                    waits=[(s_dve, s_dve.n)], inc=s_dve)
                yield
            P.emit("dve", lambda v: v.scalar_tensor_tensor(
                out=fld_sb[fi + 1][:], in0=et[:], scalar=float(CPC), in1=rkt[:],
                op0=ALU.mult, op1=ALU.add), waits=[(s_dve, s_dve.n)], inc=s_dve)
            yield
            P.emit("dve", lambda v: v.scalar_tensor_tensor(
                out=fld_sb[fi + 1][:], in0=scr[0][:],
                scalar=float((NCORES - 1) * CPC), in1=fld_sb[fi + 1][:],
                op0=ALU.mult, op1=ALU.add), waits=[(s_dve, s_dve.n)], inc=s_dve)
            yield
            P.emit("dve", lambda v: v.tensor_scalar(
                out=scr[0][:], in0=rkt[:], scalar1=float(CAP), scalar2=None,
                op0=ALU.is_lt), waits=[(s_dve, s_dve.n)], inc=s_dve)
            yield
            P.emit("dve", lambda v: v.tensor_mul(
                out=fld_sb[fi + 1][:], in0=fld_sb[fi + 1][:], in1=scr[0][:]),
                waits=[(s_dve, s_dve.n)], inc=s_dve)
            yield
            P.emit("dve", lambda v: v.tensor_scalar(
                out=scr[0][:], in0=scr[0][:], scalar1=-1.0, scalar2=1.0,
                op0=ALU.mult, op1=ALU.add), waits=[(s_dve, s_dve.n)], inc=s_dve)
            yield
            P.emit("dve", lambda v: v.scalar_tensor_tensor(
                out=fld_sb[fi + 1][:], in0=scr[0][:], scalar=float(PADROW),
                in1=fld_sb[fi + 1][:], op0=ALU.mult, op1=ALU.add),
                waits=[(s_dve, s_dve.n)], inc=s_dve)

        _ga = fields_gen(acc_sb[0], acc_sb[2], 0, tmpr_sb, n_rk2)
        _gb = fields_gen(acc_sb[1], acc_sb[3], 2, tmr2_sb, n_rk2)
        _alive = [_ga, _gb]
        while _alive:
            for _gx in list(_alive):
                try:
                    next(_gx)
                except StopIteration:
                    _alive.remove(_gx)
        n_f2 = s_dve.n
        P.emit("dve", lambda v: v.tensor_copy(out=fld_sb[4][:], in_=g1_sb[:]),
               waits=[(s_act, n_g2), (s_dve, n_f2)], inc=s_dve)
        P.emit("dve", lambda v: v.tensor_copy(out=fld_sb[5][:], in_=g2_sb[:]),
               waits=[(s_dve, s_dve.n)], inc=s_dve)
        n_flds = s_dve.n

        n_fT = []
        for fi in range(6):
            w = [(s_dve, n_flds), (d_c0, LD0)]
            if ptr_free:
                w.append(ptr_free)
            elif ps_y_free[3]:
                w.append(ps_y_free[3])
            mm_group(lambda: pb[3][:, 0:16],
                     [(lambda fi=fi: fld_sb[fi][:], lambda: rowsel_sb[:])], waits=w)
            n_m = s_pe.n
            P.emit("act", lambda a, fi=fi: a.copy(out=fT_sb[fi][:],
                                                  in_=pb[3][:, 0:16]),
                   waits=[(s_pe, n_m), (s_act, s_act.n)], inc=s_act)
            ptr_free = (s_act, s_act.n)
            n_fT.append(s_act.n)
        for i, fi in enumerate((0, 2, 1, 3)):
            P.emit("dve", lambda v, i=i, fi=fi: v.tensor_copy(
                out=fTi_sb[i][:], in_=fT_sb[fi][:]),
                waits=[(s_act, n_fT[fi]), (s_dve, s_dve.n)], inc=s_dve)
        for sl in range(2):
            P.emit("dve", lambda v, sl=sl: v.tensor_copy(
                out=val_sb[sl][:].rearrange("p (r two) -> p r two", two=2)[:, :, 0:1],
                in_=tokval_sb[:].rearrange("p (r one) -> p r one", one=1)),
                waits=[(d_c0, LD0), (s_dve, s_dve.n)], inc=s_dve)
            P.emit("dve", lambda v, sl=sl: v.tensor_copy(
                out=val_sb[sl][:].rearrange("p (r two) -> p r two", two=2)[:, :, 1:2],
                in_=fT_sb[4 + sl][:].rearrange("p (r one) -> p r one", one=1)),
                waits=[(s_act, n_fT[4 + sl]), (s_dve, s_dve.n)], inc=s_dve)
        n_vals = s_dve.n

        for sl in range(2):
            for r in range(16):
                P.emit("pool", lambda g_, sl=sl, r=r: g_.indirect_dma_start(
                    out=dram["table_loc"].ap(),
                    out_offset=IndirectOffsetOnAxis(fTi_sb[sl][:, r:r + 1], 0),
                    in_=val_sb[sl][:, 2 * r:2 * r + 2], in_offset=None,
                    bounds_check=REGS["b_scat"], oob_is_err=False),
                    waits=[(s_dve, n_vals), (d_zi, ZINIT)], inc=d_sc, inc_k=16)
        N_SCAT = d_sc.n
        P.emit("pool", lambda g_: g_.collective_compute(
            "ReduceScatter", ALU.add, replica_groups=[list(range(NCORES))],
            ins=[dram["table_loc"].ap()], outs=[dram["table_my"].ap()]),
            waits=[(d_sc, N_SCAT)], inc=c_cc)
        RS_DONE = c_cc.n

        P.marks['RT'] = P.gid
        RT = [(s_pe, s_pe.n), (s_act, s_act.n), (s_dve, s_dve.n), (d_sc, N_SCAT)]
        es5.close()

        es6 = contextlib.ExitStack()
        eW_sb = sbuf(es6, "eW_sb", [128, 8 * DFF], BF16)
        eV_sb = sbuf(es6, "eV_sb", [128, 8 * DFF], BF16)
        eWo_sb = sbuf(es6, "eWo_sb", [128, 16 * D], BF16)
        xeT = [sbuf(es6, f"xeT{i}", [128, 1024], BF16) for i in range(8)]
        gT = [sbuf(es6, f"gT{i}", [128, 512], BF16) for i in range(16)]

        ld_eW = gp_dma(lambda: eW_sb[:].rearrange("p (c n) -> p c n", n=DFF), lambda: r3("eW", DFF), d_eW, waits=RT)
        ld_eV = gp_dma(lambda: eV_sb[:].rearrange("p (c n) -> p c n", n=DFF), lambda: r3("eV", DFF), d_eV)
        ld_eWo = gp_dma(lambda: eWo_sb[:].rearrange("p (c n) -> p c n", n=D), lambda: r3("eWo", D), d_eWo)

        P.emit("sync", lambda s: s.dma_start(
            out=tab_sb[:].rearrange("p (j f) -> p j f", f=2),
            in_=dram["table_my"].ap().rearrange("(j p) f -> p j f", p=128)),
            waits=[(c_cc, RS_DONE)], inc=d_ms, inc_k=16)
        n_tab = d_ms.n
        P.emit("dve", lambda v: v.tensor_copy(
            out=gidx_sb[:].rearrange("p (r one) -> p r one", one=1),
            in_=tab_sb[:].rearrange("p (r two) -> p r two", two=2)[:, :, 0:1]),
            waits=[(d_ms, n_tab), (s_dve, s_dve.n)], inc=s_dve)
        n_gidx = s_dve.n
        P.emit("dve", lambda v: v.tensor_copy(
            out=gate_sb[:].rearrange("p (r one) -> p r one", one=1),
            in_=tab_sb[:].rearrange("p (r two) -> p r two", two=2)[:, :, 1:2]),
            waits=[(s_dve, s_dve.n)], inc=s_dve)
        n_gate = s_dve.n

        xe_free = [None, None]
        pa_free = [pp_free[0], (s_act, n_bse)]
        pbk_free = [pp_free[1], ptr_free]
        py_free = [pbc_free if False else None, None]
        py_free = [None, None]
        gT_users = None
        ye_free = [None, None]
        TRD = {0: None}
        TRQ = []
        TR_DONE = {}
        tr_left = {}

        def emit_gather(j):
            w = [(s_dve, n_gidx), (c_cc, XM_AG), (d_zi, ZINIT)]
            if xe_free[j % 2]:
                w.append(xe_free[j % 2])
            P.emit("pool", lambda g_, j=j: g_.indirect_dma_start(
                out=xe_sb[j % 2][:], out_offset=None,
                in_=dram["xm_pad"].ap(),
                in_offset=IndirectOffsetOnAxis(gidx_sb[:, j:j + 1], 0),
                bounds_check=REGS["b_gath"], oob_is_err=False),
                waits=w, inc=d_ix[j % 2], inc_k=16)
            return (d_ix[j % 2], d_ix[j % 2].n)

        GATH = {}
        XU = [None, None]
        allow = [2]
        for j in range(2):
            GATH[j] = emit_gather(j)
            for fc in range(8):
                TRQ.append((j, fc))
        nextg = [2]

        def emit_transpose_pair():
            for _ in range(2):
                if not TRQ or TRQ[0][0] // 4 >= allow[0]:
                    return
                j, fc = TRQ.pop(0)
                half = (j // 4) % 2
                w = [GATH[j], (d_c0, LD0)]
                if TRD[0]:
                    w.append(TRD[0])
                if XU[half]:
                    w.append(XU[half])
                P.emit("pe", lambda te, j=j, fc=fc: te.transpose(
                    pbt[:], xe_sb[j % 2][:, fc * 128:(fc + 1) * 128], identb_sb[:]),
                    waits=w, inc=s_pe)
                n_tr = s_pe.n
                P.emit("dve", lambda v, j=j, fc=fc, half=half: v.tensor_copy(
                    out=xeT[fc][:, half * 512 + (j % 4) * 128: half * 512 + (j % 4) * 128 + 128],
                    in_=pbt[:]),
                    waits=[(s_pe, n_tr), (s_dve, s_dve.n)], inc=s_dve)
                TRD[0] = (s_dve, s_dve.n)
                if fc == 7:
                    TR_DONE[j] = s_dve.n
                    xe_free[j % 2] = (s_pe, s_pe.n)
                    if nextg[0] < 16:
                        GATH[nextg[0]] = emit_gather(nextg[0])
                        for fc2 in range(8):
                            TRQ.append((nextg[0], fc2))
                        nextg[0] += 1

        while TRQ and TRQ[0][0] < 4:
            emit_transpose_pair()

        for jg in range(4):
            for hc in range(16):
                ab = hc % 2
                w = [(d_eW, ld_eW), (s_dve, TR_DONE[jg * 4 + 3])]
                if pa_free[ab]:
                    w.append(pa_free[ab])
                mm_group(lambda ab=ab: pb[[0, 2][ab]][:],
                         [(lambda fc=fc, hc=hc: eW_sb[:, fc * DFF + hc * 128: fc * DFF + hc * 128 + 128],
                           lambda fc=fc, jg=jg: xeT[fc][:, (jg % 2) * 512:(jg % 2) * 512 + 512])
                          for fc in range(8)], waits=w)
                n_a = s_pe.n
                emit_transpose_pair()
                w = [(d_eV, ld_eV)]
                if pbk_free[ab]:
                    w.append(pbk_free[ab])
                mm_group(lambda ab=ab: pb[[1, 3][ab]][:],
                         [(lambda fc=fc, hc=hc: eV_sb[:, fc * DFF + hc * 128: fc * DFF + hc * 128 + 128],
                           lambda fc=fc, jg=jg: xeT[fc][:, (jg % 2) * 512:(jg % 2) * 512 + 512])
                          for fc in range(8)], waits=w)
                n_b = s_pe.n
                emit_transpose_pair()
                si = hc % 2
                P.emit("act", lambda a, si=si, ab=ab: a.activation(
                    out=silu_sb[si][:], in_=pb[[1, 3][ab]][:], func=AF.Silu),
                    waits=[(s_pe, n_b), (s_act, s_act.n)], inc=s_act)
                n_si = s_act.n
                pbk_free[ab] = (s_act, n_si)
                wg = [(s_act, n_si), (s_dve, s_dve.n)]
                if gT_users:
                    wg.append(gT_users)
                P.emit("dve", lambda v, hc=hc, si=si, ab=ab: v.tensor_mul(
                    out=gT[hc][:], in0=pb[[0, 2][ab]][:], in1=silu_sb[si][:]),
                    waits=wg, inc=s_dve)
                pa_free[ab] = (s_dve, s_dve.n)
            N_GT = s_dve.n
            XU[jg % 2] = (s_pe, s_pe.n)
            allow[0] = jg + 3
            for j4 in range(4):
                jt = jg * 4 + j4
                n_y = []
                for half in range(2):
                    w = [(d_eWo, ld_eWo), (s_dve, N_GT)]
                    if py_free[half]:
                        w.append(py_free[half])
                    if jg == 0 and j4 == 0:
                        if half == 0 and p4_free:
                            w.append(p4_free)
                        if half == 1 and ps_s_free[5]:
                            w.append(ps_s_free[5])
                    mm_group(lambda half=half: pb[4 + half][:],
                             [(lambda hc=hc, jt=jt: gT[hc][:, (jt % 4) * 128:(jt % 4 + 1) * 128],
                               lambda hc=hc, half=half: eWo_sb[:, hc * D + half * 512: hc * D + half * 512 + 512])
                              for hc in range(16)], waits=w)
                    n_y.append(s_pe.n)
                    emit_transpose_pair()
                buf = jt % 2
                w0 = [(s_pe, n_y[0]), (s_pe, n_y[1]), (s_dve, n_gate),
                      (s_dve, s_dve.n)]
                if ye_free[buf]:
                    w0.append(ye_free[buf])
                for half in range(2):
                    P.emit("dve", lambda v, half=half, jt=jt, buf=buf: v.tensor_scalar(
                        out=ye_sb[buf][:, half * 512:(half + 1) * 512],
                        in0=pb[4 + half][:], scalar1=gate_sb[:, jt:jt + 1],
                        scalar2=None, op0=ALU.mult),
                        waits=w0 if half == 0 else [(s_dve, s_dve.n)], inc=s_dve)
                    py_free[half] = (s_dve, s_dve.n)
                n_ye = s_dve.n
                P.emit("sync", lambda s, jt=jt, buf=buf: s.dma_start(
                    out=dram["yexp_my"].ap()[jt * 128:(jt + 1) * 128, :],
                    in_=ye_sb[buf][:]), waits=[(s_dve, n_ye)], inc=d_ye[buf], inc_k=16)
                ye_free[buf] = (d_ye[buf], d_ye[buf].n)
            gT_users = (s_pe, s_pe.n)
            P.emit("pool", lambda g_, m=jg: g_.collective_compute(
                "AllGather", ALU.bypass, replica_groups=[list(range(NCORES))],
                ins=[dram["yexp_my"].ap()[m * CPC:(m + 1) * CPC, :]],
                outs=[dram["yexp_pad"].ap()[m * RPC:(m + 1) * RPC, :]]),
                waits=[(d_ye[0], d_ye[0].n), (d_ye[1], d_ye[1].n)], inc=c_cc)
        YE_AG = c_cc.n

        P.marks['P7'] = P.gid
        EX = [(s_pe, s_pe.n), (s_act, s_act.n), (s_dve, s_dve.n)]
        es6.close()
        es7 = contextlib.ExitStack()
        yc8 = [sbuf(es7, f"yc8_{i}", [128, D], BF16) for i in range(8)]
        f4_sb = [sbuf(es7, f"f4_sb{i}", [128, D], F32) for i in range(4)]
        o4_sb = [sbuf(es7, f"o4_sb{i}", [128, D], F32) for i in range(4)]
        st4_sb = [sbuf(es7, f"st4_sb{i}", [128, 8], F32) for i in range(4)]
        sqd_sb = sbuf(es7, "sqd_sb", [128, D], BF16)
        yc_free = [None] * 8
        f1_free = [None] * 4
        o_free = [None] * 4
        sqd_user = [None]
        st_user = [None] * 4

        def emit_p8_tile(tt):
            buf = tt % 4
            st = st4_sb[buf]
            for q in range(2):
                cb = 2 * buf + q
                ww = [(c_cc, YE_AG), (d_zi, ZINIT)]
                if tt < 4:
                    ww += EX
                if yc_free[cb]:
                    ww.append(yc_free[cb])
                P.emit("pool", lambda g_, q=q, tt=tt, cb=cb: g_.indirect_dma_start(
                    out=yc8[cb][:], out_offset=None,
                    in_=dram["yexp_pad"].ap(),
                    in_offset=IndirectOffsetOnAxis(fTi_sb[2 + q][:, tt:tt + 1], 0),
                    bounds_check=REGS["b_gath"], oob_is_err=False),
                    waits=ww, inc=d_cb[cb], inc_k=16)
            n_gth = [(d_cb[2 * buf], d_cb[2 * buf].n),
                     (d_cb[2 * buf + 1], d_cb[2 * buf + 1].n)]
            ww = [(d_xmf[0], d_xmf[0].n), (d_xmf[1], d_xmf[1].n)]
            if tt < 4:
                ww += EX
            if o_free[buf]:
                ww.append(o_free[buf])
            P.emit("sync", lambda s, tt=tt, buf=buf: s.dma_start(
                out=o4_sb[buf][:], in_=dram["xmf"].ap()[tt * 128:(tt + 1) * 128, :]),
                waits=ww, inc=d_o8[buf], inc_k=16)
            n_xr = (d_o8[buf], d_o8[buf].n)
            yield
            ww = n_gth + [(s_dve, s_dve.n)]
            if tt < 4:
                ww += EX
            if f1_free[buf]:
                ww.append(f1_free[buf])
            P.emit("dve", lambda v, buf=buf: v.tensor_add(
                out=f4_sb[buf][:], in0=yc8[2 * buf][:], in1=yc8[2 * buf + 1][:]),
                waits=ww, inc=s_dve)
            yc_free[2 * buf] = (s_dve, s_dve.n)
            yc_free[2 * buf + 1] = (s_dve, s_dve.n)
            n_ad = s_dve.n
            yield
            P.emit("dve", lambda v, buf=buf: v.tensor_add(
                out=f4_sb[buf][:], in0=f4_sb[buf][:], in1=o4_sb[buf][:]),
                waits=[(s_dve, n_ad), n_xr], inc=s_dve)
            n_r2 = s_dve.n
            yield
            P.emit("dve", lambda v, buf=buf, st=st: v.reduce_sum(
                out=st[:, 0:1], in_=f4_sb[buf][:], axis=AX.X),
                waits=[(s_dve, n_r2)], inc=s_dve)
            n_s1 = s_dve.n
            ww = [(s_dve, n_r2), (s_act, s_act.n)]
            if sqd_user[0]:
                ww.append(sqd_user[0])
            if st_user[buf]:
                ww.append(st_user[buf])
            P.emit("act", lambda a, buf=buf, st=st: a.activation(
                out=sqd_sb[:], in_=f4_sb[buf][:], func=AF.Square,
                accum_out=st[:, 1:2]), waits=ww, inc=s_act)
            n_sq = s_act.n
            sqd_user[0] = (s_act, n_sq)
            yield
            P.emit("dve", lambda v, st=st: v.tensor_scalar(
                out=st[:, 2:3], in0=st[:, 0:1], scalar1=1.0 / D,
                scalar2=None, op0=ALU.mult),
                waits=[(s_dve, n_s1), (s_dve, s_dve.n)], inc=s_dve)
            P.emit("dve", lambda v, st=st: v.tensor_mul(
                out=st[:, 3:4], in0=st[:, 2:3], in1=st[:, 2:3]),
                waits=[(s_dve, s_dve.n)], inc=s_dve)
            P.emit("dve", lambda v, st=st: v.scalar_tensor_tensor(
                out=st[:, 4:5], in0=st[:, 1:2], scalar=1.0 / D, in1=st[:, 3:4],
                op0=ALU.mult, op1=ALU.subtract),
                waits=[(s_act, n_sq), (s_dve, s_dve.n)], inc=s_dve)
            n_var = s_dve.n
            yield
            P.emit("act", lambda a, st=st: a.activation(
                out=st[:, 5:6], in_=st[:, 4:5], func=AF.Ln,
                bias=EPS_LN), waits=[(s_dve, n_var), (s_act, s_act.n)], inc=s_act)
            we = [(s_act, s_act.n)]
            if st_user[buf]:
                we.append(st_user[buf])
            P.emit("act", lambda a, st=st: a.activation(
                out=st[:, 6:7], in_=st[:, 5:6], func=AF.Exp,
                scale=-0.5), waits=we, inc=s_act)
            n_rs = s_act.n
            yield
            ww = [(s_act, n_rs), (s_dve, s_dve.n)]
            P.emit("dve", lambda v, buf=buf, st=st: v.tensor_scalar(
                out=o4_sb[buf][:], in0=f4_sb[buf][:], scalar1=st[:, 2:3],
                scalar2=st[:, 6:7], op0=ALU.subtract, op1=ALU.mult),
                waits=ww, inc=s_dve)
            n_o = s_dve.n
            f1_free[buf] = (s_dve, n_o)
            st_user[buf] = (s_dve, n_o)
            P.emit("sync", lambda s, tt=tt, buf=buf: s.dma_start(
                out=out_t.ap()[tt * 128:(tt + 1) * 128, :], in_=o4_sb[buf][:]),
                waits=[(s_dve, n_o)], inc=d_s8[buf], inc_k=16)
            o_free[buf] = (d_s8[buf], d_s8[buf].n)

        gens = [emit_p8_tile(t) for t in range(NT)]
        for i in range(0, NT, 4):
            alive = list(gens[i:i + 4])
            while alive:
                for gx in list(alive):
                    try:
                        next(gx)
                    except StopIteration:
                        alive.remove(gx)
        P.emit("sync", lambda s: s.nop(),
               waits=[(d_s8[i], d_s8[i].n) for i in range(4)])
        P.marks['P8'] = P.gid
        es7.close()

        _LAST_PLAN["marks"] = dict(P.marks)
        _LAST_PLAN["total"] = P.gid
        P.trunc = trunc
        for s in P.sems:
            s.h = es0.enter_context(nc.semaphore(s.name))
        with nc.Block() as block:
            @block.tensor
            def _(e):
                P.replay("pe", e, nc)

            @block.scalar
            def _(e):
                P.replay("act", e, nc)

            @block.vector
            def _(e):
                P.replay("dve", e, nc)

            @block.gpsimd
            def _(e):
                P.replay("pool", e, nc)

            @block.sync
            def _(e):
                P.replay("sync", e, nc)
    return nc


_NC_CACHE = {}


def kernel(L0_attn_x, L0_attn_Wq, L0_attn_Wk, L0_attn_Wv, L0_attn_Wo,
           L0_ffn_router_w, L0_ffn_router_b, L0_ffn_W, L0_ffn_V, L0_ffn_W_out):
    X = np.asarray(L0_attn_x, dtype=np.float32)
    Wq = np.asarray(L0_attn_Wq, dtype=np.float32)
    Wk = np.asarray(L0_attn_Wk, dtype=np.float32)
    Wv = np.asarray(L0_attn_Wv, dtype=np.float32)
    Wo = np.asarray(L0_attn_Wo, dtype=np.float32)
    RW = np.asarray(L0_ffn_router_w, dtype=np.float32)
    RB = np.asarray(L0_ffn_router_b, dtype=np.float32)
    EW = np.asarray(L0_ffn_W, dtype=np.float32)
    EV = np.asarray(L0_ffn_V, dtype=np.float32)
    EWo = np.asarray(L0_ffn_W_out, dtype=np.float32)
    if "nc" not in _NC_CACHE:
        _NC_CACHE["nc"] = build_nc()
    in_maps = [stage_core_inputs(c, X, Wq, Wk, Wv, Wo, RW, RB, EW, EV, EWo)
               for c in range(NCORES)]
    res = run_bass_kernel_spmd(_NC_CACHE["nc"], in_maps, core_ids=list(range(NCORES)))
    out = np.concatenate([np.asarray(res.results[c]["out"]) for c in range(NCORES)],
                         axis=0)
    return out.reshape(B, S, D).astype(np.float32)


# version 13 (final)
# speedup vs baseline: 1.0292x; 1.0273x over previous
import contextlib
import numpy as np
import ml_dtypes
import concourse.bass as bass
import concourse.mybir as mybir
from concourse.bass_utils import run_bass_kernel_spmd
from concourse.bass_core_rust import IndirectOffsetOnAxis

F32 = mybir.dt.float32
BF16 = mybir.dt.bfloat16
I32 = mybir.dt.int32
AF = mybir.ActivationFunctionType
ALU = mybir.AluOpType
AX = mybir.AxisListType
BF = ml_dtypes.bfloat16

NCORES = 8
B, S, D = 2, 8192, 1024
T = 2048
HALO = 512
KV = HALO + T
NKV, HD = 4, 64
LW, GT = 256, 64
DFF = 2048
NE, CAP = 8, 2048
NG, NT = T // 512, T // 128
NKVG, NKVT = KV // 512, KV // 128
AGC = 4
CPC = CAP // AGC
RPC = NCORES * CPC
TPC = T // AGC
EPS_RMS = float(np.finfo(np.float32).eps)
EPS_LN = 1e-5
BIG = 1.0e9
PADROW = NCORES * CAP
CHUNK_RANGES = [(0, 127), (0, 255), (0, 383), (128, 512), (256, 512), (384, 512)]


def _allowed(q, k):
    return (k <= q) & ((k >= q - LW + 1) | (k < GT))


def _kvtok(col, s0):
    col = np.asarray(col)
    pos = np.where(col < GT, col, s0 - (HALO - GT) + (col - GT))
    pos = np.where(col >= HALO, s0 + (col - HALO), pos)
    return pos


def _local_masks(g, s0):
    outs = []
    for j in range(6):
        a, b = CHUNK_RANGES[j]
        kpos = _kvtok(g * 512 + 256 + 128 * j + np.arange(128), s0)
        qpos = s0 + g * 512 + np.arange(a, b)
        outs.append(_allowed(qpos[None, :], kpos[:, None]) & (kpos[:, None] >= 0))
    return outs


def _host_masks(s0):
    shared = np.concatenate(_local_masks(1, s0), axis=1).astype(BF)
    if s0 != 0:
        return shared, shared.copy(), np.ones((GT, 512), dtype=BF)
    g0_list = _local_masks(0, s0)
    g0 = np.concatenate(g0_list, axis=1).astype(BF)
    allowed = _allowed(np.arange(512)[None, :], np.arange(GT)[:, None])
    covered = np.zeros_like(allowed)
    for j in range(6):
        a, b = CHUNK_RANGES[j]
        kp = _kvtok(256 + 128 * j + np.arange(128), s0)
        for w in range(128):
            if 0 <= kp[w] < GT:
                covered[kp[w], a:b] |= g0_list[j][w]
    return shared, g0, (allowed & ~covered).astype(BF)


def _host_cossin(s0):
    pos = np.maximum(_kvtok(np.arange(KV), s0), 0).astype(np.float64)
    inv = 1.0 / (10000.0 ** (np.arange(0, HD, 2, dtype=np.float64) / HD))
    fr = pos[None, :] * inv[:, None]
    c = np.cos(fr).astype(np.float32)
    s = np.sin(fr).astype(np.float32)
    return (np.concatenate([c, c, c, c], axis=0),
            np.concatenate([s, -s, s, -s], axis=0))


def _host_tokval(core):
    out = np.zeros((128, 16), dtype=np.float32)
    n = np.arange(128)
    for r in range(16):
        tloc = r * 128 + n
        out[:, r] = 1 + (tloc // TPC) * RPC + core * TPC + (tloc % TPC)
    return out


def stage_core_inputs(core, X, Wq, Wk, Wv, Wo, RW, RB, EW, EV, EWo):
    b, s0 = core // 4, (core % 4) * T
    x = X[b]
    kvx = np.zeros((KV, D), dtype=np.float32)
    kvx[:GT] = x[:GT]
    lo = s0 - (HALO - GT)
    if lo >= 0:
        kvx[GT:HALO] = x[lo:s0]
    kvx[HALO:] = x[s0:s0 + T]
    shared, g0, gl = _host_masks(s0)
    csc, css = _host_cossin(s0)
    ident = np.eye(128, dtype=np.float32)
    rowsel = np.zeros((128, 16), dtype=np.float32)
    rowsel[16 * core + np.arange(16), np.arange(16)] = 1.0
    bdiag = np.zeros((128, 2), dtype=np.float32)
    bdiag[0:64, 0] = 1.0
    bdiag[64:128, 1] = 1.0
    exp2 = np.zeros((2, 128), dtype=np.float32)
    exp2[0, 0:64] = 1.0
    exp2[1, 64:128] = 1.0
    return {
        "xT": np.ascontiguousarray(kvx.T),
        "x_tok": np.ascontiguousarray(x[s0:s0 + T]),
        "wq": np.ascontiguousarray(Wq),
        "wk": np.ascontiguousarray(Wk),
        "wv": np.ascontiguousarray(Wv),
        "wo": np.ascontiguousarray(Wo),
        "woT": np.ascontiguousarray(Wo.T),
        "rwT": np.ascontiguousarray(RW.T),
        "rb": np.ascontiguousarray(RB.reshape(1, NE)),
        "eW": np.ascontiguousarray(EW[core]),
        "eV": np.ascontiguousarray(EV[core]),
        "eWo": np.ascontiguousarray(EWo[core]),
        "csc": csc,
        "css": css,
        "masksh": np.ascontiguousarray(shared),
        "maskg0": np.ascontiguousarray(g0),
        "maskgl": np.ascontiguousarray(gl),
        "identb": ident.astype(BF),
        "identf": ident,
        "lstrict": np.triu(np.ones((128, 128), dtype=np.float32), 1),
        "bdiag": bdiag.astype(BF),
        "expand2": exp2.astype(BF),
        "onesk1": np.ones((1, 128), dtype=BF),
        "onesk1f": np.ones((1, 128), dtype=np.float32),
        "onescol": np.ones((128, 1), dtype=np.float32),
        "tokval": _host_tokval(core),
        "rowsel": rowsel,
        "fconst": np.tile(np.array([[EPS_RMS, EPS_LN, 1.0, 1.0 / HD, 0.125, -1.0,
                                     0.0, 0.0]], dtype=np.float32), (128, 1)),
        "iotae": np.tile(np.arange(NE, dtype=np.float32), (128, 128)),
    }


class Sem:
    __slots__ = ("name", "n", "h")

    def __init__(self, name):
        self.name, self.n, self.h = name, 0, None


class Plan:
    ENGINES = ("pe", "act", "dve", "pool", "sync")

    def __init__(self):
        self.ops = {e: [] for e in self.ENGINES}
        self.sems = []
        self.gid = 0
        self.marks = {}

    def sem(self, name):
        s = Sem(name)
        self.sems.append(s)
        return s

    def emit(self, eng, fn, waits=(), inc=None, inc_k=1):
        waits = [(s, v) for (s, v) in waits if s is not None and v]
        if inc is not None:
            inc.n += inc_k
        self.ops[eng].append((self.gid, fn, waits, inc, inc_k))
        self.gid += 1
        return inc.n if inc is not None else None

    trunc = None

    def replay(self, eng_name, eng, nc=None):
        ctx = (nc.allow_low_precision(reason="intentional bf16 intermediates; "
               "tolerance admits bf16 matmul operands")
               if nc is not None else contextlib.nullcontext())
        with ctx:
            self._replay(eng_name, eng)

    def _replay(self, eng_name, eng):
        for gid, fn, waits, inc, inc_k in self.ops[eng_name]:
            if self.trunc is not None and gid >= self.trunc:
                continue
            for s, v in waits:
                eng.wait_ge(s.h, v)
            ins = fn(eng)
            if inc is not None:
                ins.then_inc(inc.h, inc_k)


_LAST_PLAN = {}


def build_nc(trunc=None):
    nc = bass.Bass("TRN2", target_bir_lowering=False)
    P = Plan()
    dram = {}

    def din(name, shape, dt):
        dram[name] = nc.dram_tensor(name, shape, dt, kind="ExternalInput")

    def dint(name, shape, dt):
        dram[name] = nc.dram_tensor(name, shape, dt)

    din("xT", [D, KV], F32)
    din("x_tok", [T, D], F32)
    din("wq", [D, D], F32)
    din("wk", [D, NKV * HD], F32)
    din("wv", [D, NKV * HD], F32)
    din("wo", [D, D], F32)
    din("woT", [D, D], F32)
    din("rwT", [D, NE], F32)
    din("rb", [1, NE], F32)
    din("eW", [D, DFF], F32)
    din("eV", [D, DFF], F32)
    din("eWo", [DFF, D], F32)
    din("csc", [128, KV], F32)
    din("css", [128, KV], F32)
    din("masksh", [128, 1533], BF16)
    din("maskg0", [128, 1533], BF16)
    din("maskgl", [GT, 512], BF16)
    din("identb", [128, 128], BF16)
    din("identf", [128, 128], F32)
    din("lstrict", [128, 128], F32)
    din("bdiag", [128, 2], BF16)
    din("expand2", [2, 128], BF16)
    din("onesk1", [1, 128], BF16)
    din("onesk1f", [1, 128], F32)
    din("onescol", [128, 1], F32)
    din("tokval", [128, 16], F32)
    din("rowsel", [128, 16], F32)
    din("fconst", [128, 8], F32)
    din("iotae", [128, 1024], F32)
    out_t = nc.dram_tensor("out", [T, D], F32, kind="ExternalOutput")

    dint("lg_loc", [T, NE], F32)
    dint("lg_all", [NCORES * T, NE], F32)
    dint("table_loc", [NCORES * CAP, 2], F32)
    dint("table_my", [CAP, 2], F32)
    dint("xm_my", [T, D], BF16)
    dint("xm_pad", [1 + NCORES * T, D], BF16)
    dint("xmf", [T, D], F32)
    dint("yexp_my", [CAP, D], BF16)
    dint("yexp_pad", [NCORES * CAP + 1, D], BF16)

    s_pe = P.sem("s_pe")
    s_act = P.sem("s_act")
    s_dve = P.sem("s_dve")
    d_c0 = P.sem("d_c0")
    d_zi = P.sem("d_zi")
    d_wq = P.sem("d_wq")
    d_wk = P.sem("d_wk")
    d_wv = P.sem("d_wv")
    d_wo = P.sem("d_wo")
    d_wrb = P.sem("d_wrb")
    d_woT = P.sem("d_woT")
    d_eW = P.sem("d_eW")
    d_eV = P.sem("d_eV")
    d_eWo = P.sem("d_eWo")
    d_x = [P.sem("d_x0"), P.sem("d_x1")]
    d_cs = [P.sem("d_cs0"), P.sem("d_cs1")]
    d_xt = [P.sem("d_xt0"), P.sem("d_xt1")]
    d_xmf = [P.sem("d_xmf0"), P.sem("d_xmf1")]
    d_xmb = [P.sem("d_xmb0"), P.sem("d_xmb1")]
    d_ye = [P.sem("d_ye0"), P.sem("d_ye1")]
    d_st = [P.sem("d_st0"), P.sem("d_st1")]
    d_ix = [P.sem("d_ix0"), P.sem("d_ix1")]
    d_cb = [P.sem(f"d_cb{i}") for i in range(8)]
    d_o8 = [P.sem(f"d_o8_{i}") for i in range(4)]
    d_s8 = [P.sem(f"d_s8_{i}") for i in range(4)]
    d_sc = P.sem("d_sc")
    d_ms = P.sem("d_ms")
    d_mk = P.sem("d_mk")
    d_io = P.sem("d_io")
    d_lg = P.sem("d_lg")
    c_cc = P.sem("c_cc")

    def mm_group(psum_ap, pairs, waits=()):
        n = len(pairs)

        def fn(te):
            last = None
            for i, (lf, rf) in enumerate(pairs):
                last = te.matmul(psum_ap(), lf(), rf(), start=(i == 0),
                                 stop=(i == n - 1))
            return last
        return P.emit("pe", fn, waits=waits, inc=s_pe)

    def gp_dma(dst, src, sem, waits=()):
        return P.emit("pool", lambda g, dst=dst, src=src: g.dma_start(
            out=dst(), in_=src()), waits=waits, inc=sem, inc_k=16)

    def sy_dma(dst, src, sem, waits=()):
        return P.emit("sync", lambda g, dst=dst, src=src: g.dma_start(
            out=dst(), in_=src()), waits=waits, inc=sem, inc_k=16)

    def r3(name, n):
        return dram[name].ap().rearrange("(c p) n -> p c n", p=128)

    REGS = {}

    def _setup_regs(g):
        r1 = g.alloc_register()
        g.reg_mov(r1, NCORES * CAP - 1)
        r2 = g.alloc_register()
        ins = g.reg_mov(r2, NCORES * T)
        REGS["b_scat"] = r1
        REGS["b_gath"] = r2
        return ins

    es0 = contextlib.ExitStack()
    with es0:
        sbufs = {}

        def sbuf(stk, name, shape, dt):
            t = stk.enter_context(nc.sbuf_tensor(name, shape, dt))
            sbufs[name] = t
            return t

        identb_sb = sbuf(es0, "identb_sb", [128, 128], BF16)
        identf_sb = sbuf(es0, "identf_sb", [128, 128], F32)
        lstrict_sb = sbuf(es0, "lstrict_sb", [128, 128], F32)
        zero_sb = sbuf(es0, "zero_sb", [128, 256], F32)
        zrow_sb = sbuf(es0, "zrow_sb", [128, 128], F32)
        zerobf_sb = sbuf(es0, "zerobf_sb", [1, D], BF16)
        onesk1_sb = sbuf(es0, "onesk1_sb", [1, 128], BF16)
        onesk1f_sb = sbuf(es0, "onesk1f_sb", [1, 128], F32)
        onescol_sb = sbuf(es0, "onescol_sb", [128, 1], F32)
        bdiag_sb = sbuf(es0, "bdiag_sb", [128, 2], BF16)
        expand2_sb = sbuf(es0, "expand2_sb", [2, 128], BF16)
        rwT_sb = sbuf(es0, "rwT_sb", [128, 8 * NE], F32)
        rwTb_sb = sbuf(es0, "rwTb_sb", [128, 8 * NE], BF16)
        rb_sb = sbuf(es0, "rb_sb", [1, NE], F32)
        tokval_sb = sbuf(es0, "tokval_sb", [128, 16], F32)
        rowsel_sb = sbuf(es0, "rowsel_sb", [128, 16], F32)
        w2_sb = sbuf(es0, "w2_sb", [128, 8 * NE], BF16)
        sbc_sb = sbuf(es0, "sbc_sb", [128, NE], F32)
        rbb_sb = sbuf(es0, "rbb_sb", [128, NE], F32)
        st_sb = [sbuf(es0, f"st_sb{i}", [128, 8], F32) for i in range(2)]
        fconst_sb = sbuf(es0, "fconst_sb", [128, 8], F32)
        lgl_sb = sbuf(es0, "lgl_sb", [128, NT * NE], F32)
        xrw_sb = sbuf(es0, "xrw_sb", [8, T], F32)
        lgs_sb = sbuf(es0, "lgs_sb", [128, NE], F32)
        xrt_sb = [sbuf(es0, f"xrt_sb{i}", [128, NE], F32) for i in range(2)]
        fT_sb = [sbuf(es0, f"fT_sb{i}", [128, 16], F32) for i in range(6)]
        fTi_sb = [sbuf(es0, f"fTi_sb{i}", [128, 16], I32) for i in range(4)]
        val_sb = [sbuf(es0, f"val_sb{i}", [128, 32], F32) for i in range(2)]
        tab_sb = sbuf(es0, "tab_sb", [128, 32], F32)
        gidx_sb = sbuf(es0, "gidx_sb", [128, 16], I32)
        gate_sb = sbuf(es0, "gate_sb", [128, 16], F32)
        sqln_sb = [sbuf(es0, f"sqln_sb{i}", [128, D], BF16) for i in range(2)]
        r_sb = [sbuf(es0, f"r_sb{i}", [128, D], F32) for i in range(2)]
        xmf_sb = [sbuf(es0, f"xmf_sb{i}", [128, D], F32) for i in range(2)]
        xmb_sb = [sbuf(es0, f"xmb_sb{i}", [128, D], BF16) for i in range(2)]
        yc_sb = [sbuf(es0, f"yc_sb{i}", [128, D], BF16) for i in range(4)]
        ye_sb = [sbuf(es0, f"ye_sb{i}", [128, D], BF16) for i in range(2)]
        silu_sb = [sbuf(es0, f"silu_sb{i}", [128, 512], F32) for i in range(2)]
        xe_sb = [sbuf(es0, f"xe_sb{i}", [128, D], BF16) for i in range(2)]

        pb = [es0.enter_context(nc.psum_tensor(f"pb{i}", [128, 512], F32))
              for i in range(7)]
        pbt = es0.enter_context(nc.psum_tensor("pbt", [128, 128], BF16))

        P.emit("pool", _setup_regs)
        sy_dma(lambda: identb_sb[:], lambda: dram["identb"].ap(), d_c0)
        sy_dma(lambda: identf_sb[:], lambda: dram["identf"].ap(), d_c0)
        sy_dma(lambda: lstrict_sb[:], lambda: dram["lstrict"].ap(), d_c0)
        sy_dma(lambda: onesk1_sb[:], lambda: dram["onesk1"].ap(), d_c0)
        sy_dma(lambda: onesk1f_sb[:], lambda: dram["onesk1f"].ap(), d_c0)
        sy_dma(lambda: onescol_sb[:], lambda: dram["onescol"].ap(), d_c0)
        sy_dma(lambda: bdiag_sb[:], lambda: dram["bdiag"].ap(), d_c0)
        sy_dma(lambda: expand2_sb[:], lambda: dram["expand2"].ap(), d_c0)
        sy_dma(lambda: rwT_sb[:].rearrange("p (c n) -> p c n", n=NE), lambda: r3("rwT", NE), d_c0)
        sy_dma(lambda: rb_sb[:], lambda: dram["rb"].ap(), d_c0)
        sy_dma(lambda: tokval_sb[:], lambda: dram["tokval"].ap(), d_c0)
        sy_dma(lambda: rowsel_sb[:], lambda: dram["rowsel"].ap(), d_c0)
        sy_dma(lambda: fconst_sb[:], lambda: dram["fconst"].ap(), d_c0)
        for _i, _v in enumerate([EPS_RMS, EPS_LN, 1.0, 1.0 / HD, 0.125, -1.0]):
            nc.const_aps.register(F32, float(_v), fconst_sb[:, _i:_i + 1])
        LD0 = d_c0.n
        ld_rwb = gp_dma(lambda: rwTb_sb[:].rearrange("p (c n) -> p c n", n=NE), lambda: r3("rwT", NE), d_wrb)

        n_zero = P.emit("dve", lambda v: v.memset(zero_sb[:], 0.0), inc=s_dve)
        n_zrow = P.emit("dve", lambda v: v.memset(zrow_sb[:], 0.0),
                        waits=[(s_dve, n_zero)], inc=s_dve)
        n_zbf = P.emit("dve", lambda v: v.memset(zerobf_sb[:], 0.0),
                       waits=[(s_dve, n_zrow)], inc=s_dve)
        sy_dma(lambda: dram["xm_pad"].ap()[0:1, :], lambda: zerobf_sb[:], d_zi,
               waits=[(s_dve, n_zbf)])
        sy_dma(lambda: dram["yexp_pad"].ap()[PADROW:PADROW + 1, :],
               lambda: zerobf_sb[:], d_zi, waits=[(s_dve, n_zbf)])
        sy_dma(lambda: dram["table_loc"].ap().rearrange("(a b) c -> a (b c)", a=128),
               lambda: zero_sb[:], d_zi, waits=[(s_dve, n_zero)])
        ZINIT = d_zi.n

        mm_group(lambda: pb[3][0:NE, 0:1],
                 [(lambda fc=fc: rwT_sb[:, fc * NE:(fc + 1) * NE],
                   lambda: onescol_sb[:]) for fc in range(8)],
                 waits=[(d_c0, LD0)])
        n_S = s_pe.n
        P.emit("act", lambda a: a.copy(out=st_sb[0][0:NE, 0:1],
                                       in_=pb[3][0:NE, 0:1]),
               waits=[(s_pe, n_S)], inc=s_act)
        n_Se = s_act.n
        P.emit("pe", lambda te: te.transpose(pb[3][0:1, 0:NE], st_sb[0][0:NE, 0:1],
                                             identf_sb[0:NE, 0:NE]),
               waits=[(s_act, n_Se)], inc=s_pe)
        n_St = s_pe.n
        P.emit("act", lambda a: a.copy(out=st_sb[1][0:1, 0:NE], in_=pb[3][0:1, 0:NE]),
               waits=[(s_pe, n_St)], inc=s_act)
        n_Str = s_act.n
        P.emit("pe", lambda te: te.matmul(pb[3][:, 0:NE], onesk1f_sb[0:1, :],
                                          st_sb[1][0:1, 0:NE], start=True, stop=True),
               waits=[(s_act, n_Str)], inc=s_pe)
        n_Sb = s_pe.n
        P.emit("act", lambda a: a.copy(out=sbc_sb[:], in_=pb[3][:, 0:NE]),
               waits=[(s_pe, n_Sb)], inc=s_act)
        n_Sbc = s_act.n
        P.emit("pe", lambda te: te.matmul(pb[3][:, 0:NE], onesk1f_sb[0:1, :],
                                          rb_sb[0:1, :], start=True, stop=True),
               waits=[(s_act, n_Sbc)], inc=s_pe)
        n_rb = s_pe.n
        P.emit("act", lambda a: a.copy(out=rbb_sb[:], in_=pb[3][:, 0:NE]),
               waits=[(s_pe, n_rb)], inc=s_act)
        n_rbb = s_act.n
        SETUP_PB3 = (s_act, n_rbb)

        es_a = contextlib.ExitStack()
        qhat = [sbuf(es_a, f"qhat{i}", [128, 1024], BF16) for i in range(8)]
        kTA = [sbuf(es_a, f"kTA{i}", [128, 1536], BF16) for i in range(2)]
        kTB = [sbuf(es_a, f"kTB{i}", [128, 1536], BF16) for i in range(2)]
        kTGA = sbuf(es_a, "kTGA", [128, 128], BF16)
        kTGB = sbuf(es_a, "kTGB", [128, 128], BF16)
        vst = sbuf(es_a, "vst", [128, 8 * 260], BF16)
        vstG = sbuf(es_a, "vstG", [64, 260], BF16)
        yT = [sbuf(es_a, f"yT{i}", [128, 1024], BF16) for i in range(8)]
        wo_sb = sbuf(es_a, "wo_sb", [128, 8 * D], BF16)
        masksh_sb = sbuf(es_a, "masksh_sb", [128, 1533], BF16)
        maskg0_sb = sbuf(es_a, "maskg0_sb", [128, 1533], BF16)
        maskgl_sb = sbuf(es_a, "maskgl_sb", [GT, 512], BF16)
        eT_sb = [sbuf(es_a, f"eT_sb{i}", [128, 512], BF16) for i in range(4)]
        rec_sb = [sbuf(es_a, f"rec_sb{i}", [1, 512], BF16) for i in range(2)]
        yv_sb = [sbuf(es_a, f"yv_sb{i}", [65, 512], F32) for i in range(2)]
        xTg = [sbuf(es_a, f"xTg{i}", [128, 8 * 512], BF16) for i in range(2)]
        wq_sb = sbuf(es_a, "wq_sb", [128, 8 * D], BF16)
        wk_sb = sbuf(es_a, "wk_sb", [128, 8 * 256], BF16)
        wv_sb = sbuf(es_a, "wv_sb", [128, 8 * 256], BF16)
        cscg = [sbuf(es_a, f"cscg{i}", [128, 512], F32) for i in range(2)]
        cssg = [sbuf(es_a, f"cssg{i}", [128, 512], F32) for i in range(2)]
        sq_sb = [sbuf(es_a, f"sq_sb{i}", [128, 512], BF16) for i in range(2)]
        rinv_sb = [sbuf(es_a, f"rinv_sb{i}", [2, 512], BF16) for i in range(2)]
        hat_sb = [sbuf(es_a, f"hat_sb{i}", [128, 512], F32) for i in range(2)]
        tmp_sb = [sbuf(es_a, f"tmp_sb{i}", [128, 512], F32) for i in range(2)]
        rot_sb = [sbuf(es_a, f"rot_sb{i}", [128, 512], F32) for i in range(2)]

        ld_wk = gp_dma(lambda: wk_sb[:].rearrange("p (c n) -> p c n", n=256), lambda: r3("wk", 256), d_wk)
        ld_wv = gp_dma(lambda: wv_sb[:].rearrange("p (c n) -> p c n", n=256), lambda: r3("wv", 256), d_wv)
        ld_msh = sy_dma(lambda: masksh_sb[:], lambda: dram["masksh"].ap(), d_mk)
        ld_mg0 = sy_dma(lambda: maskg0_sb[:], lambda: dram["maskg0"].ap(), d_mk)
        ld_mgl = sy_dma(lambda: maskgl_sb[:], lambda: dram["maskgl"].ap(), d_mk)
        ld_wq = ld_wo = None
        ld_woT = []
        n_w2mm = n_w2e = None

        def emit_late_w():
            nonlocal ld_wq, ld_wo, n_w2mm, n_w2e
            ld_wq = gp_dma(lambda: wq_sb[:].rearrange("p (c n) -> p c n", n=D),
                           lambda: r3("wq", D), d_wq)
            ld_wo = gp_dma(lambda: wo_sb[:].rearrange("p (c n) -> p c n", n=D),
                           lambda: r3("wo", D), d_wo)
            for oc in range(8):
                ld_woT.append(gp_dma(
                    lambda oc=oc: yT[oc][:, 0:1024],
                    lambda oc=oc: dram["woT"].ap()[oc * 128:(oc + 1) * 128, :],
                    d_woT))
            w = [(d_woT, ld_woT[-1]), (d_wrb, ld_rwb), SETUP_PB3]
            for ic in range(8):
                mm_group(lambda ic=ic: pb[3][:, ic * NE:(ic + 1) * NE],
                         [(lambda oc=oc, ic=ic: yT[oc][:, ic * 128:(ic + 1) * 128],
                           lambda oc=oc: rwTb_sb[:, oc * NE:(oc + 1) * NE])
                          for oc in range(8)], waits=w if ic == 0 else [])
            n_w2mm = s_pe.n
            P.emit("act", lambda a: a.copy(out=w2_sb[:], in_=pb[3][:, 0:64]),
                   waits=[(s_pe, n_w2mm)], inc=s_act)
            n_w2e = s_act.n
            ps_y_free[3] = (s_act, n_w2e)

        n_vone = P.emit("dve", lambda v: v.memset(
            vst[:].rearrange("p (t g c) -> p t g c", t=8, g=4)[:, :, :, 64:65], 1.0),
            waits=[(s_dve, n_zbf)], inc=s_dve)
        n_voneg = P.emit("dve", lambda v: v.memset(
            vstG[:].rearrange("p (g c) -> p g c", g=4)[:, :, 64:65], 1.0),
            waits=[(s_dve, n_vone)], inc=s_dve)

        pp_free = [None, None]
        p4_free = None
        unit = [0]
        KDONE, QDONE = {}, {}
        vdone = {}
        ATTG, P4G = {}, {}
        XRWDONE = None
        xg_users = [None, None]
        cs_users = [None, None]
        rot_user = [None, None]
        pb01_owner = [None, None]
        vdefer = [None]
        VUSED = {}
        FINQ = []
        REC_USER = [None, None]
        YV_USER = [None, None]
        LG_CHUNK = []

        def proj_unit_gen(kind, gg, od, xbuf, DONE, key):
            nonlocal p4_free
            buf = unit[0] % 2
            unit[0] += 1
            if kind == "q":
                wsb, wstride = wq_sb, D
                dst, dcol, ldw = qhat[od], (gg % 2) * 512, (d_wq, ld_wq)
                warw = ATTG.get(gg - 2)
            else:
                wsb, wstride = wk_sb, 256
                dst, dcol, ldw = kTA[od], (gg % 3) * 512, (d_wk, ld_wk)
                warw = ATTG.get(gg - 3)
            while pb01_owner[buf] is not None:
                yield
            pb01_owner[buf] = 1
            w = [ldw, (d_x[xbuf[0]], xbuf[1])]
            if pp_free[buf]:
                w.append(pp_free[buf])
            mm_group(lambda buf=buf: pb[buf][:],
                     [(lambda fc=fc: wsb[:, fc * wstride + od * 128: fc * wstride + od * 128 + 128],
                       lambda fc=fc: xTg[xbuf[0]][:, fc * 512:(fc + 1) * 512])
                      for fc in range(8)], waits=w)
            n_mm = s_pe.n
            yield
            P.emit("act", lambda a, buf=buf: a.activation(
                out=sq_sb[buf][:], in_=pb[buf][:], func=AF.Square),
                waits=[(s_pe, n_mm)], inc=s_act)
            n_sq = s_act.n
            yield
            w = [(s_act, n_sq), (d_c0, LD0)]
            if p4_free:
                w.append(p4_free)
            mm_group(lambda: pb[4][0:2, :],
                     [(lambda: bdiag_sb[:], lambda buf=buf: sq_sb[buf][:])], waits=w)
            n_ssm = s_pe.n
            w = [(s_pe, n_ssm)]
            if rot_user[buf]:
                w.append(rot_user[buf])
            P.emit("act", lambda a, buf=buf: a.activation(
                out=rot_sb[buf][0:2, :], in_=pb[4][0:2, :], func=AF.Ln,
                scale=1.0 / HD, bias=EPS_RMS), waits=w, inc=s_act)
            n_rsq = s_act.n
            p4_free = (s_act, n_rsq)
            P.emit("act", lambda a, buf=buf: a.activation(
                out=rinv_sb[buf][:], in_=rot_sb[buf][0:2, :], func=AF.Exp, scale=-0.5),
                waits=[(s_act, n_rsq)], inc=s_act)
            n_ri = (s_act, s_act.n)
            yield
            w = [n_ri]
            if p4_free:
                w.append(p4_free)
            mm_group(lambda: pb[4][:],
                     [(lambda: expand2_sb[:], lambda buf=buf: rinv_sb[buf][:])],
                     waits=w)
            n_bcm = s_pe.n
            P.emit("act", lambda a, buf=buf: a.copy(
                out=sq_sb[buf][:], in_=pb[4][:]), waits=[(s_pe, n_bcm)], inc=s_act)
            n_bce = s_act.n
            p4_free = (s_act, n_bce)
            yield
            P.emit("dve", lambda v, buf=buf: v.tensor_mul(
                out=hat_sb[buf][:], in0=pb[buf][:], in1=sq_sb[buf][:]),
                waits=[(s_act, n_bce), (s_dve, s_dve.n)], inc=s_dve)
            n_hat = s_dve.n
            pp_free[buf] = (s_dve, n_hat)
            pb01_owner[buf] = None
            yield

            def swaps(v, buf=buf):
                last = None
                for h0 in (0, 64):
                    last = v.tensor_copy(out=tmp_sb[buf][h0:h0 + 32, :],
                                         in_=hat_sb[buf][h0 + 32:h0 + 64, :])
                    last = v.tensor_copy(out=tmp_sb[buf][h0 + 32:h0 + 64, :],
                                         in_=hat_sb[buf][h0:h0 + 32, :])
                return last
            P.emit("dve", swaps, waits=[(s_dve, n_hat)], inc=s_dve)
            n_sw = s_dve.n
            yield
            P.emit("dve", lambda v, buf=buf, cb=xbuf[0]: v.tensor_mul(
                out=rot_sb[buf][:], in0=hat_sb[buf][:], in1=cscg[cb][:]),
                waits=[(s_dve, n_sw), (d_cs[xbuf[0]], xbuf[2]), n_ri], inc=s_dve)
            n_r1 = s_dve.n
            yield
            P.emit("dve", lambda v, buf=buf, cb=xbuf[0]: v.tensor_mul(
                out=tmp_sb[buf][:], in0=tmp_sb[buf][:], in1=cssg[cb][:]),
                waits=[(s_dve, n_r1)], inc=s_dve)
            n_r2 = s_dve.n
            yield
            w = [(s_dve, n_r2)]
            if warw:
                w.append((s_dve, warw))
            P.emit("dve", lambda v, buf=buf, dst=dst, dcol=dcol: v.tensor_add(
                out=dst[:, dcol:dcol + 512], in0=rot_sb[buf][:], in1=tmp_sb[buf][:]),
                waits=w, inc=s_dve)
            rot_user[buf] = (s_dve, s_dve.n)
            if kind == "k":
                P.emit("dve", lambda v, od=od, dcol=dcol: v.tensor_copy(
                    out=kTB[od][0:64, dcol:dcol + 512],
                    in_=kTA[od][64:128, dcol:dcol + 512]),
                    waits=[(s_dve, s_dve.n)], inc=s_dve)
                P.emit("dve", lambda v, od=od, dcol=dcol: v.tensor_copy(
                    out=kTB[od][64:128, dcol:dcol + 512],
                    in_=kTA[od][0:64, dcol:dcol + 512]),
                    waits=[(s_dve, s_dve.n)], inc=s_dve)
            DONE[key] = s_dve.n

        def vproj_gen(t, xbuf):
            buf = unit[0] % 2
            unit[0] += 1
            while pb01_owner[buf] is not None:
                yield
            pb01_owner[buf] = 1
            w = [(d_wv, ld_wv), (d_x[xbuf[0]], xbuf[1]), (s_dve, n_voneg)]
            if pp_free[buf]:
                w.append(pp_free[buf])
            if t == 8:
                w.append((s_dve, GLOBC[0]))
            if t >= 10:
                nrd = 8 * len([1 for j in range(6)
                               if (t - 10 - j) % 4 == 0 and 0 <= (t - 10 - j) // 4 <= 3])
                while VUSED.get(t - 8, [0, None])[0] < nrd:
                    yield
                w.append(VUSED[t - 8][1])
            mm_group(lambda buf=buf: pb[buf][:, 0:256],
                     [(lambda fc=fc, t=t: xTg[xbuf[0]][:, fc * 512 + (t % 4) * 128: fc * 512 + (t % 4) * 128 + 128],
                       lambda fc=fc: wv_sb[:, fc * 256:(fc + 1) * 256]) for fc in range(8)],
                     waits=w)
            n_mm = s_pe.n
            yield
            P.emit("act", lambda a, buf=buf, t=t: a.copy(
                out=vst[:].rearrange("p (t g c) -> p t g c", t=8, g=4)[:, t % 8, :, 0:64],
                in_=pb[buf][:, 0:256]), waits=[(s_pe, n_mm)], inc=s_act)
            vdone[t] = (s_act, s_act.n)
            pp_free[buf] = (s_act, s_act.n)
            pb01_owner[buf] = None

        def xrw_gen(qg, gb, ldx):
            nonlocal p4_free, XRWDONE
            w = [(d_wrb, ld_rwb), (d_x[gb], ldx)]
            if XRWDONE:
                w.append(XRWDONE)
            if p4_free:
                w.append(p4_free)
            mm_group(lambda: pb[4][0:NE, :],
                     [(lambda fc=fc: rwTb_sb[:, fc * NE:(fc + 1) * NE],
                       lambda fc=fc, gb=gb: xTg[gb][:, fc * 512:(fc + 1) * 512])
                      for fc in range(8)], waits=w)
            n_x = s_pe.n
            P.emit("act", lambda a, qg=qg: a.copy(
                out=xrw_sb[:, qg * 512:(qg + 1) * 512], in_=pb[4][0:NE, :]),
                waits=[(s_pe, n_x)], inc=s_act)
            XRWDONE = (s_act, s_act.n)
            p4_free = (s_act, s_act.n)
            yield

        def glob_copy_gen():
            for od in range(2):
                P.emit("dve", lambda v, od=od: v.tensor_copy(
                    out=kTGA[:, od * 64:(od + 1) * 64], in_=kTA[od][:, 0:64]),
                    waits=[(s_dve, KDONE[(0, od)]), (s_dve, s_dve.n)], inc=s_dve)
                P.emit("dve", lambda v, od=od: v.tensor_copy(
                    out=kTGB[:, od * 64:(od + 1) * 64], in_=kTB[od][:, 0:64]),
                    waits=[(s_dve, s_dve.n)], inc=s_dve)
                yield
            P.emit("dve", lambda v: v.tensor_copy(out=vstG[:], in_=vst[0:64, 0:260]),
                   waits=[vdone[0], (s_dve, s_dve.n), (s_dve, n_voneg)], inc=s_dve)
            GLOBC[0] = s_dve.n

        GLOBC = [None]

        def drive2(att_q, fill_q, p4_q):
            cur_att, cur_p4 = None, None
            act_f = []
            while att_q or cur_att or fill_q or act_f or p4_q or cur_p4:
                if cur_att is None and att_q:
                    cur_att = att_q.pop(0)
                if cur_p4 is None and p4_q:
                    cur_p4 = p4_q.pop(0)
                while fill_q and len(act_f) < 2:
                    act_f.append(fill_q.pop(0))
                for slot, gx in (("a", cur_att), ("p", cur_p4), (0, None)):
                    pass
                for gx in [g_ for g_ in (cur_att, cur_p4) if g_] + list(act_f):
                    try:
                        next(gx)
                    except StopIteration:
                        if gx is cur_att:
                            cur_att = None
                        elif gx is cur_p4:
                            cur_p4 = None
                        else:
                            act_f.remove(gx)

        MOFF = np.cumsum([0] + [b - a for (a, b) in CHUNK_RANGES]).tolist()
        SBANK = [[5, 5], [6, 6]]
        ps_s_free = {5: None, 6: None}
        ps_y_free = {2: None, 3: None}
        eT_free = [None] * 4
        ATT_DONE = {}

        r_free = [None, None]
        xmf_free = [None, None]
        xmb_free = [None, None]
        LGDONE = []

        def emit_p4_tile(tt):
            nonlocal p4_free
            buf = tt % 2
            g = tt // 4
            attw = [(s_dve, ATT_DONE[(g, hp)]) for hp in range(8)]
            w = [r_free[buf]] if r_free[buf] else []
            P.emit("sync", lambda s, tt=tt, buf=buf: s.dma_start(
                out=r_sb[buf][:], in_=dram["x_tok"].ap()[tt * 128:(tt + 1) * 128, :]),
                waits=w, inc=d_xt[buf], inc_k=16)
            n_xl = (d_xt[buf], d_xt[buf].n)
            n_r = [None, None]
            for nh2 in range(2):
                while pb01_owner[nh2] is not None:
                    yield
                pb01_owner[nh2] = 1
                w = [(d_wo, ld_wo)] + attw
                if pp_free[nh2]:
                    w.append(pp_free[nh2])
                ytc = (g % 2) * 512 + (tt % 4) * 128
                mm_group(lambda nh2=nh2: pb[nh2][:],
                         [(lambda ic=ic, ytc=ytc: yT[ic][:, ytc:ytc + 128],
                           lambda ic=ic, nh2=nh2: wo_sb[:, ic * D + nh2 * 512: ic * D + nh2 * 512 + 512])
                          for ic in range(8)], waits=w)
                n_oh = s_pe.n
                w = [(s_pe, n_oh), n_xl, (s_dve, s_dve.n)]
                P.emit("dve", lambda v, nh2=nh2, buf=buf: v.tensor_add(
                    out=r_sb[buf][:, nh2 * 512:(nh2 + 1) * 512], in0=pb[nh2][:],
                    in1=r_sb[buf][:, nh2 * 512:(nh2 + 1) * 512]),
                    waits=w, inc=s_dve)
                n_r[nh2] = s_dve.n
                pp_free[nh2] = (s_dve, s_dve.n)
                pb01_owner[nh2] = None
                yield
            n_r = [n_r[0], n_r[1]]
            w = [(s_act, n_w2e)] + attw
            if p4_free:
                w.append(p4_free)
            ytc = (g % 2) * 512 + (tt % 4) * 128
            mm_group(lambda: pb[4][:, 0:NE],
                     [(lambda ic=ic, ytc=ytc: yT[ic][:, ytc:ytc + 128],
                       lambda ic=ic: w2_sb[:, ic * NE:(ic + 1) * NE]) for ic in range(8)],
                     waits=w)
            n_yw2 = s_pe.n
            P.emit("dve", lambda v: v.tensor_copy(out=lgs_sb[:], in_=pb[4][:, 0:NE]),
                   waits=[(s_pe, n_yw2), (s_dve, s_dve.n)], inc=s_dve)
            n_yw2c = s_dve.n
            p4_free = (s_dve, n_yw2c)
            yield
            w = [XRWDONE, (s_dve, n_yw2c), (d_c0, LD0)]
            if p4_free:
                w.append(p4_free)
            P.emit("pe", lambda te, tt=tt: te.transpose(
                pb[4][:, 0:NE], xrw_sb[:, tt * 128:(tt + 1) * 128],
                identf_sb[0:NE, 0:NE]), waits=w, inc=s_pe)
            n_xt = s_pe.n
            P.emit("dve", lambda v, buf=buf: v.tensor_copy(
                out=xrt_sb[buf][:], in_=pb[4][:, 0:NE]),
                waits=[(s_pe, n_xt), (s_dve, s_dve.n)], inc=s_dve)
            p4_free = (s_dve, s_dve.n)
            yield
            P.emit("dve", lambda v, buf=buf: v.reduce_sum(
                out=st_sb[buf][:, 0:1], in_=r_sb[buf][:], axis=AX.X),
                waits=[(s_dve, n_r[-1])], inc=s_dve)
            n_s1 = s_dve.n
            P.emit("act", lambda a, buf=buf: a.activation(
                out=sqln_sb[buf][:], in_=r_sb[buf][:], func=AF.Square),
                waits=[(s_dve, n_r[-1]), (s_act, s_act.n)], inc=s_act)
            n_sq = s_act.n
            yield
            P.emit("dve", lambda v, buf=buf: v.reduce_sum(
                out=st_sb[buf][:, 1:2], in_=sqln_sb[buf][:], axis=AX.X),
                waits=[(s_act, n_sq), (s_dve, n_s1)], inc=s_dve)
            P.emit("dve", lambda v, buf=buf: v.tensor_scalar(
                out=st_sb[buf][:, 2:3], in0=st_sb[buf][:, 0:1], scalar1=1.0 / D,
                scalar2=None, op0=ALU.mult), waits=[(s_dve, s_dve.n)], inc=s_dve)
            P.emit("dve", lambda v, buf=buf: v.tensor_mul(
                out=st_sb[buf][:, 3:4], in0=st_sb[buf][:, 2:3], in1=st_sb[buf][:, 2:3]),
                waits=[(s_dve, s_dve.n)], inc=s_dve)
            P.emit("dve", lambda v, buf=buf: v.tensor_scalar(
                out=st_sb[buf][:, 4:5], in0=st_sb[buf][:, 1:2], scalar1=1.0 / D,
                scalar2=None, op0=ALU.mult), waits=[(s_dve, s_dve.n)], inc=s_dve)
            P.emit("dve", lambda v, buf=buf: v.tensor_sub(
                out=st_sb[buf][:, 4:5], in0=st_sb[buf][:, 4:5], in1=st_sb[buf][:, 3:4]),
                waits=[(s_dve, s_dve.n)], inc=s_dve)
            n_var = s_dve.n
            yield
            P.emit("act", lambda a, buf=buf: a.activation(
                out=st_sb[buf][:, 5:6], in_=st_sb[buf][:, 4:5], func=AF.Ln,
                bias=EPS_LN), waits=[(s_dve, n_var)], inc=s_act)
            P.emit("act", lambda a, buf=buf: a.activation(
                out=st_sb[buf][:, 6:7], in_=st_sb[buf][:, 5:6], func=AF.Exp,
                scale=-0.5), waits=[(s_act, s_act.n)], inc=s_act)
            n_rs = s_act.n
            yield
            w = [(s_act, n_rs), (s_dve, s_dve.n)]
            if xmf_free[buf]:
                w.append(xmf_free[buf])
            P.emit("dve", lambda v, buf=buf: v.tensor_scalar(
                out=xmf_sb[buf][:], in0=r_sb[buf][:], scalar1=st_sb[buf][:, 2:3],
                scalar2=st_sb[buf][:, 6:7], op0=ALU.subtract, op1=ALU.mult),
                waits=w, inc=s_dve)
            n_xm = s_dve.n
            r_free[buf] = (s_dve, n_xm)
            w = [(s_dve, n_xm)]
            if xmb_free[buf]:
                w.append(xmb_free[buf])
            P.emit("act", lambda a, buf=buf: a.copy(out=xmb_sb[buf][:],
                                                    in_=xmf_sb[buf][:]),
                   waits=w, inc=s_act)
            n_xb = s_act.n
            P.emit("sync", lambda s, tt=tt, buf=buf: s.dma_start(
                out=dram["xmf"].ap()[tt * 128:(tt + 1) * 128, :], in_=xmf_sb[buf][:]),
                waits=[(s_dve, n_xm)], inc=d_xmf[buf], inc_k=16)
            xmf_free[buf] = (d_xmf[buf], d_xmf[buf].n)
            P.emit("sync", lambda s, tt=tt, buf=buf: s.dma_start(
                out=dram["xm_my"].ap()[tt * 128:(tt + 1) * 128, :], in_=xmb_sb[buf][:]),
                waits=[(s_act, n_xb)], inc=d_xmb[buf], inc_k=16)
            n_xmst = [(d_xmb[0], d_xmb[0].n), (d_xmb[1], d_xmb[1].n)]
            xmb_free[buf] = (d_xmb[buf], d_xmb[buf].n)
            yield
            P.emit("dve", lambda v, buf=buf: v.tensor_add(
                out=lgs_sb[:], in0=lgs_sb[:], in1=xrt_sb[buf][:]),
                waits=[(s_dve, s_dve.n)], inc=s_dve)
            P.emit("dve", lambda v, buf=buf: v.tensor_scalar(
                out=sqln_sb[buf][:, 0:NE], in0=sbc_sb[:],
                scalar1=st_sb[buf][:, 2:3], scalar2=None, op0=ALU.mult),
                waits=[(s_act, n_Sbc), (s_dve, s_dve.n)], inc=s_dve)
            P.emit("dve", lambda v, buf=buf: v.tensor_sub(
                out=lgs_sb[:], in0=lgs_sb[:], in1=sqln_sb[buf][:, 0:NE]),
                waits=[(s_dve, s_dve.n)], inc=s_dve)
            P.emit("dve", lambda v, buf=buf: v.tensor_scalar(
                out=lgs_sb[:], in0=lgs_sb[:], scalar1=st_sb[buf][:, 6:7],
                scalar2=None, op0=ALU.mult), waits=[(s_dve, s_dve.n)], inc=s_dve)
            P.emit("dve", lambda v, tt=tt: v.tensor_add(
                out=lgl_sb[:, tt * NE:(tt + 1) * NE], in0=lgs_sb[:], in1=rbb_sb[:]),
                waits=[(s_dve, s_dve.n), (s_act, n_rbb)], inc=s_dve)
            LGDONE.append(s_dve.n)
            if tt % 2 == 1:
                k2 = tt // 2
                P.emit("sync", lambda s, k2=k2: s.dma_start(
                    out=dram["lg_loc"].ap().rearrange(
                        "(tt p) e -> p tt e", p=128)[:, 2 * k2:2 * k2 + 2, :],
                    in_=lgl_sb[:].rearrange(
                        "p (tt e) -> p tt e", e=NE)[:, 2 * k2:2 * k2 + 2, :]),
                    waits=[(s_dve, LGDONE[-1]), (d_ms, d_ms.n)], inc=d_ms,
                    inc_k=16)
                P.emit("pool", lambda g_, k2=k2: g_.collective_compute(
                    "AllGather", ALU.bypass, replica_groups=[list(range(NCORES))],
                    ins=[dram["lg_loc"].ap()[k2 * 256:(k2 + 1) * 256, :]],
                    outs=[dram["lg_all"].ap()[k2 * 2048:(k2 + 1) * 2048, :]]),
                    waits=[(d_ms, d_ms.n)], inc=c_cc)
                LG_CHUNK.append(c_cc.n)
            if tt % (NT // AGC) == (NT // AGC) - 1:
                k = tt // (NT // AGC)
                P.emit("pool", lambda g_, k=k: g_.collective_compute(
                    "AllGather", ALU.bypass, replica_groups=[list(range(NCORES))],
                    ins=[dram["xm_my"].ap()[k * TPC:(k + 1) * TPC, :]],
                    outs=[dram["xm_pad"].ap()[1 + k * RPC: 1 + (k + 1) * RPC, :]]),
                    waits=n_xmst, inc=c_cc)

        def att_unit_gen(g, hp):
            nonlocal p4_free
            gv = hp // 2
            jj, sel = gv // 2, gv % 2
            kt0 = kTA[jj] if sel == 0 else kTB[jj]
            kt1 = kTB[jj] if sel == 0 else kTA[jj]
            ktg0 = kTGA if sel == 0 else kTGB
            ktg1 = kTGB if sel == 0 else kTGA
            qcol = (g % 2) * 512
            chunks = [("g", 0, 512, GT, 0)] + [
                (j, CHUNK_RANGES[j][0], CHUNK_RANGES[j][1], 128, 4 * g + 2 + j)
                for j in range(6)]
            av_last = [None, None]
            for ci, (j, a0_, b0, kw, vt) in enumerate(chunks):
                kc = 0 if j == "g" else g * 512 + 256 + 128 * j
                kg = kc // 512
                banks = [SBANK[0][ci % 2], SBANK[1][ci % 2]]
                w = [(s_dve, QDONE[(g, hp)])]
                if j == "g":
                    w.append((s_dve, GLOBC[0]))
                else:
                    w.append((s_dve, KDONE[(kg, jj)]))
                for bk in set(banks):
                    if ps_s_free[bk]:
                        w.append(ps_s_free[bk])

                if j == "g":
                    l0 = lambda jj=jj, ktg0=ktg0: ktg0[0:64, jj * 64:(jj + 1) * 64]
                    l1 = lambda jj=jj, ktg1=ktg1: ktg1[64:128, jj * 64:(jj + 1) * 64]
                else:
                    kr = (kc // 512 % 3) * 512 + (kc % 512)
                    l0 = lambda kr=kr, kw=kw, kt0=kt0: kt0[0:64, kr:kr + kw]
                    l1 = lambda kr=kr, kw=kw, kt1=kt1: kt1[64:128, kr:kr + kw]

                def sc_pair(te, a0_=a0_, b0=b0, kw=kw, banks=banks, l0=l0, l1=l1,
                            hp=hp, qcol=qcol):
                    te.matmul(pb[banks[0]][0:kw, a0_:b0], l0(),
                              qhat[hp][0:64, qcol + a0_:qcol + b0],
                              start=True, stop=True, tile_position=(0, 0))
                    return te.matmul(pb[banks[1]][0:kw, a0_:b0], l1(),
                                     qhat[hp][64:128, qcol + a0_:qcol + b0],
                                     start=True, stop=True, tile_position=(64, 0))
                P.emit("pe", sc_pair, waits=w, inc=s_pe)
                n_sc = s_pe.n
                n_ex = []
                for hh in range(2):
                    eb = 2 * hh + (ci % 2)
                    w = [(s_pe, n_sc)]
                    if eT_free[eb]:
                        w.append(eT_free[eb])
                    P.emit("act", lambda a_, eb=eb, a0_=a0_, b0=b0, kw=kw,
                           bk=banks[hh]: a_.activation(
                               out=eT_sb[eb][0:kw, 0:b0 - a0_],
                               in_=pb[bk][0:kw, a0_:b0], func=AF.Exp, scale=0.125),
                           waits=w, inc=s_act)
                    n_ex.append(s_act.n)
                    ps_s_free[banks[hh]] = (s_act, s_act.n)
                yield
                n_mk = [None, None]
                for hh in range(2):
                    eb = 2 * hh + (ci % 2)
                    if j == "g":
                        if g == 0:
                            P.emit("dve", lambda v, eb=eb: v.tensor_mul(
                                out=eT_sb[eb][0:GT, 0:512],
                                in0=eT_sb[eb][0:GT, 0:512], in1=maskgl_sb[:]),
                                waits=[(s_act, n_ex[hh]), (d_mk, ld_mgl),
                                       (s_dve, s_dve.n)], inc=s_dve)
                            n_mk[hh] = s_dve.n
                    else:
                        msb = maskg0_sb if g == 0 else masksh_sb
                        mo = MOFF[j]
                        P.emit("dve", lambda v, eb=eb, a0_=a0_, b0=b0, msb=msb,
                               mo=mo: v.tensor_mul(
                                   out=eT_sb[eb][:, 0:b0 - a0_],
                                   in0=eT_sb[eb][:, 0:b0 - a0_],
                                   in1=msb[:, mo:mo + (b0 - a0_)]),
                               waits=[(s_act, n_ex[hh]), (d_mk, ld_mgl),
                                      (s_dve, s_dve.n)], inc=s_dve)
                        n_mk[hh] = s_dve.n
                yield
                if j != "g":
                    while vt not in vdone:
                        yield
                for hh in range(2):
                    eb = 2 * hh + (ci % 2)
                    yb = 2 + hh
                    w = [(s_act, n_ex[hh])]
                    if j == "g":
                        w.append((s_dve, GLOBC[0]))
                    else:
                        w.append(vdone[vt])
                    if n_mk[hh] is not None:
                        w.append((s_dve, n_mk[hh]))
                    if ci == 0 and ps_y_free[yb]:
                        w.append(ps_y_free[yb])

                    if j == "g":
                        vap = lambda gv=gv: vstG[0:64, gv * 65: gv * 65 + 65]
                    else:
                        vs = (vt % 8) * 260
                        vap = lambda vs=vs, kw=kw, gv=gv: vst[0:kw, vs + gv * 65: vs + gv * 65 + 65]

                    def av(te, hh=hh, eb=eb, a0_=a0_, b0=b0, kw=kw, vap=vap,
                           ci=ci, yb=yb, nch=len(chunks)):
                        return te.matmul(
                            pb[yb][0:65, a0_:b0], vap(),
                            eT_sb[eb][0:kw, 0:b0 - a0_],
                            start=(ci == 0), stop=(ci == nch - 1),
                            skip_group_check=True)
                    P.emit("pe", av, waits=w, inc=s_pe)
                    av_last[hh] = s_pe.n
                    eT_free[eb] = (s_pe, s_pe.n)
                if j != "g":
                    rec = VUSED.setdefault(vt, [0, None])
                    rec[0] += 1
                    rec[1] = (s_pe, s_pe.n)
                yield
            n_rc = [None, None]
            for hh in range(2):
                yb = 2 + hh
                P.emit("act", lambda a, hh=hh, yb=yb: a.activation(
                    out=yv_sb[hh][64:65, :], in_=pb[yb][64:65, :], func=AF.Ln),
                    waits=[(s_pe, av_last[hh]), (s_act, s_act.n)], inc=s_act)
                P.emit("act", lambda a, hh=hh: a.activation(
                    out=rec_sb[hh][:], in_=yv_sb[hh][64:65, :], func=AF.Exp,
                    scale=-1.0), waits=[(s_act, s_act.n)], inc=s_act)
                n_rc[hh] = s_act.n
            for hh in range(2):
                yb = 2 + hh
                w = [(s_act, n_rc[hh]), (d_c0, LD0)]
                if p4_free:
                    w.append(p4_free)
                P.emit("pe", lambda te, hh=hh: te.matmul(
                    pb[4][0:64, :], onesk1_sb[0:1, 0:64], rec_sb[hh][:],
                    start=True, stop=True), waits=w, inc=s_pe)
                n_bc = s_pe.n
                P.emit("act", lambda a, hh=hh, yb=yb: a.copy(
                    out=yv_sb[hh][0:64, :], in_=pb[yb][0:64, :]),
                    waits=[(s_pe, n_bc)], inc=s_act)
                n_yv = s_act.n
                ps_y_free[yb] = (s_act, n_yv)
                wy = [(s_act, n_yv), (s_pe, n_w2mm), (s_dve, s_dve.n)]
                if P4G.get(g - 2) is not None:
                    wy.append((s_dve, P4G[g - 2]))
                P.emit("dve", lambda v, hh=hh, hp=hp, g=g: v.tensor_mul(
                    out=yT[hp][hh * 64:(hh + 1) * 64,
                               (g % 2) * 512:(g % 2) * 512 + 512],
                    in0=yv_sb[hh][0:64, :], in1=pb[4][0:64, :]),
                    waits=wy, inc=s_dve)
                p4_free = (s_dve, s_dve.n)
                yield
            ATT_DONE[(g, hp)] = s_dve.n

        for it in range(7):
            fill_q = []
            if it < NKVG:
                gb = it % 2
                w = []
                if xg_users[gb] is not None:
                    w.append((s_pe, xg_users[gb]))
                ldx = gp_dma(lambda gb=gb: xTg[gb][:],
                             lambda g5=it: dram["xT"].ap().rearrange(
                                 "(c p) n -> p c n", p=128)[:, :, g5 * 512:(g5 + 1) * 512],
                             d_x[gb], waits=w)
                wc = [cs_users[gb]] if cs_users[gb] else []
                ldc = sy_dma(lambda gb=gb: cscg[gb][:],
                             lambda g5=it: dram["csc"].ap()[:, g5 * 512:(g5 + 1) * 512],
                             d_cs[gb], waits=wc)
                ldc2 = sy_dma(lambda gb=gb: cssg[gb][:],
                              lambda g5=it: dram["css"].ap()[:, g5 * 512:(g5 + 1) * 512],
                              d_cs[gb])
                xbuf = (gb, ldx, ldc2)
                for od in range(2):
                    fill_q.append(proj_unit_gen("k", it, od, xbuf, KDONE, (it, od)))
                nt4 = 1 if it == 4 else 4
                for t4 in range(nt4):
                    fill_q.append(vproj_gen(it * 4 + t4, xbuf))
                if it == 4:
                    vdefer[0] = xbuf
                if it == 0:
                    fill_q.append(glob_copy_gen())
                if it >= 1:
                    for od in range(8):
                        fill_q.append(proj_unit_gen("q", it - 1, od, xbuf,
                                                    QDONE, (it - 1, od)))
                    fill_q.append(xrw_gen(it - 1, gb, ldx))
            if it == 5:
                for t4 in range(1, 4):
                    fill_q.append(vproj_gen(16 + t4, vdefer[0]))
            att_q = []
            if 2 <= it <= 5:
                att_q = [att_unit_gen(it - 2, hp) for hp in range(8)]
            p4_q = []
            if 3 <= it <= 6:
                p4_q = [emit_p4_tile(t) for t in range(4 * (it - 3), 4 * (it - 3) + 4)]
            drive2(att_q, fill_q, p4_q)
            if it == 0:
                emit_late_w()
            if it < NKVG:
                xg_users[gb] = s_pe.n
                cs_users[gb] = (s_dve, s_dve.n)
            if 2 <= it <= 5:
                ATTG[it - 2] = s_dve.n
            if 3 <= it <= 6:
                P4G[it - 3] = s_dve.n
        P.marks['P3'] = P.gid
        XM_AG = c_cc.n

        P.marks['P4'] = P.gid
        P4 = [(s_pe, s_pe.n), (s_act, s_act.n), (s_dve, s_dve.n)]
        es_a.close()

        ptr_free = None
        es5 = contextlib.ExitStack()
        lga_sb = sbuf(es5, "lga_sb", [128, 1024], F32)
        iotae_sb = sbuf(es5, "iotae_sb", [128, 1024], F32)
        prod_sb = sbuf(es5, "prod_sb", [128, 1024], F32)
        tmr2_sb = [sbuf(es5, f"tmr2_sb{i}", [128, 128], F32) for i in range(2)]
        v1_sb = sbuf(es5, "v1_sb", [128, 128], F32)
        v2_sb = sbuf(es5, "v2_sb", [128, 128], F32)
        top1_sb = sbuf(es5, "top1_sb", [128, 1024], F32)
        top2_sb = sbuf(es5, "top2_sb", [128, 1024], F32)
        mskd_sb = sbuf(es5, "mskd_sb", [128, 1024], F32)
        asg_sb = sbuf(es5, "asg_sb", [128, 1024], F32)
        rank_sb = sbuf(es5, "rank_sb", [128, 1024], F32)
        rt_sb = sbuf(es5, "rt_sb", [128, NE], F32)
        base_sb = sbuf(es5, "base_sb", [128, NE], F32)
        g1_sb = sbuf(es5, "g1_sb", [128, 128], F32)
        g2_sb = sbuf(es5, "g2_sb", [128, 128], F32)
        fld_sb = [sbuf(es5, f"fld_sb{i}", [128, 128], F32) for i in range(6)]
        acc_sb = [sbuf(es5, f"acc_sb{i}", [128, 128], F32) for i in range(4)]
        tmpr_sb = [sbuf(es5, f"tmpr_sb{i}", [128, 128], F32) for i in range(2)]

        for kk in range(8):
            for cc in range(NCORES):
                P.emit("sync", lambda s, kk=kk, cc=cc: s.dma_start(
                    out=lga_sb[cc * 16 + kk * 2: cc * 16 + kk * 2 + 2, :]
                        .rearrange("p (n e) -> p n e", e=NE),
                    in_=dram["lg_all"].ap()[kk * 2048 + cc * 256:
                                            kk * 2048 + (cc + 1) * 256, :]
                        .rearrange("(p n) e -> p n e", p=2)),
                    waits=[(c_cc, LG_CHUNK[kk])] + P4, inc=d_lg, inc_k=16)
        n_lga = d_lg.n
        P.emit("sync", lambda s: s.dma_start(out=iotae_sb[:], in_=dram["iotae"].ap()),
               waits=[], inc=d_io, inc_k=16)
        n_lga2 = d_io.n

        def ev(t, e):
            return t[:].rearrange("p (n e) -> p e n", e=NE)[:, e:e + 1, :] \
                       .rearrange("p a n -> p (a n)")

        P.emit("dve", lambda v: v.reduce_max(
            out=v1_sb[:], in_=lga_sb[:].rearrange("p (n e) -> p n e", e=NE),
            axis=AX.X), waits=[(d_lg, n_lga), (s_dve, s_dve.n)], inc=s_dve)
        n_v1 = s_dve.n
        for e in range(NE):
            P.emit("dve", lambda v, e=e: v.tensor_tensor(
                out=ev(top1_sb, e), in0=ev(lga_sb, e), in1=v1_sb[:], op=ALU.is_equal),
                waits=[(s_dve, n_v1)], inc=s_dve)
        n_t1 = s_dve.n
        P.emit("dve", lambda v: v.scalar_tensor_tensor(
            out=mskd_sb[:], in0=top1_sb[:], scalar=-BIG, in1=lga_sb[:],
            op0=ALU.mult, op1=ALU.add), waits=[(s_dve, n_t1)], inc=s_dve)
        n_msk = s_dve.n
        P.emit("dve", lambda v: v.reduce_max(
            out=v2_sb[:], in_=mskd_sb[:].rearrange("p (n e) -> p n e", e=NE),
            axis=AX.X), waits=[(s_dve, n_msk)], inc=s_dve)
        n_v2 = s_dve.n
        for e in range(NE):
            P.emit("dve", lambda v, e=e: v.tensor_tensor(
                out=ev(top2_sb, e), in0=ev(mskd_sb, e), in1=v2_sb[:], op=ALU.is_equal),
                waits=[(s_dve, n_v2)], inc=s_dve)
        n_t2 = s_dve.n
        P.emit("dve", lambda v: v.tensor_sub(out=tmpr_sb[0][:], in0=v1_sb[:],
                                             in1=v2_sb[:]),
               waits=[(s_dve, n_t2)], inc=s_dve)
        n_d = s_dve.n
        P.emit("act", lambda a: a.activation(out=g1_sb[:], in_=tmpr_sb[0][:],
                                             func=AF.Sigmoid),
               waits=[(s_dve, n_d), (s_act, s_act.n)], inc=s_act)
        n_g1 = s_act.n
        P.emit("act", lambda a: a.activation(out=g2_sb[:], in_=g1_sb[:],
                                             func=AF.Copy, scale=-1.0, bias=1.0),
               waits=[(s_act, n_g1)], inc=s_act)
        n_g2 = s_act.n
        P.emit("dve", lambda v: v.tensor_add(out=asg_sb[:], in0=top1_sb[:],
                                             in1=top2_sb[:]),
               waits=[(s_dve, n_t2)], inc=s_dve)
        n_asg = s_dve.n
        P.emit("dve", lambda v: v.reduce_sum(
            out=rt_sb[:], in_=asg_sb[:].rearrange("p (n e) -> p e n", e=NE),
            axis=AX.X), waits=[(s_dve, n_asg)], inc=s_dve)
        n_rt = s_dve.n
        w = [(s_dve, n_rt), (d_c0, LD0)]
        if ps_y_free[2]:
            w.append(ps_y_free[2])
        mm_group(lambda: pb[2][:, 0:NE], [(lambda: lstrict_sb[:], lambda: rt_sb[:])],
                 waits=w)
        n_bm = s_pe.n
        P.emit("act", lambda a: a.copy(out=base_sb[:], in_=pb[2][:, 0:NE]),
               waits=[(s_pe, n_bm)], inc=s_act)
        n_bse = s_act.n
        for e in range(NE):
            P.emit("dve", lambda v, e=e: v.tensor_tensor_scan(
                out=ev(rank_sb, e), data0=ev(asg_sb, e), data1=zrow_sb[:],
                initial=0.0, op0=ALU.add, op1=ALU.add),
                waits=[(s_dve, s_dve.n), (s_dve, n_zrow)], inc=s_dve)
            P.emit("dve", lambda v, e=e: v.scalar_tensor_tensor(
                out=ev(rank_sb, e), in0=ev(rank_sb, e), scalar=base_sb[:, e:e + 1],
                in1=ev(asg_sb, e), op0=ALU.add, op1=ALU.subtract),
                waits=[(s_dve, s_dve.n), (s_act, n_bse)], inc=s_dve)
        n_rank = s_dve.n

        def accum2(dst, maskt, srct, n_after, extra=()):
            P.emit("dve", lambda v: v.tensor_mul(out=prod_sb[:], in0=maskt[:],
                                                 in1=srct[:]),
                   waits=[(s_dve, n_after)] + list(extra), inc=s_dve)
            P.emit("dve", lambda v: v.reduce_sum(
                out=dst[:], in_=prod_sb[:].rearrange("p (n e) -> p n e", e=NE),
                axis=AX.X), waits=[(s_dve, s_dve.n)], inc=s_dve)
            return s_dve.n

        n_e1 = accum2(acc_sb[0], top1_sb, iotae_sb, n_rank, [(d_io, n_lga2)])
        n_e2 = accum2(acc_sb[1], top2_sb, iotae_sb, n_e1)
        n_rk1 = accum2(acc_sb[2], top1_sb, rank_sb, n_e2)
        n_rk2 = accum2(acc_sb[3], top2_sb, rank_sb, n_rk1)

        def fields_gen(et, rkt, fi, scr, n_after):
            P.emit("dve", lambda v: v.tensor_scalar(
                out=scr[0][:], in0=rkt[:], scalar1=float(CAP), scalar2=None,
                op0=ALU.is_lt), waits=[(s_dve, n_after)], inc=s_dve)
            n_a = s_dve.n
            yield
            P.emit("dve", lambda v: v.tensor_scalar(
                out=scr[1][:], in0=scr[0][:], scalar1=-1.0, scalar2=1.0,
                op0=ALU.mult, op1=ALU.add), waits=[(s_dve, n_a)], inc=s_dve)
            n_b = s_dve.n
            yield
            P.emit("dve", lambda v: v.scalar_tensor_tensor(
                out=fld_sb[fi][:], in0=et[:], scalar=float(CAP), in1=rkt[:],
                op0=ALU.mult, op1=ALU.add), waits=[(s_dve, n_b)], inc=s_dve)
            n_c = s_dve.n
            yield
            P.emit("dve", lambda v: v.scalar_tensor_tensor(
                out=fld_sb[fi][:], in0=scr[1][:], scalar=BIG, in1=fld_sb[fi][:],
                op0=ALU.mult, op1=ALU.add), waits=[(s_dve, n_c)], inc=s_dve)
            yield
            P.emit("dve", lambda v: v.memset(scr[0][:], 0.0),
                   waits=[(s_dve, s_dve.n)], inc=s_dve)
            n_d = s_dve.n
            yield
            for i in range(1, AGC):
                P.emit("dve", lambda v, i=i: v.tensor_scalar(
                    out=scr[1][:], in0=rkt[:], scalar1=float(CPC * i),
                    scalar2=None, op0=ALU.is_ge),
                    waits=[(s_dve, s_dve.n)], inc=s_dve)
                P.emit("dve", lambda v: v.tensor_add(
                    out=scr[0][:], in0=scr[0][:], in1=scr[1][:]),
                    waits=[(s_dve, s_dve.n)], inc=s_dve)
                yield
            P.emit("dve", lambda v: v.scalar_tensor_tensor(
                out=fld_sb[fi + 1][:], in0=et[:], scalar=float(CPC), in1=rkt[:],
                op0=ALU.mult, op1=ALU.add), waits=[(s_dve, s_dve.n)], inc=s_dve)
            yield
            P.emit("dve", lambda v: v.scalar_tensor_tensor(
                out=fld_sb[fi + 1][:], in0=scr[0][:],
                scalar=float((NCORES - 1) * CPC), in1=fld_sb[fi + 1][:],
                op0=ALU.mult, op1=ALU.add), waits=[(s_dve, s_dve.n)], inc=s_dve)
            yield
            P.emit("dve", lambda v: v.tensor_scalar(
                out=scr[0][:], in0=rkt[:], scalar1=float(CAP), scalar2=None,
                op0=ALU.is_lt), waits=[(s_dve, s_dve.n)], inc=s_dve)
            yield
            P.emit("dve", lambda v: v.tensor_mul(
                out=fld_sb[fi + 1][:], in0=fld_sb[fi + 1][:], in1=scr[0][:]),
                waits=[(s_dve, s_dve.n)], inc=s_dve)
            yield
            P.emit("dve", lambda v: v.tensor_scalar(
                out=scr[0][:], in0=scr[0][:], scalar1=-1.0, scalar2=1.0,
                op0=ALU.mult, op1=ALU.add), waits=[(s_dve, s_dve.n)], inc=s_dve)
            yield
            P.emit("dve", lambda v: v.scalar_tensor_tensor(
                out=fld_sb[fi + 1][:], in0=scr[0][:], scalar=float(PADROW),
                in1=fld_sb[fi + 1][:], op0=ALU.mult, op1=ALU.add),
                waits=[(s_dve, s_dve.n)], inc=s_dve)

        _ga = fields_gen(acc_sb[0], acc_sb[2], 0, tmpr_sb, n_rk2)
        _gb = fields_gen(acc_sb[1], acc_sb[3], 2, tmr2_sb, n_rk2)
        _alive = [_ga, _gb]
        while _alive:
            for _gx in list(_alive):
                try:
                    next(_gx)
                except StopIteration:
                    _alive.remove(_gx)
        n_f2 = s_dve.n
        P.emit("dve", lambda v: v.tensor_copy(out=fld_sb[4][:], in_=g1_sb[:]),
               waits=[(s_act, n_g2), (s_dve, n_f2)], inc=s_dve)
        P.emit("dve", lambda v: v.tensor_copy(out=fld_sb[5][:], in_=g2_sb[:]),
               waits=[(s_dve, s_dve.n)], inc=s_dve)
        n_flds = s_dve.n

        n_fT = []
        for fi in range(6):
            w = [(s_dve, n_flds), (d_c0, LD0)]
            if ptr_free:
                w.append(ptr_free)
            elif ps_y_free[3]:
                w.append(ps_y_free[3])
            mm_group(lambda: pb[3][:, 0:16],
                     [(lambda fi=fi: fld_sb[fi][:], lambda: rowsel_sb[:])], waits=w)
            n_m = s_pe.n
            P.emit("act", lambda a, fi=fi: a.copy(out=fT_sb[fi][:],
                                                  in_=pb[3][:, 0:16]),
                   waits=[(s_pe, n_m), (s_act, s_act.n)], inc=s_act)
            ptr_free = (s_act, s_act.n)
            n_fT.append(s_act.n)
        for i, fi in enumerate((0, 2, 1, 3)):
            P.emit("dve", lambda v, i=i, fi=fi: v.tensor_copy(
                out=fTi_sb[i][:], in_=fT_sb[fi][:]),
                waits=[(s_act, n_fT[fi]), (s_dve, s_dve.n)], inc=s_dve)
        for sl in range(2):
            P.emit("dve", lambda v, sl=sl: v.tensor_copy(
                out=val_sb[sl][:].rearrange("p (r two) -> p r two", two=2)[:, :, 0:1],
                in_=tokval_sb[:].rearrange("p (r one) -> p r one", one=1)),
                waits=[(d_c0, LD0), (s_dve, s_dve.n)], inc=s_dve)
            P.emit("dve", lambda v, sl=sl: v.tensor_copy(
                out=val_sb[sl][:].rearrange("p (r two) -> p r two", two=2)[:, :, 1:2],
                in_=fT_sb[4 + sl][:].rearrange("p (r one) -> p r one", one=1)),
                waits=[(s_act, n_fT[4 + sl]), (s_dve, s_dve.n)], inc=s_dve)
        n_vals = s_dve.n

        for sl in range(2):
            for r in range(16):
                P.emit("pool", lambda g_, sl=sl, r=r: g_.indirect_dma_start(
                    out=dram["table_loc"].ap(),
                    out_offset=IndirectOffsetOnAxis(fTi_sb[sl][:, r:r + 1], 0),
                    in_=val_sb[sl][:, 2 * r:2 * r + 2], in_offset=None,
                    bounds_check=REGS["b_scat"], oob_is_err=False),
                    waits=[(s_dve, n_vals), (d_zi, ZINIT)], inc=d_sc, inc_k=16)
        N_SCAT = d_sc.n
        P.emit("pool", lambda g_: g_.collective_compute(
            "ReduceScatter", ALU.add, replica_groups=[list(range(NCORES))],
            ins=[dram["table_loc"].ap()], outs=[dram["table_my"].ap()]),
            waits=[(d_sc, N_SCAT)], inc=c_cc)
        RS_DONE = c_cc.n

        P.marks['RT'] = P.gid
        RT = [(s_pe, s_pe.n), (s_act, s_act.n), (s_dve, s_dve.n), (d_sc, N_SCAT)]
        es5.close()

        es6 = contextlib.ExitStack()
        eW_sb = sbuf(es6, "eW_sb", [128, 8 * DFF], BF16)
        eV_sb = sbuf(es6, "eV_sb", [128, 8 * DFF], BF16)
        eWo_sb = sbuf(es6, "eWo_sb", [128, 16 * D], BF16)
        xeT = [sbuf(es6, f"xeT{i}", [128, 1024], BF16) for i in range(8)]
        gT = [sbuf(es6, f"gT{i}", [128, 512], BF16) for i in range(16)]

        ld_eW = gp_dma(lambda: eW_sb[:].rearrange("p (c n) -> p c n", n=DFF), lambda: r3("eW", DFF), d_eW, waits=RT)
        ld_eV = gp_dma(lambda: eV_sb[:].rearrange("p (c n) -> p c n", n=DFF), lambda: r3("eV", DFF), d_eV)
        ld_eWo = gp_dma(lambda: eWo_sb[:].rearrange("p (c n) -> p c n", n=D), lambda: r3("eWo", D), d_eWo)

        P.emit("sync", lambda s: s.dma_start(
            out=tab_sb[:].rearrange("p (j f) -> p j f", f=2),
            in_=dram["table_my"].ap().rearrange("(j p) f -> p j f", p=128)),
            waits=[(c_cc, RS_DONE)], inc=d_ms, inc_k=16)
        n_tab = d_ms.n
        P.emit("dve", lambda v: v.tensor_copy(
            out=gidx_sb[:].rearrange("p (r one) -> p r one", one=1),
            in_=tab_sb[:].rearrange("p (r two) -> p r two", two=2)[:, :, 0:1]),
            waits=[(d_ms, n_tab), (s_dve, s_dve.n)], inc=s_dve)
        n_gidx = s_dve.n
        P.emit("dve", lambda v: v.tensor_copy(
            out=gate_sb[:].rearrange("p (r one) -> p r one", one=1),
            in_=tab_sb[:].rearrange("p (r two) -> p r two", two=2)[:, :, 1:2]),
            waits=[(s_dve, s_dve.n)], inc=s_dve)
        n_gate = s_dve.n

        xe_free = [None, None]
        pa_free = [pp_free[0], (s_act, n_bse)]
        pbk_free = [pp_free[1], ptr_free]
        py_free = [pbc_free if False else None, None]
        py_free = [None, None]
        gT_users = None
        ye_free = [None, None]
        TRD = {0: None}
        TRQ = []
        TR_DONE = {}
        tr_left = {}

        def emit_gather(j):
            w = [(s_dve, n_gidx), (c_cc, XM_AG), (d_zi, ZINIT)]
            if xe_free[j % 2]:
                w.append(xe_free[j % 2])
            P.emit("pool", lambda g_, j=j: g_.indirect_dma_start(
                out=xe_sb[j % 2][:], out_offset=None,
                in_=dram["xm_pad"].ap(),
                in_offset=IndirectOffsetOnAxis(gidx_sb[:, j:j + 1], 0),
                bounds_check=REGS["b_gath"], oob_is_err=False),
                waits=w, inc=d_ix[j % 2], inc_k=16)
            return (d_ix[j % 2], d_ix[j % 2].n)

        GATH = {}
        XU = [None, None]
        allow = [2]
        for j in range(2):
            GATH[j] = emit_gather(j)
            for fc in range(8):
                TRQ.append((j, fc))
        nextg = [2]

        def emit_transpose_pair():
            for _ in range(2):
                if not TRQ or TRQ[0][0] // 4 >= allow[0]:
                    return
                j, fc = TRQ.pop(0)
                half = (j // 4) % 2
                w = [GATH[j], (d_c0, LD0)]
                if TRD[0]:
                    w.append(TRD[0])
                if XU[half]:
                    w.append(XU[half])
                P.emit("pe", lambda te, j=j, fc=fc: te.transpose(
                    pbt[:], xe_sb[j % 2][:, fc * 128:(fc + 1) * 128], identb_sb[:]),
                    waits=w, inc=s_pe)
                n_tr = s_pe.n
                P.emit("dve", lambda v, j=j, fc=fc, half=half: v.tensor_copy(
                    out=xeT[fc][:, half * 512 + (j % 4) * 128: half * 512 + (j % 4) * 128 + 128],
                    in_=pbt[:]),
                    waits=[(s_pe, n_tr), (s_dve, s_dve.n)], inc=s_dve)
                TRD[0] = (s_dve, s_dve.n)
                if fc == 7:
                    TR_DONE[j] = s_dve.n
                    xe_free[j % 2] = (s_pe, s_pe.n)
                    if nextg[0] < 16:
                        GATH[nextg[0]] = emit_gather(nextg[0])
                        for fc2 in range(8):
                            TRQ.append((nextg[0], fc2))
                        nextg[0] += 1

        while TRQ and TRQ[0][0] < 4:
            emit_transpose_pair()

        for jg in range(4):
            for hc in range(16):
                ab = hc % 2
                w = [(d_eW, ld_eW), (s_dve, TR_DONE[jg * 4 + 3])]
                if pa_free[ab]:
                    w.append(pa_free[ab])
                mm_group(lambda ab=ab: pb[[0, 2][ab]][:],
                         [(lambda fc=fc, hc=hc: eW_sb[:, fc * DFF + hc * 128: fc * DFF + hc * 128 + 128],
                           lambda fc=fc, jg=jg: xeT[fc][:, (jg % 2) * 512:(jg % 2) * 512 + 512])
                          for fc in range(8)], waits=w)
                n_a = s_pe.n
                emit_transpose_pair()
                w = [(d_eV, ld_eV)]
                if pbk_free[ab]:
                    w.append(pbk_free[ab])
                mm_group(lambda ab=ab: pb[[1, 3][ab]][:],
                         [(lambda fc=fc, hc=hc: eV_sb[:, fc * DFF + hc * 128: fc * DFF + hc * 128 + 128],
                           lambda fc=fc, jg=jg: xeT[fc][:, (jg % 2) * 512:(jg % 2) * 512 + 512])
                          for fc in range(8)], waits=w)
                n_b = s_pe.n
                emit_transpose_pair()
                si = hc % 2
                P.emit("act", lambda a, si=si, ab=ab: a.activation(
                    out=silu_sb[si][:], in_=pb[[1, 3][ab]][:], func=AF.Silu),
                    waits=[(s_pe, n_b), (s_act, s_act.n)], inc=s_act)
                n_si = s_act.n
                pbk_free[ab] = (s_act, n_si)
                wg = [(s_act, n_si), (s_dve, s_dve.n)]
                if gT_users:
                    wg.append(gT_users)
                P.emit("dve", lambda v, hc=hc, si=si, ab=ab: v.tensor_mul(
                    out=gT[hc][:], in0=pb[[0, 2][ab]][:], in1=silu_sb[si][:]),
                    waits=wg, inc=s_dve)
                pa_free[ab] = (s_dve, s_dve.n)
            N_GT = s_dve.n
            XU[jg % 2] = (s_pe, s_pe.n)
            allow[0] = jg + 3
            for j4 in range(4):
                jt = jg * 4 + j4
                n_y = []
                for half in range(2):
                    w = [(d_eWo, ld_eWo), (s_dve, N_GT)]
                    if py_free[half]:
                        w.append(py_free[half])
                    if jg == 0 and j4 == 0:
                        if half == 0 and p4_free:
                            w.append(p4_free)
                        if half == 1 and ps_s_free[5]:
                            w.append(ps_s_free[5])
                    mm_group(lambda half=half: pb[4 + half][:],
                             [(lambda hc=hc, jt=jt: gT[hc][:, (jt % 4) * 128:(jt % 4 + 1) * 128],
                               lambda hc=hc, half=half: eWo_sb[:, hc * D + half * 512: hc * D + half * 512 + 512])
                              for hc in range(16)], waits=w)
                    n_y.append(s_pe.n)
                    emit_transpose_pair()
                buf = jt % 2
                w0 = [(s_pe, n_y[0]), (s_pe, n_y[1]), (s_dve, n_gate),
                      (s_dve, s_dve.n)]
                if ye_free[buf]:
                    w0.append(ye_free[buf])
                for half in range(2):
                    P.emit("dve", lambda v, half=half, jt=jt, buf=buf: v.tensor_scalar(
                        out=ye_sb[buf][:, half * 512:(half + 1) * 512],
                        in0=pb[4 + half][:], scalar1=gate_sb[:, jt:jt + 1],
                        scalar2=None, op0=ALU.mult),
                        waits=w0 if half == 0 else [(s_dve, s_dve.n)], inc=s_dve)
                    py_free[half] = (s_dve, s_dve.n)
                n_ye = s_dve.n
                P.emit("sync", lambda s, jt=jt, buf=buf: s.dma_start(
                    out=dram["yexp_my"].ap()[jt * 128:(jt + 1) * 128, :],
                    in_=ye_sb[buf][:]), waits=[(s_dve, n_ye)], inc=d_ye[buf], inc_k=16)
                ye_free[buf] = (d_ye[buf], d_ye[buf].n)
            gT_users = (s_pe, s_pe.n)
            P.emit("pool", lambda g_, m=jg: g_.collective_compute(
                "AllGather", ALU.bypass, replica_groups=[list(range(NCORES))],
                ins=[dram["yexp_my"].ap()[m * CPC:(m + 1) * CPC, :]],
                outs=[dram["yexp_pad"].ap()[m * RPC:(m + 1) * RPC, :]]),
                waits=[(d_ye[0], d_ye[0].n), (d_ye[1], d_ye[1].n)], inc=c_cc)
        YE_AG = c_cc.n

        P.marks['P7'] = P.gid
        EX = [(s_pe, s_pe.n), (s_act, s_act.n), (s_dve, s_dve.n)]
        es6.close()
        es7 = contextlib.ExitStack()
        yc8 = [sbuf(es7, f"yc8_{i}", [128, D], BF16) for i in range(8)]
        f4_sb = [sbuf(es7, f"f4_sb{i}", [128, D], F32) for i in range(4)]
        o4_sb = [sbuf(es7, f"o4_sb{i}", [128, D], F32) for i in range(4)]
        st4_sb = [sbuf(es7, f"st4_sb{i}", [128, 8], F32) for i in range(4)]
        sqd_sb = sbuf(es7, "sqd_sb", [128, D], BF16)
        yc_free = [None] * 8
        f1_free = [None] * 4
        o_free = [None] * 4
        sqd_user = [None]
        st_user = [None] * 4

        def emit_p8_tile(tt):
            buf = tt % 4
            st = st4_sb[buf]
            for q in range(2):
                cb = 2 * buf + q
                ww = [(c_cc, YE_AG), (d_zi, ZINIT)]
                if tt < 4:
                    ww += EX
                if yc_free[cb]:
                    ww.append(yc_free[cb])
                P.emit("pool", lambda g_, q=q, tt=tt, cb=cb: g_.indirect_dma_start(
                    out=yc8[cb][:], out_offset=None,
                    in_=dram["yexp_pad"].ap(),
                    in_offset=IndirectOffsetOnAxis(fTi_sb[2 + q][:, tt:tt + 1], 0),
                    bounds_check=REGS["b_gath"], oob_is_err=False),
                    waits=ww, inc=d_cb[cb], inc_k=16)
            n_gth = [(d_cb[2 * buf], d_cb[2 * buf].n),
                     (d_cb[2 * buf + 1], d_cb[2 * buf + 1].n)]
            ww = [(d_xmf[0], d_xmf[0].n), (d_xmf[1], d_xmf[1].n)]
            if tt < 4:
                ww += EX
            if o_free[buf]:
                ww.append(o_free[buf])
            P.emit("sync", lambda s, tt=tt, buf=buf: s.dma_start(
                out=o4_sb[buf][:], in_=dram["xmf"].ap()[tt * 128:(tt + 1) * 128, :]),
                waits=ww, inc=d_o8[buf], inc_k=16)
            n_xr = (d_o8[buf], d_o8[buf].n)
            yield
            ww = n_gth + [(s_dve, s_dve.n)]
            if tt < 4:
                ww += EX
            if f1_free[buf]:
                ww.append(f1_free[buf])
            P.emit("dve", lambda v, buf=buf: v.tensor_add(
                out=f4_sb[buf][:], in0=yc8[2 * buf][:], in1=yc8[2 * buf + 1][:]),
                waits=ww, inc=s_dve)
            yc_free[2 * buf] = (s_dve, s_dve.n)
            yc_free[2 * buf + 1] = (s_dve, s_dve.n)
            n_ad = s_dve.n
            yield
            P.emit("dve", lambda v, buf=buf: v.tensor_add(
                out=f4_sb[buf][:], in0=f4_sb[buf][:], in1=o4_sb[buf][:]),
                waits=[(s_dve, n_ad), n_xr], inc=s_dve)
            n_r2 = s_dve.n
            yield
            P.emit("dve", lambda v, buf=buf, st=st: v.reduce_sum(
                out=st[:, 0:1], in_=f4_sb[buf][:], axis=AX.X),
                waits=[(s_dve, n_r2)], inc=s_dve)
            n_s1 = s_dve.n
            ww = [(s_dve, n_r2), (s_act, s_act.n)]
            if sqd_user[0]:
                ww.append(sqd_user[0])
            if st_user[buf]:
                ww.append(st_user[buf])
            P.emit("act", lambda a, buf=buf, st=st: a.activation(
                out=sqd_sb[:], in_=f4_sb[buf][:], func=AF.Square,
                accum_out=st[:, 1:2]), waits=ww, inc=s_act)
            n_sq = s_act.n
            sqd_user[0] = (s_act, n_sq)
            yield
            P.emit("dve", lambda v, st=st: v.tensor_scalar(
                out=st[:, 2:3], in0=st[:, 0:1], scalar1=1.0 / D,
                scalar2=None, op0=ALU.mult),
                waits=[(s_dve, n_s1), (s_dve, s_dve.n)], inc=s_dve)
            P.emit("dve", lambda v, st=st: v.tensor_mul(
                out=st[:, 3:4], in0=st[:, 2:3], in1=st[:, 2:3]),
                waits=[(s_dve, s_dve.n)], inc=s_dve)
            P.emit("dve", lambda v, st=st: v.scalar_tensor_tensor(
                out=st[:, 4:5], in0=st[:, 1:2], scalar=1.0 / D, in1=st[:, 3:4],
                op0=ALU.mult, op1=ALU.subtract),
                waits=[(s_act, n_sq), (s_dve, s_dve.n)], inc=s_dve)
            n_var = s_dve.n
            yield
            P.emit("act", lambda a, st=st: a.activation(
                out=st[:, 5:6], in_=st[:, 4:5], func=AF.Ln,
                bias=EPS_LN), waits=[(s_dve, n_var), (s_act, s_act.n)], inc=s_act)
            we = [(s_act, s_act.n)]
            if st_user[buf]:
                we.append(st_user[buf])
            P.emit("act", lambda a, st=st: a.activation(
                out=st[:, 6:7], in_=st[:, 5:6], func=AF.Exp,
                scale=-0.5), waits=we, inc=s_act)
            n_rs = s_act.n
            yield
            ww = [(s_act, n_rs), (s_dve, s_dve.n)]
            P.emit("dve", lambda v, buf=buf, st=st: v.tensor_scalar(
                out=o4_sb[buf][:], in0=f4_sb[buf][:], scalar1=st[:, 2:3],
                scalar2=st[:, 6:7], op0=ALU.subtract, op1=ALU.mult),
                waits=ww, inc=s_dve)
            n_o = s_dve.n
            f1_free[buf] = (s_dve, n_o)
            st_user[buf] = (s_dve, n_o)
            P.emit("sync", lambda s, tt=tt, buf=buf: s.dma_start(
                out=out_t.ap()[tt * 128:(tt + 1) * 128, :], in_=o4_sb[buf][:]),
                waits=[(s_dve, n_o)], inc=d_s8[buf], inc_k=16)
            o_free[buf] = (d_s8[buf], d_s8[buf].n)

        gens = [emit_p8_tile(t) for t in range(NT)]
        for i in range(0, NT, 4):
            alive = list(gens[i:i + 4])
            while alive:
                for gx in list(alive):
                    try:
                        next(gx)
                    except StopIteration:
                        alive.remove(gx)
        P.emit("sync", lambda s: s.nop(),
               waits=[(d_s8[i], d_s8[i].n) for i in range(4)])
        P.marks['P8'] = P.gid
        es7.close()

        _LAST_PLAN["marks"] = dict(P.marks)
        _LAST_PLAN["total"] = P.gid
        P.trunc = trunc
        for s in P.sems:
            s.h = es0.enter_context(nc.semaphore(s.name))
        with nc.Block() as block:
            @block.tensor
            def _(e):
                P.replay("pe", e, nc)

            @block.scalar
            def _(e):
                P.replay("act", e, nc)

            @block.vector
            def _(e):
                P.replay("dve", e, nc)

            @block.gpsimd
            def _(e):
                P.replay("pool", e, nc)

            @block.sync
            def _(e):
                P.replay("sync", e, nc)
    return nc


_NC_CACHE = {}


def kernel(L0_attn_x, L0_attn_Wq, L0_attn_Wk, L0_attn_Wv, L0_attn_Wo,
           L0_ffn_router_w, L0_ffn_router_b, L0_ffn_W, L0_ffn_V, L0_ffn_W_out):
    X = np.asarray(L0_attn_x, dtype=np.float32)
    Wq = np.asarray(L0_attn_Wq, dtype=np.float32)
    Wk = np.asarray(L0_attn_Wk, dtype=np.float32)
    Wv = np.asarray(L0_attn_Wv, dtype=np.float32)
    Wo = np.asarray(L0_attn_Wo, dtype=np.float32)
    RW = np.asarray(L0_ffn_router_w, dtype=np.float32)
    RB = np.asarray(L0_ffn_router_b, dtype=np.float32)
    EW = np.asarray(L0_ffn_W, dtype=np.float32)
    EV = np.asarray(L0_ffn_V, dtype=np.float32)
    EWo = np.asarray(L0_ffn_W_out, dtype=np.float32)
    if "nc" not in _NC_CACHE:
        _NC_CACHE["nc"] = build_nc()
    in_maps = [stage_core_inputs(c, X, Wq, Wk, Wv, Wo, RW, RB, EW, EV, EWo)
               for c in range(NCORES)]
    res = run_bass_kernel_spmd(_NC_CACHE["nc"], in_maps, core_ids=list(range(NCORES)))
    out = np.concatenate([np.asarray(res.results[c]["out"]) for c in range(NCORES)],
                         axis=0)
    return out.reshape(B, S, D).astype(np.float32)
